# version 23
# baseline (speedup 1.0000x reference)
import numpy as np
import ml_dtypes
from contextlib import ExitStack

import concourse.bass as bass
import concourse.mybir as mybir
from concourse.bass_utils import run_bass_kernel_spmd

F32 = mybir.dt.float32
BF16 = mybir.dt.bfloat16
AF = mybir.ActivationFunctionType
ALU = mybir.AluOpType
AX = mybir.AxisListType
bfloat16 = ml_dtypes.bfloat16

CM = 12582912.0
INV7 = float(np.float32(1.0 / 7.0))
INVQ = float(np.float32(1.0 / 32767.0))
P16 = float(np.float32(2.0 ** -16))
T = 2048
NT = 16
NH = 8
NI = 24


class SemC:
    def __init__(self, h):
        self.h = h
        self.n = 0

    def inc(self, k):
        self.n += k
        return self.n


def build():
    nc = bass.Bass("TRN2", target_bir_lowering=False)

    x_in = nc.dram_tensor("x", [T, 1024], F32, kind="ExternalInput").ap()
    wqkv_sl = nc.dram_tensor("wqkv_sl", [128, 3072], F32, kind="ExternalInput").ap()
    wup_sl = nc.dram_tensor("wup_sl", [384, 1024], F32, kind="ExternalInput").ap()
    wgate_sl = nc.dram_tensor("wgate_sl", [384, 1024], F32, kind="ExternalInput").ap()
    wdown_sl = nc.dram_tensor("wdown_sl", [128, 3072], F32, kind="ExternalInput").ap()
    c_triuT = nc.dram_tensor("c_triuT", [128, 128], BF16, kind="ExternalInput").ap()
    c_tri16T = nc.dram_tensor("c_tri16T", [16, 16], BF16, kind="ExternalInput").ap()
    c_ones16 = nc.dram_tensor("c_ones16", [16, 1], BF16, kind="ExternalInput").ap()
    c_sel16 = nc.dram_tensor("c_sel16", [8, 16], BF16, kind="ExternalInput").ap()
    c_esel = nc.dram_tensor("c_esel", [16, 2048], BF16, kind="ExternalInput").ap()
    c_pos = nc.dram_tensor("c_pos", [128, 16], F32, kind="ExternalInput").ap()
    c_bsel = nc.dram_tensor("c_bsel", [128, 256], BF16, kind="ExternalInput").ap()
    out_d = nc.dram_tensor("out", [T, 1024], F32, kind="ExternalOutput").ap()

    dram = lambda n, s, d: nc.dram_tensor(n, s, d).ap()
    xn_d = dram("xn_d", [T, 1024], BF16)
    wqkvq_in = dram("wqkvq_in", [128, 3072], BF16)
    wqkvq_all = dram("wqkvq_all", [1024, 3072], BF16)
    upgate_in = dram("upgate_in", [384, 2048], BF16)
    ugT_in = dram("ugT_in", [2048, 384], BF16)
    ugT_all = dram("ugT_all", [16384, 384], BF16)
    wdownq_in = dram("wdownq_in", [128, 3072], BF16)
    wdownq_all = dram("wdownq_all", [1024, 3072], BF16)
    v_d = dram("v_d", [T, 1024], BF16)
    up_d = dram("up_d", [3072, T], BF16)
    gate_d = dram("gate_d", [3072, T], BF16)
    vsum_in = dram("vsum_in", [1, 1024], BF16)
    vsum_all = dram("vsum_all", [8, 1024], BF16)
    mrow_d = [dram(f"mrow{r}_d", [2 if r == 1 else 1, 128], F32) for r in range(4)]
    cc_in = [dram(f"cc{r}_in", [1, 8], F32) for r in range(4)]
    cc_out = [dram(f"cc{r}_out", [1, 8], F32) for r in range(4)]

    es = ExitStack()
    with es:
        sb = lambda n, s, d: es.enter_context(nc.sbuf_tensor(n, s, d))
        pst = lambda n, s, d: es.enter_context(nc.psum_tensor(n, s, d))
        mksem = lambda n: SemC(es.enter_context(nc.semaphore(n)))

        A = sb("A", [128, NH * T], BF16)
        B = sb("B", [128, NH * 3072], BF16)
        D = sb("Dw", [128, NH * 3072], BF16)
        gsl = sb("gsl", [128, 2 * 2048], BF16)
        upsl = sb("upsl", [128, 2048], BF16)
        ost = sb("ost", [128, 2 * 3072], BF16)
        fstage = ost[:, :].bitcast(F32)

        def wq6(a, b):
            a = int(a * 1024); b = int(b * 1024)
            if b <= 4096:
                assert a < b <= 4096
                return gsl[:, a:b]
            assert a >= 4096
            return upsl[:, a - 4096:b - 4096]
        xstage = sb("xstage", [128, 2048], F32)
        xbstage = sb("xbstage", [128, 2048], BF16)
        vstage = sb("vstage", [128, 2048], BF16)
        kstage = sb("kstage", [128, 1024], F32)
        tsc = sb("tsc", [128, 2 * 2048], F32)
        upstage = sb("upstage", [128, 2048], BF16)
        coefC = sb("coefC", [128, 256], F32)
        coefV = sb("coefV", [128, 256], F32)
        sdg = sb("sdg", [128, 256], F32)
        B_sb = sb("B_sb", [16, 1024], BF16)
        P_sb = sb("P_sb", [16, 1024], BF16)
        T_sb = P_sb[0:1, :]
        vg_sb = sb("vg_sb", [8, 1024], BF16)
        ctriu = sb("ctriu", [128, 128], BF16)
        ctri16 = sb("ctri16", [16, 16], BF16)
        cone16 = sb("cone16", [16, 1], BF16)
        csel = sb("csel", [8, 16], BF16)
        cesel = sb("cesel", [16, 2048], BF16)
        cpos = sb("cpos", [128, 16], F32)
        cbsel = sb("cbsel", [128, 256], BF16)
        runm = sb("runm", [128, 8], F32)
        mrow_sb = sb("mrow_sb", [1, 256], F32)
        msc_sb = sb("msc_sb", [1, 8], F32)
        gbc = sb("gbc", [128, 2], F32)
        sc = sb("sc", [128, 16], F32)
        lnst = sb("lnst", [128, 8], F32)

        psA = pst("psA", [128, 2048], F32)
        psB = pst("psB", [128, 2048], F32)

        sSP = mksem("sSP")
        sCC = mksem("sCC")
        sPE = mksem("sPE")
        sDV = mksem("sDV")
        sAC = mksem("sAC")
        sPC = mksem("sPC")
        sCst = mksem("sCst")
        sW = mksem("sW")
        sWg = mksem("sWg")
        sTl = mksem("sTl")
        sX = [mksem(f"sX{i}") for i in range(4)]
        sXs = [mksem("sXs0"), mksem("sXs1")]
        sTx = mksem("sTx"); sTu = mksem("sTu"); sTg = mksem("sTg"); sTd = mksem("sTd")
        sTx3 = mksem("sTx3"); sTu2 = mksem("sTu2"); sTu3 = mksem("sTu3")
        sTg2 = mksem("sTg2"); sTg3 = mksem("sTg3"); sTx2 = mksem("sTx2")
        sQw = mksem("sQw")
        sVs = mksem("sVs")
        sV = [mksem("sV0"), mksem("sV1")]
        sUsp = [mksem("sUsp0"), mksem("sUsp1")]
        sGl = [mksem(f"sGl{i}") for i in range(4)]
        sJl = [mksem(f"sJl{i}") for i in range(4)]
        sGs = [mksem(f"sGs{i}") for i in range(4)]
        sUl = [mksem(f"sUl{i}") for i in range(6)]
        sOb = [mksem("sOb0"), mksem("sOb1")]
        sO = [mksem("sO0"), mksem("sO1")]

        t2f = upstage[:, :].bitcast(F32)

        plan = {k: [] for k in ("sync", "gpsimd", "vector", "scalar", "tensor")}
        _chain = {"vector": sDV, "scalar": sAC}

        def em(eng, fn):
            sem = _chain.get(eng)
            if sem is None:
                plan[eng].append(fn)
                return None
            v0 = sem.n
            if v0 > 0:
                plan[eng].append(lambda e, s=sem.h, vv=v0: e.wait_ge(s, vv))
            v = sem.inc(1)
            plan[eng].append(lambda e, fn=fn, s=sem.h: fn(e).then_inc(s, 1))
            return v

        def empool(fn):
            v = sPC.inc(1)
            plan["gpsimd"].append(lambda e, fn=fn, s=sPC.h: fn(e).then_inc(s, 1))
            return v

        def wait(eng, sem, val):
            if val > 0:
                plan[eng].append(lambda e, s=sem.h, v=val: e.wait_ge(s, v))

        def dma(eng, sem, outap, inap):
            v = sem.inc(16)
            plan[eng].append(lambda e, o=outap, i=inap, s=sem.h: e.dma_start(out=o, in_=i).then_inc(s, 16))
            return v

        def dmaT(eng, sem, outap, inap):
            v = sem.inc(16)
            plan[eng].append(lambda e, o=outap, i=inap, s=sem.h: e.dma_start_transpose(o, i).then_inc(s, 16))
            return v

        def gs(j):
            return gsl[:, (j % 2) * 2048:(j % 2) * 2048 + 2048]

        def ts2(j):
            return tsc[:, (j % 2) * 2048:(j % 2) * 2048 + 2048]

        tscb = tsc[:, :].bitcast(BF16)

        def o_slot_c(j, c0, c1):
            if j < 8:
                return A[:, j * 2048 + c0:j * 2048 + c1]
            if j < 20:
                return B[:, (j - 8) * 2048 + c0:(j - 8) * 2048 + c1]
            if j < 22:
                return gsl[:, (j - 20) * 2048 + c0:(j - 20) * 2048 + c1]
            if j == 22:
                return upsl[:, c0:c1]
            return tscb[:, 4096 + c0:4096 + c1]

        def o_slot(j):
            return o_slot_c(j, 0, 2048)

        def warm1(sem, val):
            if val > 0:
                plan["tensor"].append(lambda e, sh=sem.h, v=val: e.wait_ge(sh, v))
            plan["tensor"].append(lambda te: te.matmul(
                psA[64:128, 1536:1600], ctriu[:, 0:64], ctriu[:, 0:64],
                start=True, stop=True))

        def gs3c(j, c0, c1):
            sl = j % 4
            if sl < 2:
                return gsl[:, sl * 2048 + c0:sl * 2048 + c1]
            if sl == 2:
                return upsl[:, c0:c1]
            return ost[:, 4096 + c0:4096 + c1]

        def gs3(j):
            return gs3c(j, 0, 2048)

        def sigb(j, h):
            buf = fstage if j % 2 == 0 else xstage
            return buf[:, h * 1024:(h + 1) * 1024]

        ld_consts = 0
        for o, i in ((ctriu[:, :], c_triuT), (ctri16[:, :], c_tri16T),
                     (cone16[:, :], c_ones16), (csel[:, :], c_sel16),
                     (cesel[:, :], c_esel), (cpos[:, :], c_pos),
                     (cbsel[:, :], c_bsel)):
            ld_consts = dma("sync", sCst, o, i)

        def dequant(src, dst, nblk, scr):
            w = nblk * 32
            src3 = src.rearrange("p (b k) -> p b k", k=32)
            em("vector", lambda v, s3=src3, nb=nblk, scr=scr: v.tensor_reduce(
                scr[:, 0:nb], s3, AX.X, ALU.max, apply_absolute_value=True))
            em("vector", lambda v, nb=nblk, scr=scr: v.tensor_scalar(
                scr[:, 128:128 + nb], scr[:, 0:nb], INV7, 1e-12, ALU.mult, ALU.max))
            em("vector", lambda v, nb=nblk, scr=scr: v.reciprocal(scr[:, 256:256 + nb], scr[:, 128:128 + nb]))
            em("vector", lambda v, s3=src3, nb=nblk, w=w, scr=scr: v.tensor_tensor(
                scr[:, 512:512 + w].rearrange("p (b k) -> p b k", k=32), s3,
                scr[:, 256:256 + nb].unsqueeze(2).broadcast_to([128, nb, 32]), ALU.mult))
            em("vector", lambda v, w=w, scr=scr: v.tensor_scalar(
                scr[:, 512:512 + w], scr[:, 512:512 + w], CM, -CM, ALU.add, ALU.add))
            return em("vector", lambda v, dst=dst, nb=nblk, w=w, scr=scr: v.tensor_tensor(
                dst.rearrange("p (b k) -> p b k", k=32),
                scr[:, 512:512 + w].rearrange("p (b k) -> p b k", k=32),
                scr[:, 128:128 + nb].unsqueeze(2).broadcast_to([128, nb, 32]), ALU.mult))

        ld_ugj = [0] * 3
        for j in range(3):
            for q in range(2):
                ld_ugj[j] = dma("sync", sW,
                                fstage[:, j * 1024 + q * 512:j * 1024 + (q + 1) * 512],
                                wup_sl[j * 128:(j + 1) * 128, q * 512:(q + 1) * 512])

        em("vector", lambda v: v.memset(runm[:, :], 0.0))
        ld_x = [0] * NT
        dv_ln = [0] * NT
        ac_ln = [0] * NT
        sp_xn = [0] * NT
        dv1 = [0] * NT
        dv_up3 = [None] * 3
        dv_g3 = [None] * 3
        ld_g3j = [0] * 3
        sp_ug = 0
        cc_ug = 0
        def lnx(t):
            sl = t % 4
            if sl < 2:
                return xstage[:, sl * 1024:(sl + 1) * 1024]
            return tsc[:, (sl - 2) * 1024 + 2048:(sl - 1) * 1024 + 2048]

        def ln1(t):
            xa = lnx(t)
            a = 4 * (t % 2)
            if t >= 4:
                wait("sync", sDV, dv_ln[t - 4])
                wait("sync", sAC, ac_ln[t - 4])
            for q in range(4):
                ld_x[t] = dma("sync", sX[t % 4], xa[:, q * 256:(q + 1) * 256],
                              x_in[t * 128:(t + 1) * 128, q * 256:(q + 1) * 256])
            wait("vector", sX[t % 4], ld_x[t])
            em("vector", lambda v, xa=xa, a=a: v.reduce_sum(lnst[:, a:a + 1], xa, AX.X))
            dv1[t] = em("vector", lambda v, a=a: v.tensor_scalar(
                lnst[:, a + 1:a + 2], lnst[:, a:a + 1], -1.0 / 1024.0, None, ALU.mult))

        def ln2(t):
            xa = lnx(t)
            xb = xbstage[:, (t % 2) * 1024:(t % 2) * 1024 + 1024]
            a = 4 * (t % 2)
            wait("scalar", sDV, dv1[t])
            ac1 = em("scalar", lambda s, xa=xa, a=a: s.activation(
                kstage[:, 0:1024], xa, AF.Square,
                bias=lnst[:, a + 1:a + 2], scale=1.0, accum_out=lnst[:, a + 3:a + 4]))
            wait("vector", sAC, ac1)
            dv2 = em("vector", lambda v, a=a: v.tensor_scalar(
                lnst[:, a + 3:a + 4], lnst[:, a + 3:a + 4], 1.0 / 1024.0, 1e-5, ALU.mult, ALU.add))
            wait("scalar", sDV, dv2)
            ac2 = em("scalar", lambda s, a=a: s.sqrt(lnst[:, a + 2:a + 3], lnst[:, a + 3:a + 4]))
            wait("vector", sAC, ac2)
            em("vector", lambda v, a=a: v.reciprocal(lnst[:, a + 2:a + 3], lnst[:, a + 2:a + 3]))
            dv3 = em("vector", lambda v, a=a: v.tensor_tensor(
                lnst[:, a + 1:a + 2], lnst[:, a + 1:a + 2], lnst[:, a + 2:a + 3], ALU.mult))
            wait("scalar", sDV, dv3)
            if t >= 2:
                wait("scalar", sXs[t % 2], sp_xn[t - 2])
            ac_ln[t] = em("scalar", lambda s, xa=xa, xb=xb, a=a: s.activation(
                xb, xa, AF.Identity, bias=lnst[:, a + 1:a + 2], scale=lnst[:, a + 2:a + 3]))
            wait("vector", sAC, ac_ln[t])
            em("vector", lambda v, xb=xb, a=a: v.tensor_reduce(
                lnst[:, a + 3:a + 4], xb, AX.X, ALU.max, apply_absolute_value=True))
            dv_ln[t] = em("vector", lambda v, a=a: v.tensor_tensor(
                runm[:, 0:1], runm[:, 0:1], lnst[:, a + 3:a + 4], ALU.max))
            wait("gpsimd", sAC, ac_ln[t])
            for q in range(2):
                sp_xn[t] = dma("gpsimd", sXs[t % 2],
                               xn_d[t * 128:(t + 1) * 128, q * 512:(q + 1) * 512],
                               xb[:, q * 512:(q + 1) * 512])

        for t in range(NT):
            ln1(t)
            if t >= 1:
                ln2(t - 1)
            if t in (1, 2, 3):
                j = t - 1
                wait("vector", sW, ld_ugj[2])
                dv_up3[j] = dequant(fstage[:, j * 1024:(j + 1) * 1024],
                                    wq6(2 * j, 2 * j + 1), 32, tsc[:, 0:2048])
            if t == 3:
                for j in range(3):
                    wait("gpsimd", sDV, dv_up3[j])
                    for q in range(2):
                        ld_g3j[j] = dma("gpsimd", sWg,
                                        fstage[:, j * 1024 + q * 512:j * 1024 + (q + 1) * 512],
                                        wgate_sl[j * 128:(j + 1) * 128, q * 512:(q + 1) * 512])
            if t in (5, 6, 7):
                j = t - 5
                wait("vector", sWg, ld_g3j[2])
                dv_g3[j] = dequant(fstage[:, j * 1024:(j + 1) * 1024],
                                   wq6(2 * j + 1, 2 * j + 2), 32, tsc[:, 0:2048])
            if t == 4:
                wait("gpsimd", sDV, dv_up3[2])
                for j3 in range(3):
                    sp_ug = dma("gpsimd", sSP,
                                upgate_in[j3 * 128:(j3 + 1) * 128, 0:1024],
                                wq6(2 * j3, 2 * j3 + 1))
            if t == 5:
                wait("sync", sSP, sp_ug)
                for c in range(8):
                    tl = dmaT("sync", sTl, D[:, c * 384:(c + 1) * 384],
                              upgate_in[:, c * 128:(c + 1) * 128])
            if t == 8:
                wait("gpsimd", sDV, dv_g3[2])
                for j3 in range(3):
                    sp_ug = dma("gpsimd", sSP,
                                upgate_in[j3 * 128:(j3 + 1) * 128, 1024:2048],
                                wq6(2 * j3 + 1, 2 * j3 + 2))
            if t == 9:
                wait("sync", sSP, sp_ug)
                for c in range(8, 16):
                    tl = dmaT("sync", sTl, D[:, c * 384:(c + 1) * 384],
                              upgate_in[:, c * 128:(c + 1) * 128])
            if t == 10:
                wait("gpsimd", sTl, tl)
                for c in range(16):
                    sp_ugT = dma("gpsimd", sSP, ugT_in[c * 128:(c + 1) * 128, :],
                                 D[:, c * 384:(c + 1) * 384])
                wait("gpsimd", sSP, sp_ugT)
                cc_ug = sCC.inc(1)
                em("gpsimd", lambda g: g.collective_compute(
                    "AllGather", ALU.bypass, replica_groups=[list(range(8))],
                    ins=[ugT_in], outs=[ugT_all]).then_inc(sCC.h, 1))
        ln2(NT - 1)

        em("vector", lambda v: v.memset(sc[:, 10:11], CM))
        em("vector", lambda v: v.memset(sc[:, 12:13], -CM))

        mr_state = {"msc_free": 0, "gbc_free": 0}

        def max_round(ridx, cols, run_cols, dv_ready):
            n = len(run_cols)
            rc0 = run_cols[0]
            assert run_cols == list(range(rc0, rc0 + n))
            wait("gpsimd", sDV, dv_ready)
            for ci in range(n):
                sp1 = dma("gpsimd", sSP, mrow_d[ridx][ci:ci + 1, :].rearrange("o p -> p o"),
                          runm[:, rc0 + ci:rc0 + ci + 1])
            wait("gpsimd", sSP, sp1)
            sp2 = dma("gpsimd", sSP, mrow_sb[:, 0:n * 128],
                      mrow_d[ridx].rearrange("o p -> (o p)").unsqueeze(0))
            wait("vector", sSP, sp2)
            wait("vector", sSP, mr_state["msc_free"])
            em("vector", lambda v, n=n: v.tensor_reduce(
                msc_sb[:, 0:n], mrow_sb[:, 0:n * 128].rearrange("a (o p) -> a o p", p=128),
                AX.X, ALU.max))
            dvz = None
            for ci in range(len(run_cols), 8):
                dvz = em("vector", lambda v, ci=ci: v.memset(msc_sb[:, ci:ci + 1], 0.0))
            wait("gpsimd", sDV, dvz)
            sp3 = dma("gpsimd", sSP, cc_in[ridx], msc_sb[:, :])
            mr_state["msc_free"] = sp3
            wait("gpsimd", sSP, sp3)
            wait("gpsimd", sCC, sCC.n)
            ccv = sCC.inc(1)
            em("gpsimd", lambda g, ridx=ridx: g.collective_compute(
                "AllReduce", ALU.max, replica_groups=[list(range(8))],
                ins=[cc_in[ridx]], outs=[cc_out[ridx]]).then_inc(sCC.h, 1))
            wait("gpsimd", sCC, ccv)
            if mr_state["gbc_free"]:
                wait("gpsimd", sDV, mr_state["gbc_free"])
            n = len(run_cols)
            sp4 = dma("gpsimd", sSP, gbc[:, 0:n], cc_out[ridx][0:1, 0:n].broadcast_to([128, n]))
            wait("vector", sSP, sp4)
            dvs = None
            for ci, c in enumerate(cols):
                em("vector", lambda v, ci=ci, c=c: v.tensor_scalar(
                    sc[:, 2 * c:2 * c + 1], gbc[:, ci:ci + 1], INVQ, 1e-12, ALU.mult, ALU.max))
                dvs = em("vector", lambda v, c=c: v.reciprocal(sc[:, 2 * c + 1:2 * c + 2], sc[:, 2 * c:2 * c + 1]))
            mr_state["gbc_free"] = dvs
            return ccv, dvs

        cc_m0, dv_s0 = max_round(0, [0], [0], dv_ln[NT - 1])

        wait("sync", sXs[0], sp_xn[NT - 2])
        wait("sync", sXs[1], sp_xn[NT - 1])
        wait("scalar", sAC, ac_ln[NT - 1])
        tr_xn1s = 0
        trx_v = []
        for k in range(NH):
            tr_xn1s = dmaT("sync", sTx, A[:, k * 2048:(k + 1) * 2048], xn_d[:, k * 128:(k + 1) * 128])
            trx_v.append(tr_xn1s)
        wait("sync", sCC, cc_ug)
        ugT4 = ugT_all.rearrange("(r q p) c -> p r q c", q=16, p=128)
        ld_upw = 0
        ld_gw = 0
        for k in range(NH):
            for rh in range(2):
                ld_upw = dma("sync", sTu,
                             D[:, k * 3072 + rh * 1536:k * 3072 + (rh + 1) * 1536]
                             .rearrange("p (r c) -> p r c", c=384),
                             ugT4[:, rh * 4:(rh + 1) * 4, k:k + 1, :].squeeze(2))
        for k in range(NH):
            for rh in range(2):
                ld_gw = dma("sync", sTg,
                            B[:, k * 3072 + rh * 1536:k * 3072 + (rh + 1) * 1536]
                            .rearrange("p (r c) -> p r c", c=384),
                            ugT4[:, rh * 4:(rh + 1) * 4, 8 + k:9 + k, :].squeeze(2))
        dv_xq = [0] * NH
        for k in range(NH):
            if k == 0:
                wait("vector", sTx, tr_xn1s)
                wait("vector", sDV, dv_s0)
            em("vector", lambda v, k=k: v.tensor_scalar(
                tsc[:, 0:2048], A[:, k * 2048:(k + 1) * 2048], sc[:, 1:2], CM, ALU.mult, ALU.add))
            dv_xq[k] = em("vector", lambda v, k=k: v.tensor_scalar(
                A[:, k * 2048:(k + 1) * 2048], tsc[:, 0:2048], -CM, sc[:, 0:1], ALU.add, ALU.mult))

        wait("sync", sDV, dv_g3[2])
        ld_wqkv = 0
        for q in range(4):
            ld_wqkv = dma("sync", sW, fstage[:, q * 768:(q + 1) * 768],
                          wqkv_sl[:, q * 768:(q + 1) * 768])
        wait("vector", sW, ld_wqkv)
        wait("vector", sSP, sp_ug)
        dv_wqkv = em("vector", lambda v: v.tensor_copy(wq6(0, 3), fstage[:, 0:3072]))
        wait("gpsimd", sDV, dv_wqkv)
        sp_wqkv = 0
        for q in range(3):
            sp_wqkv = dma("gpsimd", sSP, wqkvq_in[:, q * 1024:(q + 1) * 1024], wq6(q, q + 1))
        wait("gpsimd", sSP, sp_wqkv)
        wait("gpsimd", sCC, sCC.n)
        cc_wqkv = sCC.inc(1)
        em("gpsimd", lambda g: g.collective_compute(
            "AllGather", ALU.bypass, replica_groups=[list(range(8))],
            ins=[wqkvq_in], outs=[wqkvq_all]).then_inc(sCC.h, 1))

        wait("sync", sDV, dv_wqkv)
        ld_wd = 0
        for q in range(4):
            ld_wd = dma("sync", sW, fstage[:, q * 768:(q + 1) * 768],
                        wdown_sl[:, q * 768:(q + 1) * 768])
        wait("vector", sW, ld_wd)
        wait("vector", sSP, sp_wqkv)
        dequant(fstage[:, 0:1536], wq6(0, 1.5), 48, tsc[:, 0:2048])
        dv_wd = dequant(fstage[:, 1536:3072], wq6(1.5, 3), 48, tsc[:, 0:2048])
        wait("gpsimd", sDV, dv_wd)
        sp_wd = 0
        for q in range(3):
            sp_wd = dma("gpsimd", sSP, wdownq_in[:, q * 1024:(q + 1) * 1024], wq6(q, q + 1))
        wait("gpsimd", sSP, sp_wd)
        wait("gpsimd", sCC, sCC.n)
        cc_wd = sCC.inc(1)
        em("gpsimd", lambda g: g.collective_compute(
            "AllGather", ALU.bypass, replica_groups=[list(range(8))],
            ins=[wdownq_in], outs=[wdownq_all]).then_inc(sCC.h, 1))

        upev_ac = {}
        upev_dv = {}
        sp_out_w = {}
        mm_pe = {}
        regions = [psA[:, 0:1024], psA[:, 1024:2048], psB[:, 0:1024], psB[:, 1024:2048]]
        halfseq = []

        def emit_mm_phase(which):
            W = D if which == "up" else B
            dst = up_d if which == "up" else gate_d
            runcol = 1 if which == "up" else 2
            for i in range(NI):
                for half in range(2):
                    key = (which, i, half)
                    halfseq.append(key)
                    g = len(halfseq) - 1
                    slot = regions[g % 4]
                    if g >= 4:
                        prev = halfseq[g - 4]
                        wait("tensor", sAC, upev_ac[prev])
                        wait("tensor", sDV, upev_dv[prev])
                    for k in range(NH):
                        for n2 in range(2):
                            fn = lambda te, W=W, i=i, k=k, half=half, n2=n2, slot=slot: te.matmul(
                                slot[:, n2 * 512:(n2 + 1) * 512],
                                W[:, k * 3072 + i * 128: k * 3072 + (i + 1) * 128],
                                A[:, k * 2048 + half * 1024 + n2 * 512: k * 2048 + half * 1024 + (n2 + 1) * 512],
                                start=(k == 0), stop=(k == NH - 1))
                            if k == NH - 1 and n2 == 1:
                                mm_pe[key] = sPE.inc(1)
                                plan["tensor"].append(lambda te, fn=fn: fn(te).then_inc(sPE.h, 1))
                            else:
                                plan["tensor"].append(fn)
                    wait("scalar", sPE, mm_pe[key])
                    wkey = (which, i - 1)
                    if which == "gate" and i < 1:
                        wkey = ("up", NI - 1)
                    if wkey in sp_out_w:
                        wait("scalar", sUsp[wkey[1] % 2], sp_out_w[wkey])
                    us = upstage[:, half * 1024:half * 1024 + 1024]
                    upev_ac[key] = em("scalar", lambda s, us=us, slot=slot: s.copy(us, slot))
                    wait("vector", sAC, upev_ac[key])
                    em("vector", lambda v, slot=slot: v.tensor_reduce(
                        lnst[:, 3:4], slot, AX.X, ALU.max, apply_absolute_value=True))
                    upev_dv[key] = em("vector", lambda v, rc=runcol: v.tensor_tensor(
                        runm[:, rc:rc + 1], runm[:, rc:rc + 1], lnst[:, 3:4], ALU.max))
                wait("sync", sAC, upev_ac[(which, i, 1)])
                for q in range(2):
                    sp_out_w[(which, i)] = dma(
                        "sync", sUsp[i % 2],
                        dst[i * 128:(i + 1) * 128, q * 1024:(q + 1) * 1024],
                        upstage[:, q * 1024:(q + 1) * 1024])

        wait("tensor", sCst, ld_consts)
        for t in range(NT):
            warm1(sDV, dv_ln[t])
        for vv in dv_up3 + dv_g3:
            warm1(sDV, vv)
        warm1(sTx, trx_v[-1])
        warm1(sDV, dv_s0)
        for k in range(NH):
            warm1(sDV, dv_xq[k])
        warm1(sTu, ld_upw)
        warm1(sTg, ld_gw)
        wait("tensor", sDV, dv_xq[NH - 1])
        wait("tensor", sTu, ld_upw)
        emit_mm_phase("up")
        wait("tensor", sTg, ld_gw)
        emit_mm_phase("gate")

        cc_m12, dv_s12a = max_round(1, [1, 2], [1, 2], upev_dv[("gate", NI - 1, 1)])
        dv_s12 = em("vector", lambda v: v.tensor_scalar(
            sc[:, 11:12], sc[:, 4:5], -CM, None, ALU.mult))

        wait("sync", sPE, mm_pe[("gate", NI - 1, 1)])
        wait("sync", sCC, cc_wqkv)
        ld_wq = [0] * NH
        for k in range(NH):
            ld_wq[k] = dma("sync", sQw, B[:, k * 3072:(k + 1) * 3072], wqkvq_all[k * 128:(k + 1) * 128, :])
        tr_xn2 = [0] * NH
        for k in range(NH):
            tr_xn2[k] = dmaT("sync", sTx, A[:, k * 2048:(k + 1) * 2048], xn_d[:, k * 128:(k + 1) * 128])

        g1_dv = [0] * NI
        ld_gt = [0] * NI
        sp_g1 = [0] * NI
        g1_pc = [0] * NI
        dvmv = [0] * NI

        def redG(j):
            wait("vector", sPC, g1_pc[j])
            em("vector", lambda v, j=j: v.tensor_reduce(
                lnst[:, 3:4], gs3(j), AX.X, ALU.max, apply_absolute_value=True))
            g1_dv[j] = em("vector", lambda v: v.tensor_tensor(
                runm[:, 3:4], runm[:, 3:4], lnst[:, 3:4], ALU.max))

        def chainG(j):
            if j >= 4:
                wait("sync", sGl[j % 4], ld_gt[j - 4])
                wait("sync", sGs[j % 4], sp_g1[j - 4])
                wait("sync", sDV, g1_dv[j - 4])
            for q in range(4):
                ld_gt[j] = dma("sync", sGl[j % 4], gs3c(j, q * 512, (q + 1) * 512),
                               gate_d[j * 128:(j + 1) * 128, q * 512:(q + 1) * 512])
            wait("scalar", sGl[j % 4], ld_gt[j])
            if j == 0:
                wait("scalar", sDV, dv_s12)
            if j >= 2:
                wait("scalar", sDV, dvmv[j - 2])
            ac1 = em("scalar", lambda s, j=j: s.activation(ts2(j), gs3(j), AF.Identity, bias=sc[:, 10:11], scale=sc[:, 5:6]))
            wait("vector", sAC, ac1)
            dvv = em("vector", lambda v, j=j: v.tensor_scalar(gs3(j), ts2(j), -CM, sc[:, 4:5], ALU.add, ALU.mult))
            if j >= 2:
                wait("scalar", sPC, g1_pc[j - 2])
            sgmv = 0
            for h in range(2):
                sb_ = sigb(j, h)
                sgmv = em("scalar", lambda s, j=j, h=h, sb_=sb_: s.activation(
                    sb_, tsc[:, (j % 2) * 2048 + h * 1024:(j % 2) * 2048 + (h + 1) * 1024],
                    AF.Sigmoid, bias=sc[:, 11:12], scale=sc[:, 4:5]))
            wait("gpsimd", sAC, sgmv)
            acs = 0
            for h in range(2):
                sb_ = sigb(j, h)
                acs = empool(lambda g, sb_=sb_: g.tensor_scalar(
                    sb_, sb_, 65536.0, CM, ALU.mult, ALU.add))
            wait("vector", sPC, acs)
            for h in range(2):
                sb_ = sigb(j, h)
                dvmv[j] = em("vector", lambda v, sb_=sb_: v.tensor_scalar(
                    sb_, sb_, -CM, 65535.0, ALU.add, ALU.min))
            wait("gpsimd", sDV, dvmv[j])
            for h in range(2):
                gh = gs3c(j, h * 1024, (h + 1) * 1024)
                g1_pc[j] = empool(lambda g, gh=gh, sb_=sigb(j, h): g.tensor_tensor(gh, sb_, gh, ALU.mult))
            wait("gpsimd", sPC, g1_pc[j])
            for q in range(4):
                sp_g1[j] = dma("gpsimd", sGs[j % 4],
                               gate_d[j * 128:(j + 1) * 128, q * 512:(q + 1) * 512],
                               gs3c(j, q * 512, (q + 1) * 512))
            if j >= 2:
                redG(j - 2)
            if j == NI - 1:
                redG(j - 1)
                redG(j)

        g2_dv = [0] * NI
        ld_g1 = [0] * NI
        ld_ur = [0] * NI
        sp_g2 = [0] * NI
        g2_pc = [0] * NI
        actu_ms = [0] * NI
        actg_ms = [0] * NI
        dvvv = [0] * NI

        def redI(j):
            O = o_slot(j) if j < 20 else gsI(j)
            wait("vector", sPC, g2_pc[j])
            em("vector", lambda v, j=j, O=O: v.tensor_reduce(
                lnst[:, 3:4], O, AX.X, ALU.max, apply_absolute_value=True))
            g2_dv[j] = em("vector", lambda v: v.tensor_tensor(
                runm[:, 4:5], runm[:, 4:5], lnst[:, 3:4], ALU.max))

        def gsI(j, c0=0, c1=2048):
            sl = j % 4
            if sl < 2:
                return gsl[:, sl * 2048 + c0:sl * 2048 + c1]
            if sl == 2:
                return upsl[:, c0:c1]
            return vstage[:, c0:c1]

        kb = kstage[:, :].bitcast(BF16)

        def chainI(j):
            if j == 0:
                wait("sync", sDV, g1_dv[NI - 1])
            if j == 3:
                wait("sync", sVs, sp_v[NT - 1])
                wait("sync", sPE, pe_B_last[0])
            if j < 3:
                wait("sync", sGl[j % 4], ld_gt[20 + j])
                wait("sync", sGs[j % 4], sp_g1[20 + j])
            elif j >= 4:
                wait("sync", sGl[j % 4], ld_g1[j - 4])
                wait("sync", sAC, actg_ms[j - 4])
            for q in range(4):
                ld_g1[j] = dma("sync", sGl[j % 4], gsI(j, q * 512, (q + 1) * 512),
                               gate_d[j * 128:(j + 1) * 128, q * 512:(q + 1) * 512])
            if j == 0:
                wait("sync", sPE, pe_half[NT - 1][1])
            if j == 20:
                wait("sync", sAC, ac_ev[NT - 1][1])
                wait("sync", sDV, dv_ev[NT - 1][1])
            if j > 20:
                wait("sync", sAC, actu_ms[j - 1])
            ub = o_slot(j) if j < 20 else kb
            for q in range(2):
                ld_ur[j] = dma("sync", sUl[j % 6], o_slot_c(j, q * 1024, (q + 1) * 1024)
                               if j < 20 else kb[:, q * 1024:(q + 1) * 1024],
                               up_d[j * 128:(j + 1) * 128, q * 1024:(q + 1) * 1024])
            wait("scalar", sGl[j % 4], ld_g1[j])
            if j == 0:
                wait("scalar", sDV, mr_state["dv_s3"])
                wait("scalar", sDV, dv_coef)
            if j == 1:
                wait("scalar", sGs[3], sp_g1[23])
            if j >= 2:
                wait("scalar", sDV, dvvv[j - 2])
            actg_ms[j] = em("scalar", lambda s, j=j: s.activation(ts2(j), gsI(j), AF.Identity, bias=sc[:, 10:11], scale=sc[:, 7:8]))
            wait("vector", sAC, actg_ms[j])
            g2b = ost[:, 0:2048] if j % 2 == 0 else ost[:, 2048:4096]
            if j >= 2:
                wait("vector", sPC, g2_pc[j - 2])
            dvvv[j] = em("vector", lambda v, j=j, g2b=g2b: v.tensor_scalar(g2b, ts2(j), -CM, sc[:, 6:7], ALU.add, ALU.mult))
            wait("scalar", sUl[j % 6], ld_ur[j])
            if j >= 2:
                wait("scalar", sPC, g2_pc[j - 2])
            gb0 = (j % 2) * 2048
            if j % 2 == 0:
                ub2h = (xstage[:, 0:2048],)
            else:
                ub2h = (fstage[:, 2048:3072], t2f[:, 0:1024])
            for hi, uh in enumerate(ub2h):
                n = len(ub2h)
                src = (ub if n == 1 else
                       (o_slot_c(j, hi * 1024, (hi + 1) * 1024) if j < 20
                        else kb[:, hi * 1024:(hi + 1) * 1024]))
                actu_ms[j] = em("scalar", lambda s, uh=uh, src=src: s.activation(
                    src if False else uh, src, AF.Identity, bias=sc[:, 10:11], scale=sc[:, 3:4]))
            if j % 2 == 0:
                dvu_s = em("scalar", lambda s, uh=ub2h[0]: s.activation(
                    uh, uh, AF.Identity, bias=sc[:, 12:13], scale=1.0))
                wait("gpsimd", sAC, dvu_s)
                wait("gpsimd", sDV, dvvv[j])
            else:
                wait("vector", sAC, actu_ms[j])
                dvu = 0
                for uh in ub2h:
                    dvu = em("vector", lambda v, uh=uh: v.tensor_scalar(uh, uh, -CM, None, ALU.add))
                wait("gpsimd", sDV, dvu)
            if len(ub2h) == 1:
                O = o_slot(j) if j < 20 else gsI(j)
                g2_pc[j] = empool(lambda g, O=O, g2b=g2b, uh=ub2h[0]: g.tensor_tensor(O, uh, g2b, ALU.mult))
            else:
                for hi, uh in enumerate(ub2h):
                    Oh = o_slot_c(j, hi * 1024, (hi + 1) * 1024) if j < 20 else gsI(j, hi * 1024, (hi + 1) * 1024)
                    gbh = ost[:, gb0 + hi * 1024:gb0 + (hi + 1) * 1024]
                    g2_pc[j] = empool(lambda g, Oh=Oh, gbh=gbh, uh=uh: g.tensor_tensor(Oh, uh, gbh, ALU.mult))
            if j >= 20:
                wait("gpsimd", sPC, g2_pc[j])
                for q in range(4):
                    sp_g2[j] = dma("gpsimd", sGs[j % 4],
                                   up_d[j * 128:(j + 1) * 128, q * 512:(q + 1) * 512],
                                   gsI(j, q * 512, (q + 1) * 512))
            if j >= 2:
                redI(j - 2)
            if j == NI - 1:
                redI(j - 1)
                redI(j)

        o_dv = [0] * NI
        ld_g2 = [0] * NI
        toenc_ms = [0] * NI

        def chainJ(j):
            if j >= 20:
                if j == 20:
                    for s4 in range(4):
                        wait("sync", sGs[s4], sp_g2[20 + s4])
                    wait("sync", sDV, g2_dv[NI - 1])
                    wait("sync", sPC, g2_pc[NI - 1])
                    wait("sync", sAC, actu_ms[NI - 1])
                if j == 23:
                    wait("sync", sDV, o_dv[22])
                for q in range(2):
                    ld_g2[j] = dma("sync", sJl[j - 20], o_slot_c(j, q * 1024, (q + 1) * 1024),
                                   up_d[j * 128:(j + 1) * 128, q * 1024:(q + 1) * 1024])
            if j == 0:
                wait("scalar", sDV, mr_state["dv_s4"])
            if j >= 2:
                wait("scalar", sDV, o_dv[j - 2])
            if j >= 20:
                wait("scalar" if j % 2 else "vector", sJl[j - 20], ld_g2[j])
            tb = ts2(j + 1)
            if j % 2 == 0:
                em("vector", lambda v, j=j, tb=tb: v.tensor_scalar(
                    tb, o_slot(j), sc[:, 9:10], CM, ALU.mult, ALU.add))
            else:
                toenc_ms[j] = em("scalar", lambda s, j=j, tb=tb: s.activation(
                    tb, o_slot(j), AF.Identity, bias=sc[:, 10:11], scale=sc[:, 9:10]))
                wait("vector", sAC, toenc_ms[j])
            o_dv[j] = em("vector", lambda v, j=j, tb=tb: v.tensor_scalar(
                o_slot(j), tb, -CM, sc[:, 8:9], ALU.add, ALU.mult))

        pe_half = [[0, 0] for _ in range(NT)]
        ac_ev = [[0, 0] for _ in range(NT)]
        dv_ev = [[0, 0] for _ in range(NT)]
        sp_v = [0] * NT
        pe_B_last = [0]

        def qkv_pe(t):
            for h in range(2):
                ps = psA if h == 0 else psB
                if t >= 1:
                    wait("tensor", sAC, ac_ev[t - 1][h])
                    wait("tensor", sDV, dv_ev[t - 1][h])
                for k in range(NH):
                    for n3 in range(3):
                        fn = lambda te, ps=ps, k=k, n3=n3, t=t, h=h: te.matmul(
                            ps[:, n3 * 512:(n3 + 1) * 512],
                            A[:, k * 2048 + t * 128: k * 2048 + (t + 1) * 128],
                            B[:, k * 3072 + h * 1536 + n3 * 512: k * 3072 + h * 1536 + (n3 + 1) * 512],
                            start=(k == 0), stop=(k == NH - 1))
                        if k == NH - 1 and n3 == 2:
                            pe_half[t][h] = sPE.inc(1)
                            plan["tensor"].append(lambda te, fn=fn: fn(te).then_inc(sPE.h, 1))
                        else:
                            plan["tensor"].append(fn)

        def qkv_B(t):
            vt = vstage[:, (t % 2) * 1024:(t % 2) * 1024 + 1024]
            wait("tensor", sAC, ac_ev[t][1])
            plan["tensor"].append(lambda te, vt=vt, t=t: te.matmul(
                psA[0:16, 1536:2048], cbsel[:, t * 16:(t + 1) * 16], vt[:, 0:512],
                start=(t == 0), stop=(t == NT - 1)))
            fnB = lambda te, vt=vt, t=t: te.matmul(
                psB[0:16, 1536:2048], cbsel[:, t * 16:(t + 1) * 16], vt[:, 512:1024],
                start=(t == 0), stop=(t == NT - 1))
            if t == NT - 1:
                pe_B_last[0] = sPE.inc(1)
                plan["tensor"].append(lambda te, fnB=fnB: fnB(te).then_inc(sPE.h, 1))
            else:
                plan["tensor"].append(fnB)

        def qkv_evac(t):
            for h in range(2):
                ps = psA if h == 0 else psB
                wait("scalar", sPE, pe_half[t][h])
                if h == 1:
                    wait("scalar", sDV, dv_ev[t][0])
                elif t >= 1:
                    wait("scalar", sDV, dv_ev[t - 1][1])
                    wait("scalar", sVs, sp_v[t - 1])
                vsl = vstage[:, (t % 2) * 1024 + h * 512:(t % 2) * 1024 + h * 512 + 512]
                em("scalar", lambda s, ps=ps, vsl=vsl: s.copy(
                    vsl.rearrange("p (hh dd) -> p hh dd", dd=64),
                    ps[:, 0:1536].rearrange("p (hh c) -> p hh c", c=192)[:, :, 128:192]))
                ac_ev[t][h] = em("scalar", lambda s, ps=ps: s.copy(
                    kstage[:, 0:512].rearrange("p (hh dd) -> p hh dd", dd=64),
                    ps[:, 0:1536].rearrange("p (hh c) -> p hh c", c=192)[:, :, 64:128]))
                wait("vector", sAC, ac_ev[t][h])
                em("vector", lambda v, ps=ps: v.tensor_tensor(
                    kstage[:, 0:512].rearrange("p (hh dd) -> p hh dd", dd=64),
                    ps[:, 0:1536].rearrange("p (hh c) -> p hh c", c=192)[:, :, 0:64],
                    kstage[:, 0:512].rearrange("p (hh dd) -> p hh dd", dd=64), ALU.mult))
                dv_ev[t][h] = em("vector", lambda v, t=t, h=h: v.tensor_reduce(
                    sdg[:, t * 16 + h * 8: t * 16 + h * 8 + 8],
                    kstage[:, 0:512].rearrange("p (hh dd) -> p hh dd", dd=64),
                    AX.X, ALU.add))
            vt = vstage[:, (t % 2) * 1024:(t % 2) * 1024 + 1024]
            wait("gpsimd", sAC, ac_ev[t][1])
            sp_v[t] = dma("gpsimd", sVs, v_d[t * 128:(t + 1) * 128, :], vt)

        wait("tensor", sCst, ld_consts)
        wait("tensor", sQw, ld_wq[NH - 1])
        wait("tensor", sTx, tr_xn2[NH - 1])
        for i in (NI - 2, NI - 1):
            for h in (0, 1):
                wait("tensor", sAC, upev_ac[("gate", i, h)])
                wait("tensor", sDV, upev_dv[("gate", i, h)])
        wait("sync", sUsp[0], sp_out_w[("gate", NI - 2)])
        wait("sync", sUsp[1], sp_out_w[("gate", NI - 1)])
        wait("sync", sSP, sp_wd)

        warm1(sDV, dv_s12)
        warm1(sQw, ld_wq[NH - 1])
        warm1(sTx, tr_xn2[NH - 1])
        chain_left = list(range(NI))
        for t in range(NT):
            if t >= 1:
                warm1(sDV, dv_ev[t - 1][1])
            qkv_pe(t)
            if t >= 1:
                qkv_B(t - 1)
            for _ in range(2 if t % 2 == 0 else 1):
                if chain_left:
                    chainG(chain_left.pop(0))
            qkv_evac(t)
        qkv_B(NT - 1)

        cc_m3, dv_s3a = max_round(2, [3], [3], g1_dv[NI - 1])
        em("vector", lambda v: v.tensor_scalar(
            sc[:, 6:7], gbc[:, 0:1], float(np.float32(P16) * np.float32(INVQ)), 1e-12, ALU.mult, ALU.max))
        em("vector", lambda v: v.reciprocal(sc[:, 7:8], sc[:, 6:7]))
        dv_s3 = em("vector", lambda v: v.tensor_scalar(sc[:, 7:8], sc[:, 7:8], P16, None, ALU.mult))
        mr_state["gbc_free"] = dv_s3
        mr_state["dv_s3"] = dv_s3

        wait("vector", sPE, pe_B_last[0])
        em("vector", lambda v: v.tensor_copy(B_sb[:, 0:512], psA[0:16, 1536:2048]))
        dv_Bev = em("vector", lambda v: v.tensor_copy(B_sb[:, 512:1024], psB[0:16, 1536:2048]))
        wait("tensor", sDV, dv_Bev)
        plan["tensor"].append(lambda te: te.matmul(psA[0:1, 0:512], cone16[:, :], B_sb[:, 0:512], start=True, stop=True))
        pe_T = sPE.inc(1)
        plan["tensor"].append(lambda te: te.matmul(psA[0:1, 512:1024], cone16[:, :], B_sb[:, 512:1024], start=True, stop=True).then_inc(sPE.h, 1))
        wait("vector", sPE, pe_T)
        dv_T = em("vector", lambda v: v.tensor_copy(T_sb, psA[0:1, 0:1024]))
        wait("gpsimd", sDV, dv_T)
        sp_T = dma("gpsimd", sSP, vsum_in, T_sb)
        wait("gpsimd", sSP, sp_T)
        wait("gpsimd", sCC, sCC.n)
        cc_carry = sCC.inc(1)
        em("gpsimd", lambda g: g.collective_compute(
            "AllGather", ALU.bypass, replica_groups=[list(range(8))],
            ins=[vsum_in], outs=[vsum_all]).then_inc(sCC.h, 1))

        wait("vector", sCst, ld_consts)
        wait("vector", sPC, g1_pc[NI - 1])
        em("vector", lambda v: v.tensor_scalar(xstage[:, 0:256], sdg[:, :], 0.125, 0.0, ALU.mult, ALU.max))
        dv_m = em("vector", lambda v: v.scalar_tensor_tensor(
            xstage[:, 256:512], sdg[:, :], 0.125, xstage[:, 0:256], ALU.mult, ALU.subtract))
        wait("scalar", sDV, dv_m)
        em("scalar", lambda s: s.activation(xstage[:, 512:768], xstage[:, 0:256], AF.Exp, scale=-1.0))
        ac_ab = em("scalar", lambda s: s.activation(xstage[:, 768:1024], xstage[:, 256:512], AF.Exp))
        wait("vector", sAC, ac_ab)
        em("vector", lambda v: v.tensor_tensor(
            xstage[:, 1024:1280].rearrange("p (tt hh) -> p tt hh", hh=16),
            xstage[:, 512:768].rearrange("p (tt hh) -> p tt hh", hh=16),
            cpos[:, :].unsqueeze(2).broadcast_to([128, 16, 16]), ALU.mult))
        em("vector", lambda v: v.tensor_tensor(xstage[:, 1024:1280], xstage[:, 1024:1280], xstage[:, 768:1024], ALU.add))
        em("vector", lambda v: v.reciprocal(xstage[:, 1280:1536], xstage[:, 1024:1280]))
        em("vector", lambda v: v.tensor_tensor(coefC[:, :], xstage[:, 512:768], xstage[:, 1280:1536], ALU.mult))
        em("vector", lambda v: v.tensor_tensor(xstage[:, 1536:1792], xstage[:, 768:1024], xstage[:, 512:768], ALU.subtract))
        dv_coef = em("vector", lambda v: v.tensor_tensor(coefV[:, :], xstage[:, 1536:1792], xstage[:, 1280:1536], ALU.mult))

        for j in range(NI):
            if j == 1:
                wait("tensor", sDV, dv_T)
            if j >= 1:
                warm1(sPC, g2_pc[j - 1])
            chainI(j)

        wait("gpsimd", sCC, cc_carry)
        sp_vg = dma("gpsimd", sOb[0], vg_sb[:, :], vsum_all)
        wait("tensor", sCst, ld_consts)
        wait("tensor", sOb[0], sp_vg)
        pe_P = 0
        for n2 in range(2):
            plan["tensor"].append(lambda te, n2=n2: te.matmul(
                psB[0:16, n2 * 512:(n2 + 1) * 512], ctri16[:, :],
                B_sb[:, n2 * 512:(n2 + 1) * 512], start=True, stop=False))
            fn = lambda te, n2=n2: te.matmul(
                psB[0:16, n2 * 512:(n2 + 1) * 512], csel[:, :],
                vg_sb[:, n2 * 512:(n2 + 1) * 512], start=False, stop=True)
            if n2 == 1:
                pe_P = sPE.inc(1)
                plan["tensor"].append(lambda te, fn=fn: fn(te).then_inc(sPE.h, 1))
            else:
                plan["tensor"].append(fn)
        wait("vector", sPE, pe_P)
        dv_P = em("vector", lambda v: v.tensor_copy(P_sb[:, :], psB[0:16, 0:1024]))

        cc_m4, dv_s4a = max_round(3, [4], [4], g2_dv[NI - 1])
        em("vector", lambda v: v.tensor_tensor(sc[:, 8:9], gbc[:, 0:1], sc[:, 2:3], ALU.mult))
        em("vector", lambda v: v.tensor_scalar(sc[:, 8:9], sc[:, 8:9], INVQ, 1e-12, ALU.mult, ALU.max))
        em("vector", lambda v: v.reciprocal(sc[:, 9:10], sc[:, 8:9]))
        dv_s4 = em("vector", lambda v: v.tensor_tensor(sc[:, 9:10], sc[:, 9:10], sc[:, 2:3], ALU.mult))
        mr_state["gbc_free"] = dv_s4
        mr_state["dv_s4"] = dv_s4

        warm1(sDV, mr_state["dv_s4"])
        for j in range(NI):
            if j >= 1:
                warm1(sDV, o_dv[j - 1])
            chainJ(j)

        wait("sync", sCC, cc_wd)
        wait("sync", sPE, mm_pe[("up", NI - 1, 1)])
        tr_d = [0] * NI
        for k in range(NI):
            tr_d[k] = dmaT("sync", sTd, D[:, k * 1024:(k + 1) * 1024], wdownq_all[:, k * 128:(k + 1) * 128])

        dn_pe = [0] * NT
        dn_dv = [0] * NT
        attn_pe = [0] * NT
        attn_dv = [0] * NT
        ld_x2 = [0] * NT
        ld_at_v = [0] * NT
        st_out = [0] * NT
        wait("sync", sVs, sp_v[NT - 1])
        wait("sync", sPE, pe_B_last[0])
        wait("sync", sDV, dv_coef)
        wait("tensor", sTd, tr_d[NI - 1])
        wait("tensor", sDV, o_dv[NI - 1])

        for t in range(NT):
            if t >= 2:
                wait("sync", sDV, attn_dv[t - 2])
            for q in range(2):
                ld_at_v[t] = dma("sync", sV[t % 2],
                                 vstage[:, (t % 2) * 1024 + q * 512:(t % 2) * 1024 + (q + 1) * 512],
                                 v_d[t * 128:(t + 1) * 128, q * 512:(q + 1) * 512])
            if t >= 2:
                wait("sync", sDV, dn_dv[t - 2])
            for q in range(4):
                ld_x2[t] = dma("sync", sX[t % 2],
                               xstage[:, (t % 2) * 1024 + q * 256:(t % 2) * 1024 + (q + 1) * 256],
                               x_in[t * 128:(t + 1) * 128, q * 256:(q + 1) * 256])

            pc = psB[:, (t % 2) * 1024:(t % 2) * 1024 + 1024]
            if t >= 2:
                wait("tensor", sDV, attn_dv[t - 2])
            if t == 0:
                wait("tensor", sDV, dv_P)
            wait("tensor", sV[t % 2], ld_at_v[t])
            for n2 in range(2):
                plan["tensor"].append(lambda te, t=t, n2=n2, pc=pc: te.matmul(
                    pc[:, n2 * 512:(n2 + 1) * 512], ctriu[:, :],
                    vstage[:, (t % 2) * 1024 + n2 * 512:(t % 2) * 1024 + (n2 + 1) * 512],
                    start=True, stop=False))
                fn = lambda te, t=t, n2=n2, pc=pc: te.matmul(
                    pc[:, n2 * 512:(n2 + 1) * 512], cesel[:, t * 128:(t + 1) * 128],
                    P_sb[:, n2 * 512:(n2 + 1) * 512], start=False, stop=True)
                if n2 == 1:
                    attn_pe[t] = sPE.inc(1)
                    plan["tensor"].append(lambda te, fn=fn: fn(te).then_inc(sPE.h, 1))
                else:
                    plan["tensor"].append(fn)

            slot = psA[:, (t % 2) * 1024:(t % 2) * 1024 + 1024]
            if t >= 2:
                wait("tensor", sDV, dn_dv[t - 2])
            elif t == 0:
                wait("tensor", sDV, dv_T)
            for k in range(NI):
                for n2 in range(2):
                    fn = lambda te, t=t, k=k, n2=n2, slot=slot: te.matmul(
                        slot[:, n2 * 512:(n2 + 1) * 512],
                        o_slot_c(k, t * 128, (t + 1) * 128),
                        D[:, k * 1024 + n2 * 512: k * 1024 + (n2 + 1) * 512],
                        start=(k == 0), stop=(k == NI - 1))
                    if k == NI - 1 and n2 == 1:
                        dn_pe[t] = sPE.inc(1)
                        plan["tensor"].append(lambda te, fn=fn: fn(te).then_inc(sPE.h, 1))
                    else:
                        plan["tensor"].append(fn)

            wait("vector", sPE, attn_pe[t])
            if t == 0:
                wait("vector", sDV, dv_coef)
            ab = xbstage[:, (t % 2) * 1024:(t % 2) * 1024 + 1024]
            em("vector", lambda v, t=t, pc=pc: v.tensor_tensor(
                kstage[:, 0:1024].rearrange("p (hh dd) -> p hh dd", dd=64),
                pc[:, :].rearrange("p (hh dd) -> p hh dd", dd=64),
                coefC[:, t * 16:(t + 1) * 16].unsqueeze(2).broadcast_to([128, 16, 64]), ALU.mult))
            em("vector", lambda v, t=t: v.tensor_tensor(
                t2f[:, 0:1024].rearrange("p (hh dd) -> p hh dd", dd=64),
                vstage[:, (t % 2) * 1024:(t % 2) * 1024 + 1024].rearrange("p (hh dd) -> p hh dd", dd=64),
                coefV[:, t * 16:(t + 1) * 16].unsqueeze(2).broadcast_to([128, 16, 64]), ALU.mult))
            attn_dv[t] = em("vector", lambda v, ab=ab: v.tensor_tensor(
                ab, kstage[:, 0:1024], t2f[:, 0:1024], ALU.add))

            wait("vector", sPE, dn_pe[t])
            wait("vector", sX[t % 2], ld_x2[t])
            if t >= 2:
                wait("vector", sO[t % 2], st_out[t - 2])
            osl = tsc[:, (t % 2) * 1024:(t % 2) * 1024 + 1024]
            em("vector", lambda v, t=t, slot=slot, osl=osl: v.tensor_tensor(
                osl, slot, xstage[:, (t % 2) * 1024:(t % 2) * 1024 + 1024], ALU.add))
            dn_dv[t] = em("vector", lambda v, t=t, osl=osl, ab=ab: v.tensor_tensor(
                osl, osl, ab, ALU.add))
            wait("gpsimd", sDV, dn_dv[t])
            for q in range(2):
                st_out[t] = dma("gpsimd", sO[t % 2],
                                out_d[t * 128:(t + 1) * 128, q * 512:(q + 1) * 512],
                                osl[:, q * 512:(q + 1) * 512])

        wait("gpsimd", sO[0], st_out[NT - 2])
        wait("gpsimd", sO[1], st_out[NT - 1])

        block = es.enter_context(nc.Block())

        @block.sync
        def _(sync):
            for fn in plan["sync"]:
                fn(sync)

        @block.gpsimd
        def _(gpsimd):
            for fn in plan["gpsimd"]:
                fn(gpsimd)

        @block.vector
        def _(vector):
            for fn in plan["vector"]:
                fn(vector)

        @block.scalar
        def _(scalar):
            for fn in plan["scalar"]:
                fn(scalar)

        @block.tensor
        def _(tensor):
            for fn in plan["tensor"]:
                fn(tensor)

    return nc


def _consts(core_id):
    triuT = np.triu(np.ones((128, 128), np.float32)).astype(bfloat16)
    tri16T = np.triu(np.ones((16, 16), np.float32), 1).astype(bfloat16)
    ones16 = np.ones((16, 1), np.float32).astype(bfloat16)
    sel16 = np.zeros((8, 16), np.float32)
    if core_id % 2 == 1:
        sel16[core_id - 1, :] = 1.0
    sel16 = sel16.astype(bfloat16)
    esel = np.zeros((16, 2048), np.float32)
    for t in range(16):
        esel[t, t * 128:(t + 1) * 128] = 1.0
    esel = esel.astype(bfloat16)
    bsel = np.zeros((128, 256), np.float32)
    for t in range(16):
        bsel[:, t * 16 + t] = 1.0
    bsel = bsel.astype(bfloat16)
    off = (core_id % 2) * T
    pos = (np.arange(T, dtype=np.float32).reshape(16, 128).T + np.float32(off)).astype(np.float32)
    return dict(c_triuT=triuT, c_tri16T=tri16T, c_ones16=ones16,
                c_sel16=sel16, c_esel=esel, c_pos=pos, c_bsel=bsel)


def kernel(L0_attn_x, L0_attn_wqkv, L0_ffn_w_gate, L0_ffn_w_up, L0_ffn_w_down):
    x = np.ascontiguousarray(np.asarray(L0_attn_x, np.float32).reshape(16384, 1024))
    wqkv = np.ascontiguousarray(np.asarray(L0_attn_wqkv, np.float32))
    wg = np.ascontiguousarray(np.asarray(L0_ffn_w_gate, np.float32))
    wu = np.ascontiguousarray(np.asarray(L0_ffn_w_up, np.float32))
    wd = np.ascontiguousarray(np.asarray(L0_ffn_w_down, np.float32))

    nc = build()
    in_maps = []
    for c in range(8):
        m = dict(
            x=x[c * T:(c + 1) * T],
            wqkv_sl=wqkv[c * 128:(c + 1) * 128],
            wup_sl=wu[c * 384:(c + 1) * 384],
            wgate_sl=wg[c * 384:(c + 1) * 384],
            wdown_sl=wd[c * 128:(c + 1) * 128],
        )
        m.update(_consts(c))
        in_maps.append(m)
    res = run_bass_kernel_spmd(nc, in_maps, core_ids=list(range(8)))
    out = np.concatenate([np.asarray(r["out"]) for r in res.results], axis=0)
    return out.reshape(4, 4096, 1024)


# version 24
# speedup vs baseline: 1.0261x; 1.0261x over previous
import numpy as np
import ml_dtypes
from contextlib import ExitStack

import concourse.bass as bass
import concourse.mybir as mybir
from concourse.bass_utils import run_bass_kernel_spmd

F32 = mybir.dt.float32
BF16 = mybir.dt.bfloat16
AF = mybir.ActivationFunctionType
ALU = mybir.AluOpType
AX = mybir.AxisListType
bfloat16 = ml_dtypes.bfloat16

CM = 12582912.0
INV7 = float(np.float32(1.0 / 7.0))
INVQ = float(np.float32(1.0 / 32767.0))
P16 = float(np.float32(2.0 ** -16))
T = 2048
NT = 16
NH = 8
NI = 24


class SemC:
    def __init__(self, h):
        self.h = h
        self.n = 0

    def inc(self, k):
        self.n += k
        return self.n


def build():
    nc = bass.Bass("TRN2", target_bir_lowering=False)

    x_in = nc.dram_tensor("x", [T, 1024], F32, kind="ExternalInput").ap()
    wqkv_sl = nc.dram_tensor("wqkv_sl", [128, 3072], F32, kind="ExternalInput").ap()
    wup_sl = nc.dram_tensor("wup_sl", [384, 1024], F32, kind="ExternalInput").ap()
    wgate_sl = nc.dram_tensor("wgate_sl", [384, 1024], F32, kind="ExternalInput").ap()
    wdown_sl = nc.dram_tensor("wdown_sl", [128, 3072], F32, kind="ExternalInput").ap()
    c_triuT = nc.dram_tensor("c_triuT", [128, 128], BF16, kind="ExternalInput").ap()
    c_tri16T = nc.dram_tensor("c_tri16T", [16, 16], BF16, kind="ExternalInput").ap()
    c_ones16 = nc.dram_tensor("c_ones16", [16, 1], BF16, kind="ExternalInput").ap()
    c_sel16 = nc.dram_tensor("c_sel16", [8, 16], BF16, kind="ExternalInput").ap()
    c_esel = nc.dram_tensor("c_esel", [16, 2048], BF16, kind="ExternalInput").ap()
    c_pos = nc.dram_tensor("c_pos", [128, 16], F32, kind="ExternalInput").ap()
    c_bsel = nc.dram_tensor("c_bsel", [128, 256], BF16, kind="ExternalInput").ap()
    out_d = nc.dram_tensor("out", [T, 1024], F32, kind="ExternalOutput").ap()

    dram = lambda n, s, d: nc.dram_tensor(n, s, d).ap()
    xn_d = dram("xn_d", [T, 1024], BF16)
    wqkvq_in = dram("wqkvq_in", [128, 3072], BF16)
    wqkvq_all = dram("wqkvq_all", [1024, 3072], BF16)
    upgate_in = dram("upgate_in", [384, 2048], BF16)
    upgate_all = dram("upgate_all", [3072, 2048], BF16)
    wdownq_in = dram("wdownq_in", [128, 3072], BF16)
    wdownq_all = dram("wdownq_all", [1024, 3072], BF16)
    v_d = dram("v_d", [T, 1024], BF16)
    up_d = dram("up_d", [3072, T], BF16)
    gate_d = dram("gate_d", [3072, T], BF16)
    vsum_in = dram("vsum_in", [1, 1024], BF16)
    vsum_all = dram("vsum_all", [8, 1024], BF16)
    mrow_d = [dram(f"mrow{r}_d", [2 if r == 1 else 1, 128], F32) for r in range(4)]
    cc_in = [dram(f"cc{r}_in", [1, 8], F32) for r in range(4)]
    cc_out = [dram(f"cc{r}_out", [1, 8], F32) for r in range(4)]

    es = ExitStack()
    with es:
        sb = lambda n, s, d: es.enter_context(nc.sbuf_tensor(n, s, d))
        pst = lambda n, s, d: es.enter_context(nc.psum_tensor(n, s, d))
        mksem = lambda n: SemC(es.enter_context(nc.semaphore(n)))

        A = sb("A", [128, NH * T], BF16)
        B = sb("B", [128, NH * 3072], BF16)
        D = sb("Dw", [128, NH * 3072], BF16)
        gsl = sb("gsl", [128, 2 * 2048], BF16)
        upsl = sb("upsl", [128, 2048], BF16)
        ost = sb("ost", [128, 2 * 3072], BF16)
        fstage = ost[:, :].bitcast(F32)

        def wq6(a, b):
            a = int(a * 1024); b = int(b * 1024)
            if b <= 4096:
                assert a < b <= 4096
                return gsl[:, a:b]
            assert a >= 4096
            return upsl[:, a - 4096:b - 4096]
        xstage = sb("xstage", [128, 2048], F32)
        xbstage = sb("xbstage", [128, 2048], BF16)
        vstage = sb("vstage", [128, 2048], BF16)
        kstage = sb("kstage", [128, 1024], F32)
        tsc = sb("tsc", [128, 2 * 2048], F32)
        upstage = sb("upstage", [128, 2048], BF16)
        coefC = sb("coefC", [128, 256], F32)
        coefV = sb("coefV", [128, 256], F32)
        sdg = sb("sdg", [128, 256], F32)
        B_sb = sb("B_sb", [16, 1024], BF16)
        P_sb = sb("P_sb", [16, 1024], BF16)
        T_sb = P_sb[0:1, :]
        vg_sb = sb("vg_sb", [8, 1024], BF16)
        ctriu = sb("ctriu", [128, 128], BF16)
        ctri16 = sb("ctri16", [16, 16], BF16)
        cone16 = sb("cone16", [16, 1], BF16)
        csel = sb("csel", [8, 16], BF16)
        cesel = sb("cesel", [16, 2048], BF16)
        cpos = sb("cpos", [128, 16], F32)
        cbsel = sb("cbsel", [128, 256], BF16)
        runm = sb("runm", [128, 8], F32)
        mrow_sb = sb("mrow_sb", [1, 256], F32)
        msc_sb = sb("msc_sb", [1, 8], F32)
        gbc = sb("gbc", [128, 2], F32)
        sc = sb("sc", [128, 16], F32)
        lnst = sb("lnst", [128, 8], F32)

        psA = pst("psA", [128, 2048], F32)
        psB = pst("psB", [128, 2048], F32)

        sSP = mksem("sSP")
        sCC = mksem("sCC")
        sPE = mksem("sPE")
        sDV = mksem("sDV")
        sAC = mksem("sAC")
        sPC = mksem("sPC")
        sCst = mksem("sCst")
        sW = mksem("sW")
        sWg = mksem("sWg")
        sX = [mksem(f"sX{i}") for i in range(4)]
        sXs = [mksem("sXs0"), mksem("sXs1")]
        sTx = mksem("sTx"); sTu = mksem("sTu"); sTg = mksem("sTg"); sTd = mksem("sTd")
        sTuK = [mksem(f"sTuK{i}") for i in range(NH)]
        sTgK = [mksem(f"sTgK{i}") for i in range(NH)]
        sQw = mksem("sQw")
        sVs = mksem("sVs")
        sV = [mksem("sV0"), mksem("sV1")]
        sUsp = [mksem("sUsp0"), mksem("sUsp1")]
        sGl = [mksem(f"sGl{i}") for i in range(4)]
        sJl = [mksem(f"sJl{i}") for i in range(4)]
        sGs = [mksem(f"sGs{i}") for i in range(4)]
        sUl = [mksem(f"sUl{i}") for i in range(6)]
        sOb = [mksem("sOb0"), mksem("sOb1")]
        sO = [mksem("sO0"), mksem("sO1")]

        t2f = upstage[:, :].bitcast(F32)

        plan = {k: [] for k in ("sync", "gpsimd", "vector", "scalar", "tensor")}
        _chain = {"vector": sDV, "scalar": sAC}

        def em(eng, fn):
            sem = _chain.get(eng)
            if sem is None:
                plan[eng].append(fn)
                return None
            v0 = sem.n
            if v0 > 0:
                plan[eng].append(lambda e, s=sem.h, vv=v0: e.wait_ge(s, vv))
            v = sem.inc(1)
            plan[eng].append(lambda e, fn=fn, s=sem.h: fn(e).then_inc(s, 1))
            return v

        def empool(fn):
            v = sPC.inc(1)
            plan["gpsimd"].append(lambda e, fn=fn, s=sPC.h: fn(e).then_inc(s, 1))
            return v

        def wait(eng, sem, val):
            if val > 0:
                plan[eng].append(lambda e, s=sem.h, v=val: e.wait_ge(s, v))

        def dma(eng, sem, outap, inap):
            v = sem.inc(16)
            plan[eng].append(lambda e, o=outap, i=inap, s=sem.h: e.dma_start(out=o, in_=i).then_inc(s, 16))
            return v

        def dmaT(eng, sem, outap, inap):
            v = sem.inc(16)
            plan[eng].append(lambda e, o=outap, i=inap, s=sem.h: e.dma_start_transpose(o, i).then_inc(s, 16))
            return v

        def gs(j):
            return gsl[:, (j % 2) * 2048:(j % 2) * 2048 + 2048]

        def ts2(j):
            return tsc[:, (j % 2) * 2048:(j % 2) * 2048 + 2048]

        tscb = tsc[:, :].bitcast(BF16)

        def o_slot_c(j, c0, c1):
            if j < 8:
                return A[:, j * 2048 + c0:j * 2048 + c1]
            if j < 20:
                return B[:, (j - 8) * 2048 + c0:(j - 8) * 2048 + c1]
            if j < 22:
                return gsl[:, (j - 20) * 2048 + c0:(j - 20) * 2048 + c1]
            if j == 22:
                return upsl[:, c0:c1]
            return tscb[:, 4096 + c0:4096 + c1]

        def o_slot(j):
            return o_slot_c(j, 0, 2048)

        def gs3c(j, c0, c1):
            sl = j % 4
            if sl < 2:
                return gsl[:, sl * 2048 + c0:sl * 2048 + c1]
            if sl == 2:
                return upsl[:, c0:c1]
            return ost[:, 4096 + c0:4096 + c1]

        def gs3(j):
            return gs3c(j, 0, 2048)

        def sigb(j, h):
            buf = fstage if j % 2 == 0 else xstage
            return buf[:, h * 1024:(h + 1) * 1024]

        ld_consts = 0
        for o, i in ((ctriu[:, :], c_triuT), (ctri16[:, :], c_tri16T),
                     (cone16[:, :], c_ones16), (csel[:, :], c_sel16),
                     (cesel[:, :], c_esel), (cpos[:, :], c_pos),
                     (cbsel[:, :], c_bsel)):
            ld_consts = dma("sync", sCst, o, i)

        def dequant(src, dst, nblk, scr):
            w = nblk * 32
            src3 = src.rearrange("p (b k) -> p b k", k=32)
            em("vector", lambda v, s3=src3, nb=nblk, scr=scr: v.tensor_reduce(
                scr[:, 0:nb], s3, AX.X, ALU.max, apply_absolute_value=True))
            em("vector", lambda v, nb=nblk, scr=scr: v.tensor_scalar(
                scr[:, 128:128 + nb], scr[:, 0:nb], INV7, 1e-12, ALU.mult, ALU.max))
            em("vector", lambda v, nb=nblk, scr=scr: v.reciprocal(scr[:, 256:256 + nb], scr[:, 128:128 + nb]))
            em("vector", lambda v, s3=src3, nb=nblk, w=w, scr=scr: v.tensor_tensor(
                scr[:, 512:512 + w].rearrange("p (b k) -> p b k", k=32), s3,
                scr[:, 256:256 + nb].unsqueeze(2).broadcast_to([128, nb, 32]), ALU.mult))
            em("vector", lambda v, w=w, scr=scr: v.tensor_scalar(
                scr[:, 512:512 + w], scr[:, 512:512 + w], CM, -CM, ALU.add, ALU.add))
            return em("vector", lambda v, dst=dst, nb=nblk, w=w, scr=scr: v.tensor_tensor(
                dst.rearrange("p (b k) -> p b k", k=32),
                scr[:, 512:512 + w].rearrange("p (b k) -> p b k", k=32),
                scr[:, 128:128 + nb].unsqueeze(2).broadcast_to([128, nb, 32]), ALU.mult))

        ld_ugj = [0] * 3
        for j in range(3):
            for q in range(2):
                ld_ugj[j] = dma("sync", sW,
                                fstage[:, j * 1024 + q * 512:j * 1024 + (q + 1) * 512],
                                wup_sl[j * 128:(j + 1) * 128, q * 512:(q + 1) * 512])

        em("vector", lambda v: v.memset(runm[:, :], 0.0))
        ld_x = [0] * NT
        dv_ln = [0] * NT
        ac_ln = [0] * NT
        sp_xn = [0] * NT
        dv1 = [0] * NT
        dv_up3 = [None] * 3
        dv_g3 = [None] * 3
        ld_g3j = [0] * 3
        sp_ug = 0
        cc_ug = 0
        def lnx(t):
            sl = t % 4
            if sl < 2:
                return xstage[:, sl * 1024:(sl + 1) * 1024]
            return tsc[:, (sl - 2) * 1024 + 2048:(sl - 1) * 1024 + 2048]

        def ln1(t):
            xa = lnx(t)
            a = 4 * (t % 2)
            if t >= 4:
                wait("sync", sDV, dv_ln[t - 4])
                wait("sync", sAC, ac_ln[t - 4])
            for q in range(4):
                ld_x[t] = dma("sync", sX[t % 4], xa[:, q * 256:(q + 1) * 256],
                              x_in[t * 128:(t + 1) * 128, q * 256:(q + 1) * 256])
            wait("vector", sX[t % 4], ld_x[t])
            em("vector", lambda v, xa=xa, a=a: v.reduce_sum(lnst[:, a:a + 1], xa, AX.X))
            dv1[t] = em("vector", lambda v, a=a: v.tensor_scalar(
                lnst[:, a + 1:a + 2], lnst[:, a:a + 1], -1.0 / 1024.0, None, ALU.mult))

        def ln2(t):
            xa = lnx(t)
            xb = xbstage[:, (t % 2) * 1024:(t % 2) * 1024 + 1024]
            a = 4 * (t % 2)
            wait("scalar", sDV, dv1[t])
            ac1 = em("scalar", lambda s, xa=xa, a=a: s.activation(
                kstage[:, 0:1024], xa, AF.Square,
                bias=lnst[:, a + 1:a + 2], scale=1.0, accum_out=lnst[:, a + 3:a + 4]))
            wait("vector", sAC, ac1)
            dv2 = em("vector", lambda v, a=a: v.tensor_scalar(
                lnst[:, a + 3:a + 4], lnst[:, a + 3:a + 4], 1.0 / 1024.0, 1e-5, ALU.mult, ALU.add))
            wait("scalar", sDV, dv2)
            ac2 = em("scalar", lambda s, a=a: s.sqrt(lnst[:, a + 2:a + 3], lnst[:, a + 3:a + 4]))
            wait("vector", sAC, ac2)
            em("vector", lambda v, a=a: v.reciprocal(lnst[:, a + 2:a + 3], lnst[:, a + 2:a + 3]))
            dv3 = em("vector", lambda v, a=a: v.tensor_tensor(
                lnst[:, a + 1:a + 2], lnst[:, a + 1:a + 2], lnst[:, a + 2:a + 3], ALU.mult))
            wait("scalar", sDV, dv3)
            if t >= 2:
                wait("scalar", sXs[t % 2], sp_xn[t - 2])
            ac_ln[t] = em("scalar", lambda s, xa=xa, xb=xb, a=a: s.activation(
                xb, xa, AF.Identity, bias=lnst[:, a + 1:a + 2], scale=lnst[:, a + 2:a + 3]))
            wait("vector", sAC, ac_ln[t])
            em("vector", lambda v, xb=xb, a=a: v.tensor_reduce(
                lnst[:, a + 3:a + 4], xb, AX.X, ALU.max, apply_absolute_value=True))
            dv_ln[t] = em("vector", lambda v, a=a: v.tensor_tensor(
                runm[:, 0:1], runm[:, 0:1], lnst[:, a + 3:a + 4], ALU.max))
            wait("gpsimd", sAC, ac_ln[t])
            for q in range(2):
                sp_xn[t] = dma("gpsimd", sXs[t % 2],
                               xn_d[t * 128:(t + 1) * 128, q * 512:(q + 1) * 512],
                               xb[:, q * 512:(q + 1) * 512])

        for t in range(NT):
            ln1(t)
            if t >= 1:
                ln2(t - 1)
            if t in (1, 2, 3):
                j = t - 1
                wait("vector", sW, ld_ugj[2])
                dv_up3[j] = dequant(fstage[:, j * 1024:(j + 1) * 1024],
                                    wq6(2 * j, 2 * j + 1), 32, tsc[:, 0:2048])
            if t == 3:
                for j in range(3):
                    wait("gpsimd", sDV, dv_up3[j])
                    for q in range(2):
                        ld_g3j[j] = dma("gpsimd", sWg,
                                        fstage[:, j * 1024 + q * 512:j * 1024 + (q + 1) * 512],
                                        wgate_sl[j * 128:(j + 1) * 128, q * 512:(q + 1) * 512])
            if t in (5, 6, 7):
                j = t - 5
                wait("vector", sWg, ld_g3j[2])
                dv_g3[j] = dequant(fstage[:, j * 1024:(j + 1) * 1024],
                                   wq6(2 * j + 1, 2 * j + 2), 32, tsc[:, 0:2048])
            if t == 8:
                wait("gpsimd", sDV, dv_g3[2])
                for j3 in range(3):
                    for q in range(2):
                        sp_ug = dma("gpsimd", sSP,
                                    upgate_in[j3 * 128:(j3 + 1) * 128, q * 1024:(q + 1) * 1024],
                                    wq6(2 * j3 + q, 2 * j3 + q + 1))
                wait("gpsimd", sSP, sp_ug)
                cc_ug = sCC.inc(1)
                em("gpsimd", lambda g: g.collective_compute(
                    "AllGather", ALU.bypass, replica_groups=[list(range(8))],
                    ins=[upgate_in], outs=[upgate_all]).then_inc(sCC.h, 1))
        ln2(NT - 1)

        em("vector", lambda v: v.memset(sc[:, 10:11], CM))

        mr_state = {"msc_free": 0, "gbc_free": 0}

        def max_round(ridx, cols, run_cols, dv_ready):
            n = len(run_cols)
            rc0 = run_cols[0]
            assert run_cols == list(range(rc0, rc0 + n))
            wait("gpsimd", sDV, dv_ready)
            for ci in range(n):
                sp1 = dma("gpsimd", sSP, mrow_d[ridx][ci:ci + 1, :].rearrange("o p -> p o"),
                          runm[:, rc0 + ci:rc0 + ci + 1])
            wait("gpsimd", sSP, sp1)
            sp2 = dma("gpsimd", sSP, mrow_sb[:, 0:n * 128],
                      mrow_d[ridx].rearrange("o p -> (o p)").unsqueeze(0))
            wait("vector", sSP, sp2)
            wait("vector", sSP, mr_state["msc_free"])
            em("vector", lambda v, n=n: v.tensor_reduce(
                msc_sb[:, 0:n], mrow_sb[:, 0:n * 128].rearrange("a (o p) -> a o p", p=128),
                AX.X, ALU.max))
            dvz = None
            for ci in range(len(run_cols), 8):
                dvz = em("vector", lambda v, ci=ci: v.memset(msc_sb[:, ci:ci + 1], 0.0))
            wait("gpsimd", sDV, dvz)
            sp3 = dma("gpsimd", sSP, cc_in[ridx], msc_sb[:, :])
            mr_state["msc_free"] = sp3
            wait("gpsimd", sSP, sp3)
            wait("gpsimd", sCC, sCC.n)
            ccv = sCC.inc(1)
            em("gpsimd", lambda g, ridx=ridx: g.collective_compute(
                "AllReduce", ALU.max, replica_groups=[list(range(8))],
                ins=[cc_in[ridx]], outs=[cc_out[ridx]]).then_inc(sCC.h, 1))
            wait("gpsimd", sCC, ccv)
            if mr_state["gbc_free"]:
                wait("gpsimd", sDV, mr_state["gbc_free"])
            n = len(run_cols)
            sp4 = dma("gpsimd", sSP, gbc[:, 0:n], cc_out[ridx][0:1, 0:n].broadcast_to([128, n]))
            wait("vector", sSP, sp4)
            dvs = None
            for ci, c in enumerate(cols):
                em("vector", lambda v, ci=ci, c=c: v.tensor_scalar(
                    sc[:, 2 * c:2 * c + 1], gbc[:, ci:ci + 1], INVQ, 1e-12, ALU.mult, ALU.max))
                dvs = em("vector", lambda v, c=c: v.reciprocal(sc[:, 2 * c + 1:2 * c + 2], sc[:, 2 * c:2 * c + 1]))
            mr_state["gbc_free"] = dvs
            return ccv, dvs

        cc_m0, dv_s0 = max_round(0, [0], [0], dv_ln[NT - 1])

        wait("sync", sXs[0], sp_xn[NT - 2])
        wait("sync", sXs[1], sp_xn[NT - 1])
        tr_xn1 = [0] * NH
        for k in range(NH):
            tr_xn1[k] = dmaT("sync", sTx, A[:, k * 2048:(k + 1) * 2048], xn_d[:, k * 128:(k + 1) * 128])
        wait("sync", sCC, cc_ug)
        tru_v = [0] * NH
        trg_v = [0] * NH
        for k in range(NH):
            tru_v[k] = dmaT("sync", sTuK[k], D[:, k * 3072:(k + 1) * 3072],
                            upgate_all[:, k * 128:(k + 1) * 128])
        for k in range(NH):
            trg_v[k] = dmaT("sync", sTgK[k], B[:, k * 3072:(k + 1) * 3072],
                            upgate_all[:, 1024 + k * 128: 1024 + (k + 1) * 128])
        dv_xq = [0] * NH
        for k in range(NH):
            if k == 0:
                wait("vector", sTx, tr_xn1[NH - 1])
                wait("vector", sDV, dv_s0)
            em("vector", lambda v, k=k: v.tensor_scalar(
                tsc[:, 0:2048], A[:, k * 2048:(k + 1) * 2048], sc[:, 1:2], CM, ALU.mult, ALU.add))
            dv_xq[k] = em("vector", lambda v, k=k: v.tensor_scalar(
                A[:, k * 2048:(k + 1) * 2048], tsc[:, 0:2048], -CM, sc[:, 0:1], ALU.add, ALU.mult))

        wait("sync", sDV, dv_g3[2])
        ld_wqkv = 0
        for q in range(4):
            ld_wqkv = dma("sync", sW, fstage[:, q * 768:(q + 1) * 768],
                          wqkv_sl[:, q * 768:(q + 1) * 768])
        wait("vector", sW, ld_wqkv)
        wait("vector", sSP, sp_ug)
        dv_wqkv = em("vector", lambda v: v.tensor_copy(wq6(0, 3), fstage[:, 0:3072]))
        wait("gpsimd", sDV, dv_wqkv)
        sp_wqkv = 0
        for q in range(3):
            sp_wqkv = dma("gpsimd", sSP, wqkvq_in[:, q * 1024:(q + 1) * 1024], wq6(q, q + 1))
        wait("gpsimd", sSP, sp_wqkv)
        wait("gpsimd", sCC, sCC.n)
        cc_wqkv = sCC.inc(1)
        em("gpsimd", lambda g: g.collective_compute(
            "AllGather", ALU.bypass, replica_groups=[list(range(8))],
            ins=[wqkvq_in], outs=[wqkvq_all]).then_inc(sCC.h, 1))

        wait("sync", sDV, dv_wqkv)
        ld_wd = 0
        for q in range(4):
            ld_wd = dma("sync", sW, fstage[:, q * 768:(q + 1) * 768],
                        wdown_sl[:, q * 768:(q + 1) * 768])
        wait("vector", sW, ld_wd)
        wait("vector", sSP, sp_wqkv)
        dequant(fstage[:, 0:1536], wq6(0, 1.5), 48, tsc[:, 0:2048])
        dv_wd = dequant(fstage[:, 1536:3072], wq6(1.5, 3), 48, tsc[:, 0:2048])
        wait("gpsimd", sDV, dv_wd)
        sp_wd = 0
        for q in range(3):
            sp_wd = dma("gpsimd", sSP, wdownq_in[:, q * 1024:(q + 1) * 1024], wq6(q, q + 1))
        wait("gpsimd", sSP, sp_wd)
        wait("gpsimd", sCC, sCC.n)
        cc_wd = sCC.inc(1)
        em("gpsimd", lambda g: g.collective_compute(
            "AllGather", ALU.bypass, replica_groups=[list(range(8))],
            ins=[wdownq_in], outs=[wdownq_all]).then_inc(sCC.h, 1))

        upev_ac = {}
        upev_dv = {}
        sp_out_w = {}
        mm_pe = {}
        regions = [psA[:, 0:1024], psA[:, 1024:2048], psB[:, 0:1024], psB[:, 1024:2048]]
        halfseq = []

        def emit_mm_phase(which):
            W = D if which == "up" else B
            dst = up_d if which == "up" else gate_d
            runcol = 1 if which == "up" else 2
            ksem = sTuK if which == "up" else sTgK
            kval = tru_v if which == "up" else trg_v
            for i in range(NI):
                for half in range(2):
                    key = (which, i, half)
                    halfseq.append(key)
                    g = len(halfseq) - 1
                    slot = regions[g % 4]
                    if g >= 4:
                        prev = halfseq[g - 4]
                        wait("tensor", sAC, upev_ac[prev])
                        wait("tensor", sDV, upev_dv[prev])
                    for k in range(NH):
                        if i == 0 and half == 0:
                            wait("tensor", ksem[k], kval[k])
                            wait("tensor", sDV, dv_xq[k])
                        for n2 in range(2):
                            fn = lambda te, W=W, i=i, k=k, half=half, n2=n2, slot=slot: te.matmul(
                                slot[:, n2 * 512:(n2 + 1) * 512],
                                W[:, k * 3072 + i * 128: k * 3072 + (i + 1) * 128],
                                A[:, k * 2048 + half * 1024 + n2 * 512: k * 2048 + half * 1024 + (n2 + 1) * 512],
                                start=(k == 0), stop=(k == NH - 1))
                            if k == NH - 1 and n2 == 1:
                                mm_pe[key] = sPE.inc(1)
                                plan["tensor"].append(lambda te, fn=fn: fn(te).then_inc(sPE.h, 1))
                            else:
                                plan["tensor"].append(fn)
                    wait("scalar", sPE, mm_pe[key])
                    wkey = (which, i - 1)
                    if which == "gate" and i < 1:
                        wkey = ("up", NI - 1)
                    if wkey in sp_out_w:
                        wait("scalar", sUsp[wkey[1] % 2], sp_out_w[wkey])
                    us = upstage[:, half * 1024:half * 1024 + 1024]
                    upev_ac[key] = em("scalar", lambda s, us=us, slot=slot: s.copy(us, slot))
                    wait("vector", sAC, upev_ac[key])
                    em("vector", lambda v, slot=slot: v.tensor_reduce(
                        lnst[:, 3:4], slot, AX.X, ALU.max, apply_absolute_value=True))
                    upev_dv[key] = em("vector", lambda v, rc=runcol: v.tensor_tensor(
                        runm[:, rc:rc + 1], runm[:, rc:rc + 1], lnst[:, 3:4], ALU.max))
                wait("sync", sAC, upev_ac[(which, i, 1)])
                for q in range(2):
                    sp_out_w[(which, i)] = dma(
                        "sync", sUsp[i % 2],
                        dst[i * 128:(i + 1) * 128, q * 1024:(q + 1) * 1024],
                        upstage[:, q * 1024:(q + 1) * 1024])

        emit_mm_phase("up")
        emit_mm_phase("gate")

        cc_m12, dv_s12a = max_round(1, [1, 2], [1, 2], upev_dv[("gate", NI - 1, 1)])
        dv_s12 = em("vector", lambda v: v.tensor_scalar(
            sc[:, 11:12], sc[:, 4:5], -CM, None, ALU.mult))

        wait("sync", sPE, mm_pe[("gate", NI - 1, 1)])
        wait("sync", sCC, cc_wqkv)
        ld_wq = [0] * NH
        for k in range(NH):
            ld_wq[k] = dma("sync", sQw, B[:, k * 3072:(k + 1) * 3072], wqkvq_all[k * 128:(k + 1) * 128, :])
        tr_xn2 = [0] * NH
        for k in range(NH):
            tr_xn2[k] = dmaT("sync", sTx, A[:, k * 2048:(k + 1) * 2048], xn_d[:, k * 128:(k + 1) * 128])

        g1_dv = [0] * NI
        ld_gt = [0] * NI
        sp_g1 = [0] * NI
        g1_pc = [0] * NI
        dvmv = [0] * NI

        def redG(j):
            wait("vector", sPC, g1_pc[j])
            em("vector", lambda v, j=j: v.tensor_reduce(
                lnst[:, 3:4], gs3(j), AX.X, ALU.max, apply_absolute_value=True))
            g1_dv[j] = em("vector", lambda v: v.tensor_tensor(
                runm[:, 3:4], runm[:, 3:4], lnst[:, 3:4], ALU.max))

        def chainG(j):
            if j >= 4:
                wait("sync", sGl[j % 4], ld_gt[j - 4])
                wait("sync", sGs[j % 4], sp_g1[j - 4])
                wait("sync", sDV, g1_dv[j - 4])
            for q in range(4):
                ld_gt[j] = dma("sync", sGl[j % 4], gs3c(j, q * 512, (q + 1) * 512),
                               gate_d[j * 128:(j + 1) * 128, q * 512:(q + 1) * 512])
            wait("scalar", sGl[j % 4], ld_gt[j])
            if j == 0:
                wait("scalar", sDV, dv_s12)
            if j >= 2:
                wait("scalar", sDV, dvmv[j - 2])
            ac1 = em("scalar", lambda s, j=j: s.activation(ts2(j), gs3(j), AF.Identity, bias=sc[:, 10:11], scale=sc[:, 5:6]))
            wait("vector", sAC, ac1)
            dvv = em("vector", lambda v, j=j: v.tensor_scalar(gs3(j), ts2(j), -CM, sc[:, 4:5], ALU.add, ALU.mult))
            if j >= 2:
                wait("scalar", sPC, g1_pc[j - 2])
            acs = 0
            for h in range(2):
                sb_ = sigb(j, h)
                em("scalar", lambda s, j=j, h=h, sb_=sb_: s.activation(
                    sb_, tsc[:, (j % 2) * 2048 + h * 1024:(j % 2) * 2048 + (h + 1) * 1024],
                    AF.Sigmoid, bias=sc[:, 11:12], scale=sc[:, 4:5]))
                acs = em("scalar", lambda s, sb_=sb_: s.activation(
                    sb_, sb_, AF.Identity, bias=sc[:, 10:11], scale=65536.0))
            wait("vector", sAC, acs)
            for h in range(2):
                sb_ = sigb(j, h)
                dvmv[j] = em("vector", lambda v, sb_=sb_: v.tensor_scalar(
                    sb_, sb_, -CM, 65535.0, ALU.add, ALU.min))
            wait("gpsimd", sDV, dvmv[j])
            for h in range(2):
                gh = gs3c(j, h * 1024, (h + 1) * 1024)
                g1_pc[j] = empool(lambda g, gh=gh, sb_=sigb(j, h): g.tensor_tensor(gh, sb_, gh, ALU.mult))
            wait("gpsimd", sPC, g1_pc[j])
            for q in range(4):
                sp_g1[j] = dma("gpsimd", sGs[j % 4],
                               gate_d[j * 128:(j + 1) * 128, q * 512:(q + 1) * 512],
                               gs3c(j, q * 512, (q + 1) * 512))
            if j >= 1:
                redG(j - 1)
            if j == NI - 1:
                redG(j)

        g2_dv = [0] * NI
        ld_g1 = [0] * NI
        ld_ur = [0] * NI
        sp_g2 = [0] * NI
        g2_pc = [0] * NI
        actu_ms = [0] * NI
        actg_ms = [0] * NI
        dvvv = [0] * NI

        def redI(j):
            O = o_slot(j) if j < 20 else gsI(j)
            wait("vector", sPC, g2_pc[j])
            em("vector", lambda v, j=j, O=O: v.tensor_reduce(
                lnst[:, 3:4], O, AX.X, ALU.max, apply_absolute_value=True))
            g2_dv[j] = em("vector", lambda v: v.tensor_tensor(
                runm[:, 4:5], runm[:, 4:5], lnst[:, 3:4], ALU.max))

        def gsI(j, c0=0, c1=2048):
            sl = j % 4
            if sl < 2:
                return gsl[:, sl * 2048 + c0:sl * 2048 + c1]
            if sl == 2:
                return upsl[:, c0:c1]
            return vstage[:, c0:c1]

        kb = kstage[:, :].bitcast(BF16)

        def chainI(j):
            if j == 0:
                wait("sync", sDV, g1_dv[NI - 1])
            if j == 3:
                wait("sync", sVs, sp_v[NT - 1])
                wait("sync", sPE, pe_B_last[0])
            if j < 3:
                wait("sync", sGl[j % 4], ld_gt[20 + j])
                wait("sync", sGs[j % 4], sp_g1[20 + j])
            elif j >= 4:
                wait("sync", sGl[j % 4], ld_g1[j - 4])
                wait("sync", sAC, actg_ms[j - 4])
            for q in range(4):
                ld_g1[j] = dma("sync", sGl[j % 4], gsI(j, q * 512, (q + 1) * 512),
                               gate_d[j * 128:(j + 1) * 128, q * 512:(q + 1) * 512])
            if j == 0:
                wait("sync", sPE, pe_half[NT - 1][1])
            if j == 20:
                wait("sync", sAC, ac_ev[NT - 1][1])
                wait("sync", sDV, dv_ev[NT - 1][1])
            if j > 20:
                wait("sync", sAC, actu_ms[j - 1])
            ub = o_slot(j) if j < 20 else kb
            for q in range(2):
                ld_ur[j] = dma("sync", sUl[j % 6], o_slot_c(j, q * 1024, (q + 1) * 1024)
                               if j < 20 else kb[:, q * 1024:(q + 1) * 1024],
                               up_d[j * 128:(j + 1) * 128, q * 1024:(q + 1) * 1024])
            wait("scalar", sGl[j % 4], ld_g1[j])
            if j == 0:
                wait("scalar", sDV, mr_state["dv_s3"])
                wait("scalar", sDV, dv_coef)
            if j == 1:
                wait("scalar", sGs[3], sp_g1[23])
            if j >= 2:
                wait("scalar", sDV, dvvv[j - 2])
            actg_ms[j] = em("scalar", lambda s, j=j: s.activation(ts2(j), gsI(j), AF.Identity, bias=sc[:, 10:11], scale=sc[:, 7:8]))
            wait("vector", sAC, actg_ms[j])
            g2b = ost[:, 0:2048] if j % 2 == 0 else ost[:, 2048:4096]
            if j >= 2:
                wait("vector", sPC, g2_pc[j - 2])
            dvvv[j] = em("vector", lambda v, j=j, g2b=g2b: v.tensor_scalar(g2b, ts2(j), -CM, sc[:, 6:7], ALU.add, ALU.mult))
            wait("scalar", sUl[j % 6], ld_ur[j])
            if j >= 2:
                wait("scalar", sPC, g2_pc[j - 2])
            gb0 = (j % 2) * 2048
            if j % 2 == 0:
                ub2h = (xstage[:, 0:2048],)
            else:
                ub2h = (fstage[:, 2048:3072], t2f[:, 0:1024])
            for hi, uh in enumerate(ub2h):
                n = len(ub2h)
                src = (ub if n == 1 else
                       (o_slot_c(j, hi * 1024, (hi + 1) * 1024) if j < 20
                        else kb[:, hi * 1024:(hi + 1) * 1024]))
                actu_ms[j] = em("scalar", lambda s, uh=uh, src=src: s.activation(
                    src if False else uh, src, AF.Identity, bias=sc[:, 10:11], scale=sc[:, 3:4]))
            wait("vector", sAC, actu_ms[j])
            dvu = 0
            for uh in ub2h:
                dvu = em("vector", lambda v, uh=uh: v.tensor_scalar(uh, uh, -CM, None, ALU.add))
            wait("gpsimd", sDV, dvu)
            if len(ub2h) == 1:
                O = o_slot(j) if j < 20 else gsI(j)
                g2_pc[j] = empool(lambda g, O=O, g2b=g2b, uh=ub2h[0]: g.tensor_tensor(O, uh, g2b, ALU.mult))
            else:
                for hi, uh in enumerate(ub2h):
                    Oh = o_slot_c(j, hi * 1024, (hi + 1) * 1024) if j < 20 else gsI(j, hi * 1024, (hi + 1) * 1024)
                    gbh = ost[:, gb0 + hi * 1024:gb0 + (hi + 1) * 1024]
                    g2_pc[j] = empool(lambda g, Oh=Oh, gbh=gbh, uh=uh: g.tensor_tensor(Oh, uh, gbh, ALU.mult))
            if j >= 20:
                wait("gpsimd", sPC, g2_pc[j])
                for q in range(4):
                    sp_g2[j] = dma("gpsimd", sGs[j % 4],
                                   up_d[j * 128:(j + 1) * 128, q * 512:(q + 1) * 512],
                                   gsI(j, q * 512, (q + 1) * 512))
            if j >= 1:
                redI(j - 1)
            if j == NI - 1:
                redI(j)

        o_dv = [0] * NI
        ld_g2 = [0] * NI
        toenc_ms = [0] * NI

        def chainJ(j):
            if j >= 20:
                if j == 20:
                    for s4 in range(4):
                        wait("sync", sGs[s4], sp_g2[20 + s4])
                    wait("sync", sDV, g2_dv[NI - 1])
                    wait("sync", sPC, g2_pc[NI - 1])
                    wait("sync", sAC, actu_ms[NI - 1])
                if j == 23:
                    wait("sync", sDV, o_dv[22])
                for q in range(2):
                    ld_g2[j] = dma("sync", sJl[j - 20], o_slot_c(j, q * 1024, (q + 1) * 1024),
                                   up_d[j * 128:(j + 1) * 128, q * 1024:(q + 1) * 1024])
            if j == 0:
                wait("scalar", sDV, mr_state["dv_s4"])
            if j >= 2:
                wait("scalar", sDV, o_dv[j - 2])
            if j >= 20:
                wait("scalar", sJl[j - 20], ld_g2[j])
            tb = ts2(j + 1)
            toenc_ms[j] = em("scalar", lambda s, j=j, tb=tb: s.activation(
                tb, o_slot(j), AF.Identity, bias=sc[:, 10:11], scale=sc[:, 9:10]))
            wait("vector", sAC, toenc_ms[j])
            o_dv[j] = em("vector", lambda v, j=j, tb=tb: v.tensor_scalar(
                o_slot(j), tb, -CM, sc[:, 8:9], ALU.add, ALU.mult))

        pe_half = [[0, 0] for _ in range(NT)]
        ac_ev = [[0, 0] for _ in range(NT)]
        dv_ev = [[0, 0] for _ in range(NT)]
        sp_v = [0] * NT
        pe_B_last = [0]

        def qkv_pe(t):
            for h in range(2):
                ps = psA if h == 0 else psB
                if t >= 1:
                    wait("tensor", sAC, ac_ev[t - 1][h])
                    wait("tensor", sDV, dv_ev[t - 1][h])
                for k in range(NH):
                    for n3 in range(3):
                        fn = lambda te, ps=ps, k=k, n3=n3, t=t, h=h: te.matmul(
                            ps[:, n3 * 512:(n3 + 1) * 512],
                            A[:, k * 2048 + t * 128: k * 2048 + (t + 1) * 128],
                            B[:, k * 3072 + h * 1536 + n3 * 512: k * 3072 + h * 1536 + (n3 + 1) * 512],
                            start=(k == 0), stop=(k == NH - 1))
                        if k == NH - 1 and n3 == 2:
                            pe_half[t][h] = sPE.inc(1)
                            plan["tensor"].append(lambda te, fn=fn: fn(te).then_inc(sPE.h, 1))
                        else:
                            plan["tensor"].append(fn)

        def qkv_B(t):
            vt = vstage[:, (t % 2) * 1024:(t % 2) * 1024 + 1024]
            wait("tensor", sAC, ac_ev[t][1])
            plan["tensor"].append(lambda te, vt=vt, t=t: te.matmul(
                psA[0:16, 1536:2048], cbsel[:, t * 16:(t + 1) * 16], vt[:, 0:512],
                start=(t == 0), stop=(t == NT - 1)))
            fnB = lambda te, vt=vt, t=t: te.matmul(
                psB[0:16, 1536:2048], cbsel[:, t * 16:(t + 1) * 16], vt[:, 512:1024],
                start=(t == 0), stop=(t == NT - 1))
            if t == NT - 1:
                pe_B_last[0] = sPE.inc(1)
                plan["tensor"].append(lambda te, fnB=fnB: fnB(te).then_inc(sPE.h, 1))
            else:
                plan["tensor"].append(fnB)

        def qkv_evac(t):
            for h in range(2):
                ps = psA if h == 0 else psB
                wait("scalar", sPE, pe_half[t][h])
                if h == 1:
                    wait("scalar", sDV, dv_ev[t][0])
                elif t >= 1:
                    wait("scalar", sDV, dv_ev[t - 1][1])
                    wait("scalar", sVs, sp_v[t - 1])
                vsl = vstage[:, (t % 2) * 1024 + h * 512:(t % 2) * 1024 + h * 512 + 512]
                em("scalar", lambda s, ps=ps, vsl=vsl: s.copy(
                    vsl.rearrange("p (hh dd) -> p hh dd", dd=64),
                    ps[:, 0:1536].rearrange("p (hh c) -> p hh c", c=192)[:, :, 128:192]))
                ac_ev[t][h] = em("scalar", lambda s, ps=ps: s.copy(
                    kstage[:, 0:512].rearrange("p (hh dd) -> p hh dd", dd=64),
                    ps[:, 0:1536].rearrange("p (hh c) -> p hh c", c=192)[:, :, 64:128]))
                wait("vector", sAC, ac_ev[t][h])
                em("vector", lambda v, ps=ps: v.tensor_tensor(
                    kstage[:, 0:512].rearrange("p (hh dd) -> p hh dd", dd=64),
                    ps[:, 0:1536].rearrange("p (hh c) -> p hh c", c=192)[:, :, 0:64],
                    kstage[:, 0:512].rearrange("p (hh dd) -> p hh dd", dd=64), ALU.mult))
                dv_ev[t][h] = em("vector", lambda v, t=t, h=h: v.tensor_reduce(
                    sdg[:, t * 16 + h * 8: t * 16 + h * 8 + 8],
                    kstage[:, 0:512].rearrange("p (hh dd) -> p hh dd", dd=64),
                    AX.X, ALU.add))
            vt = vstage[:, (t % 2) * 1024:(t % 2) * 1024 + 1024]
            wait("gpsimd", sAC, ac_ev[t][1])
            sp_v[t] = dma("gpsimd", sVs, v_d[t * 128:(t + 1) * 128, :], vt)

        wait("tensor", sCst, ld_consts)
        wait("tensor", sQw, ld_wq[NH - 1])
        wait("tensor", sTx, tr_xn2[NH - 1])
        for i in (NI - 2, NI - 1):
            for h in (0, 1):
                wait("tensor", sAC, upev_ac[("gate", i, h)])
                wait("tensor", sDV, upev_dv[("gate", i, h)])
        wait("sync", sUsp[0], sp_out_w[("gate", NI - 2)])
        wait("sync", sUsp[1], sp_out_w[("gate", NI - 1)])
        wait("sync", sSP, sp_wd)

        chain_left = list(range(NI))
        for t in range(NT):
            qkv_pe(t)
            if t >= 1:
                qkv_B(t - 1)
            for _ in range(2 if t % 2 == 0 else 1):
                if chain_left:
                    chainG(chain_left.pop(0))
            qkv_evac(t)
        qkv_B(NT - 1)

        cc_m3, dv_s3a = max_round(2, [3], [3], g1_dv[NI - 1])
        em("vector", lambda v: v.tensor_scalar(
            sc[:, 6:7], gbc[:, 0:1], float(np.float32(P16) * np.float32(INVQ)), 1e-12, ALU.mult, ALU.max))
        em("vector", lambda v: v.reciprocal(sc[:, 7:8], sc[:, 6:7]))
        dv_s3 = em("vector", lambda v: v.tensor_scalar(sc[:, 7:8], sc[:, 7:8], P16, None, ALU.mult))
        mr_state["gbc_free"] = dv_s3
        mr_state["dv_s3"] = dv_s3

        wait("vector", sPE, pe_B_last[0])
        em("vector", lambda v: v.tensor_copy(B_sb[:, 0:512], psA[0:16, 1536:2048]))
        dv_Bev = em("vector", lambda v: v.tensor_copy(B_sb[:, 512:1024], psB[0:16, 1536:2048]))
        wait("tensor", sDV, dv_Bev)
        plan["tensor"].append(lambda te: te.matmul(psA[0:1, 0:512], cone16[:, :], B_sb[:, 0:512], start=True, stop=True))
        pe_T = sPE.inc(1)
        plan["tensor"].append(lambda te: te.matmul(psA[0:1, 512:1024], cone16[:, :], B_sb[:, 512:1024], start=True, stop=True).then_inc(sPE.h, 1))
        wait("vector", sPE, pe_T)
        dv_T = em("vector", lambda v: v.tensor_copy(T_sb, psA[0:1, 0:1024]))
        wait("gpsimd", sDV, dv_T)
        sp_T = dma("gpsimd", sSP, vsum_in, T_sb)
        wait("gpsimd", sSP, sp_T)
        wait("gpsimd", sCC, sCC.n)
        cc_carry = sCC.inc(1)
        em("gpsimd", lambda g: g.collective_compute(
            "AllGather", ALU.bypass, replica_groups=[list(range(8))],
            ins=[vsum_in], outs=[vsum_all]).then_inc(sCC.h, 1))

        wait("vector", sCst, ld_consts)
        wait("vector", sPC, g1_pc[NI - 1])
        em("vector", lambda v: v.tensor_scalar(xstage[:, 0:256], sdg[:, :], 0.125, 0.0, ALU.mult, ALU.max))
        dv_m = em("vector", lambda v: v.scalar_tensor_tensor(
            xstage[:, 256:512], sdg[:, :], 0.125, xstage[:, 0:256], ALU.mult, ALU.subtract))
        wait("scalar", sDV, dv_m)
        em("scalar", lambda s: s.activation(xstage[:, 512:768], xstage[:, 0:256], AF.Exp, scale=-1.0))
        ac_ab = em("scalar", lambda s: s.activation(xstage[:, 768:1024], xstage[:, 256:512], AF.Exp))
        wait("vector", sAC, ac_ab)
        em("vector", lambda v: v.tensor_tensor(
            xstage[:, 1024:1280].rearrange("p (tt hh) -> p tt hh", hh=16),
            xstage[:, 512:768].rearrange("p (tt hh) -> p tt hh", hh=16),
            cpos[:, :].unsqueeze(2).broadcast_to([128, 16, 16]), ALU.mult))
        em("vector", lambda v: v.tensor_tensor(xstage[:, 1024:1280], xstage[:, 1024:1280], xstage[:, 768:1024], ALU.add))
        em("vector", lambda v: v.reciprocal(xstage[:, 1280:1536], xstage[:, 1024:1280]))
        em("vector", lambda v: v.tensor_tensor(coefC[:, :], xstage[:, 512:768], xstage[:, 1280:1536], ALU.mult))
        em("vector", lambda v: v.tensor_tensor(xstage[:, 1536:1792], xstage[:, 768:1024], xstage[:, 512:768], ALU.subtract))
        dv_coef = em("vector", lambda v: v.tensor_tensor(coefV[:, :], xstage[:, 1536:1792], xstage[:, 1280:1536], ALU.mult))

        for j in range(NI):
            chainI(j)

        wait("gpsimd", sCC, cc_carry)
        sp_vg = dma("gpsimd", sOb[0], vg_sb[:, :], vsum_all)
        wait("tensor", sCst, ld_consts)
        wait("tensor", sOb[0], sp_vg)
        pe_P = 0
        for n2 in range(2):
            plan["tensor"].append(lambda te, n2=n2: te.matmul(
                psB[0:16, n2 * 512:(n2 + 1) * 512], ctri16[:, :],
                B_sb[:, n2 * 512:(n2 + 1) * 512], start=True, stop=False))
            fn = lambda te, n2=n2: te.matmul(
                psB[0:16, n2 * 512:(n2 + 1) * 512], csel[:, :],
                vg_sb[:, n2 * 512:(n2 + 1) * 512], start=False, stop=True)
            if n2 == 1:
                pe_P = sPE.inc(1)
                plan["tensor"].append(lambda te, fn=fn: fn(te).then_inc(sPE.h, 1))
            else:
                plan["tensor"].append(fn)
        wait("vector", sPE, pe_P)
        dv_P = em("vector", lambda v: v.tensor_copy(P_sb[:, :], psB[0:16, 0:1024]))

        cc_m4, dv_s4a = max_round(3, [4], [4], g2_dv[NI - 1])
        em("vector", lambda v: v.tensor_tensor(sc[:, 8:9], gbc[:, 0:1], sc[:, 2:3], ALU.mult))
        em("vector", lambda v: v.tensor_scalar(sc[:, 8:9], sc[:, 8:9], INVQ, 1e-12, ALU.mult, ALU.max))
        em("vector", lambda v: v.reciprocal(sc[:, 9:10], sc[:, 8:9]))
        dv_s4 = em("vector", lambda v: v.tensor_tensor(sc[:, 9:10], sc[:, 9:10], sc[:, 2:3], ALU.mult))
        mr_state["gbc_free"] = dv_s4
        mr_state["dv_s4"] = dv_s4

        for j in range(NI):
            chainJ(j)

        wait("sync", sCC, cc_wd)
        wait("sync", sPE, mm_pe[("up", NI - 1, 1)])
        tr_d = [0] * NI
        for k in range(NI):
            tr_d[k] = dmaT("sync", sTd, D[:, k * 1024:(k + 1) * 1024], wdownq_all[:, k * 128:(k + 1) * 128])

        dn_pe = [0] * NT
        dn_dv = [0] * NT
        attn_pe = [0] * NT
        attn_dv = [0] * NT
        ld_x2 = [0] * NT
        ld_at_v = [0] * NT
        st_out = [0] * NT
        wait("sync", sVs, sp_v[NT - 1])
        wait("sync", sPE, pe_B_last[0])
        wait("sync", sDV, dv_coef)
        wait("tensor", sTd, tr_d[NI - 1])
        wait("tensor", sDV, o_dv[NI - 1])

        for t in range(NT):
            if t >= 2:
                wait("sync", sDV, attn_dv[t - 2])
            for q in range(2):
                ld_at_v[t] = dma("sync", sV[t % 2],
                                 vstage[:, (t % 2) * 1024 + q * 512:(t % 2) * 1024 + (q + 1) * 512],
                                 v_d[t * 128:(t + 1) * 128, q * 512:(q + 1) * 512])
            if t >= 2:
                wait("sync", sDV, dn_dv[t - 2])
            for q in range(4):
                ld_x2[t] = dma("sync", sX[t % 2],
                               xstage[:, (t % 2) * 1024 + q * 256:(t % 2) * 1024 + (q + 1) * 256],
                               x_in[t * 128:(t + 1) * 128, q * 256:(q + 1) * 256])

            pc = psB[:, (t % 2) * 1024:(t % 2) * 1024 + 1024]
            if t >= 2:
                wait("tensor", sDV, attn_dv[t - 2])
            if t == 0:
                wait("tensor", sDV, dv_P)
            wait("tensor", sV[t % 2], ld_at_v[t])
            for n2 in range(2):
                plan["tensor"].append(lambda te, t=t, n2=n2, pc=pc: te.matmul(
                    pc[:, n2 * 512:(n2 + 1) * 512], ctriu[:, :],
                    vstage[:, (t % 2) * 1024 + n2 * 512:(t % 2) * 1024 + (n2 + 1) * 512],
                    start=True, stop=False))
                fn = lambda te, t=t, n2=n2, pc=pc: te.matmul(
                    pc[:, n2 * 512:(n2 + 1) * 512], cesel[:, t * 128:(t + 1) * 128],
                    P_sb[:, n2 * 512:(n2 + 1) * 512], start=False, stop=True)
                if n2 == 1:
                    attn_pe[t] = sPE.inc(1)
                    plan["tensor"].append(lambda te, fn=fn: fn(te).then_inc(sPE.h, 1))
                else:
                    plan["tensor"].append(fn)

            slot = psA[:, (t % 2) * 1024:(t % 2) * 1024 + 1024]
            if t >= 2:
                wait("tensor", sDV, dn_dv[t - 2])
            elif t == 0:
                wait("tensor", sDV, dv_T)
            for k in range(NI):
                for n2 in range(2):
                    fn = lambda te, t=t, k=k, n2=n2, slot=slot: te.matmul(
                        slot[:, n2 * 512:(n2 + 1) * 512],
                        o_slot_c(k, t * 128, (t + 1) * 128),
                        D[:, k * 1024 + n2 * 512: k * 1024 + (n2 + 1) * 512],
                        start=(k == 0), stop=(k == NI - 1))
                    if k == NI - 1 and n2 == 1:
                        dn_pe[t] = sPE.inc(1)
                        plan["tensor"].append(lambda te, fn=fn: fn(te).then_inc(sPE.h, 1))
                    else:
                        plan["tensor"].append(fn)

            wait("vector", sPE, attn_pe[t])
            if t == 0:
                wait("vector", sDV, dv_coef)
            ab = xbstage[:, (t % 2) * 1024:(t % 2) * 1024 + 1024]
            em("vector", lambda v, t=t, pc=pc: v.tensor_tensor(
                kstage[:, 0:1024].rearrange("p (hh dd) -> p hh dd", dd=64),
                pc[:, :].rearrange("p (hh dd) -> p hh dd", dd=64),
                coefC[:, t * 16:(t + 1) * 16].unsqueeze(2).broadcast_to([128, 16, 64]), ALU.mult))
            em("vector", lambda v, t=t: v.tensor_tensor(
                t2f[:, 0:1024].rearrange("p (hh dd) -> p hh dd", dd=64),
                vstage[:, (t % 2) * 1024:(t % 2) * 1024 + 1024].rearrange("p (hh dd) -> p hh dd", dd=64),
                coefV[:, t * 16:(t + 1) * 16].unsqueeze(2).broadcast_to([128, 16, 64]), ALU.mult))
            attn_dv[t] = em("vector", lambda v, ab=ab: v.tensor_tensor(
                ab, kstage[:, 0:1024], t2f[:, 0:1024], ALU.add))

            wait("vector", sPE, dn_pe[t])
            wait("vector", sX[t % 2], ld_x2[t])
            if t >= 2:
                wait("vector", sO[t % 2], st_out[t - 2])
            osl = tsc[:, (t % 2) * 1024:(t % 2) * 1024 + 1024]
            em("vector", lambda v, t=t, slot=slot, osl=osl: v.tensor_tensor(
                osl, slot, xstage[:, (t % 2) * 1024:(t % 2) * 1024 + 1024], ALU.add))
            dn_dv[t] = em("vector", lambda v, t=t, osl=osl, ab=ab: v.tensor_tensor(
                osl, osl, ab, ALU.add))
            wait("gpsimd", sDV, dn_dv[t])
            for q in range(2):
                st_out[t] = dma("gpsimd", sO[t % 2],
                                out_d[t * 128:(t + 1) * 128, q * 512:(q + 1) * 512],
                                osl[:, q * 512:(q + 1) * 512])

        wait("gpsimd", sO[0], st_out[NT - 2])
        wait("gpsimd", sO[1], st_out[NT - 1])

        block = es.enter_context(nc.Block())

        @block.sync
        def _(sync):
            for fn in plan["sync"]:
                fn(sync)

        @block.gpsimd
        def _(gpsimd):
            for fn in plan["gpsimd"]:
                fn(gpsimd)

        @block.vector
        def _(vector):
            for fn in plan["vector"]:
                fn(vector)

        @block.scalar
        def _(scalar):
            for fn in plan["scalar"]:
                fn(scalar)

        @block.tensor
        def _(tensor):
            for fn in plan["tensor"]:
                fn(tensor)

    return nc


def _consts(core_id):
    triuT = np.triu(np.ones((128, 128), np.float32)).astype(bfloat16)
    tri16T = np.triu(np.ones((16, 16), np.float32), 1).astype(bfloat16)
    ones16 = np.ones((16, 1), np.float32).astype(bfloat16)
    sel16 = np.zeros((8, 16), np.float32)
    if core_id % 2 == 1:
        sel16[core_id - 1, :] = 1.0
    sel16 = sel16.astype(bfloat16)
    esel = np.zeros((16, 2048), np.float32)
    for t in range(16):
        esel[t, t * 128:(t + 1) * 128] = 1.0
    esel = esel.astype(bfloat16)
    bsel = np.zeros((128, 256), np.float32)
    for t in range(16):
        bsel[:, t * 16 + t] = 1.0
    bsel = bsel.astype(bfloat16)
    off = (core_id % 2) * T
    pos = (np.arange(T, dtype=np.float32).reshape(16, 128).T + np.float32(off)).astype(np.float32)
    return dict(c_triuT=triuT, c_tri16T=tri16T, c_ones16=ones16,
                c_sel16=sel16, c_esel=esel, c_pos=pos, c_bsel=bsel)


def kernel(L0_attn_x, L0_attn_wqkv, L0_ffn_w_gate, L0_ffn_w_up, L0_ffn_w_down):
    x = np.ascontiguousarray(np.asarray(L0_attn_x, np.float32).reshape(16384, 1024))
    wqkv = np.ascontiguousarray(np.asarray(L0_attn_wqkv, np.float32))
    wg = np.ascontiguousarray(np.asarray(L0_ffn_w_gate, np.float32))
    wu = np.ascontiguousarray(np.asarray(L0_ffn_w_up, np.float32))
    wd = np.ascontiguousarray(np.asarray(L0_ffn_w_down, np.float32))

    nc = build()
    in_maps = []
    for c in range(8):
        m = dict(
            x=x[c * T:(c + 1) * T],
            wqkv_sl=wqkv[c * 128:(c + 1) * 128],
            wup_sl=wu[c * 384:(c + 1) * 384],
            wgate_sl=wg[c * 384:(c + 1) * 384],
            wdown_sl=wd[c * 128:(c + 1) * 128],
        )
        m.update(_consts(c))
        in_maps.append(m)
    res = run_bass_kernel_spmd(nc, in_maps, core_ids=list(range(8)))
    out = np.concatenate([np.asarray(r["out"]) for r in res.results], axis=0)
    return out.reshape(4, 4096, 1024)


# version 26
# speedup vs baseline: 1.0429x; 1.0163x over previous
import numpy as np
import ml_dtypes
from contextlib import ExitStack

import concourse.bass as bass
import concourse.mybir as mybir
from concourse.bass_utils import run_bass_kernel_spmd

F32 = mybir.dt.float32
BF16 = mybir.dt.bfloat16
AF = mybir.ActivationFunctionType
ALU = mybir.AluOpType
AX = mybir.AxisListType
bfloat16 = ml_dtypes.bfloat16

CM = 12582912.0
INV7 = float(np.float32(1.0 / 7.0))
INVQ = float(np.float32(1.0 / 32767.0))
P16 = float(np.float32(2.0 ** -16))
T = 2048
NT = 16
NH = 8
NI = 24


class SemC:
    def __init__(self, h):
        self.h = h
        self.n = 0

    def inc(self, k):
        self.n += k
        return self.n


def build():
    nc = bass.Bass("TRN2", target_bir_lowering=False)

    x_in = nc.dram_tensor("x", [T, 1024], F32, kind="ExternalInput").ap()
    wqkv_sl = nc.dram_tensor("wqkv_sl", [128, 3072], F32, kind="ExternalInput").ap()
    wup_sl = nc.dram_tensor("wup_sl", [384, 1024], F32, kind="ExternalInput").ap()
    wgate_sl = nc.dram_tensor("wgate_sl", [384, 1024], F32, kind="ExternalInput").ap()
    wdown_sl = nc.dram_tensor("wdown_sl", [128, 3072], F32, kind="ExternalInput").ap()
    c_triuT = nc.dram_tensor("c_triuT", [128, 128], BF16, kind="ExternalInput").ap()
    c_tri16T = nc.dram_tensor("c_tri16T", [16, 16], BF16, kind="ExternalInput").ap()
    c_ones16 = nc.dram_tensor("c_ones16", [16, 1], BF16, kind="ExternalInput").ap()
    c_sel16 = nc.dram_tensor("c_sel16", [8, 16], BF16, kind="ExternalInput").ap()
    c_esel = nc.dram_tensor("c_esel", [16, 2048], BF16, kind="ExternalInput").ap()
    c_pos = nc.dram_tensor("c_pos", [128, 16], F32, kind="ExternalInput").ap()
    c_bsel = nc.dram_tensor("c_bsel", [128, 256], BF16, kind="ExternalInput").ap()
    out_d = nc.dram_tensor("out", [T, 1024], F32, kind="ExternalOutput").ap()

    dram = lambda n, s, d: nc.dram_tensor(n, s, d).ap()
    xn_d = dram("xn_d", [T, 1024], BF16)
    wqkvq_in = dram("wqkvq_in", [128, 3072], BF16)
    wqkvq_all = dram("wqkvq_all", [1024, 3072], BF16)
    upgate_in = dram("upgate_in", [384, 2048], BF16)
    upgate_all = dram("upgate_all", [3072, 2048], BF16)
    wdownq_in = dram("wdownq_in", [128, 3072], BF16)
    wdownq_all = dram("wdownq_all", [1024, 3072], BF16)
    v_d = dram("v_d", [T, 1024], BF16)
    up_d = dram("up_d", [3072, T], BF16)
    gate_d = dram("gate_d", [3072, T], BF16)
    vsum_in = dram("vsum_in", [1, 1024], BF16)
    vsum_all = dram("vsum_all", [8, 1024], BF16)
    mrow_d = [dram(f"mrow{r}_d", [2 if r == 1 else 1, 128], F32) for r in range(4)]
    cc_in = [dram(f"cc{r}_in", [1, 8], F32) for r in range(4)]
    cc_out = [dram(f"cc{r}_out", [1, 8], F32) for r in range(4)]

    es = ExitStack()
    with es:
        sb = lambda n, s, d: es.enter_context(nc.sbuf_tensor(n, s, d))
        pst = lambda n, s, d: es.enter_context(nc.psum_tensor(n, s, d))
        mksem = lambda n: SemC(es.enter_context(nc.semaphore(n)))

        A = sb("A", [128, NH * T], BF16)
        B = sb("B", [128, NH * 3072], BF16)
        D = sb("Dw", [128, NH * 3072], BF16)
        gsl = sb("gsl", [128, 2 * 2048], BF16)
        upsl = sb("upsl", [128, 2048], BF16)
        ost = sb("ost", [128, 2 * 3072], BF16)
        fstage = ost[:, :].bitcast(F32)

        def wq6(a, b):
            a = int(a * 1024); b = int(b * 1024)
            if b <= 4096:
                assert a < b <= 4096
                return gsl[:, a:b]
            assert a >= 4096
            return upsl[:, a - 4096:b - 4096]
        xstage = sb("xstage", [128, 2048], F32)
        xbstage = sb("xbstage", [128, 2048], BF16)
        vstage = sb("vstage", [128, 2048], BF16)
        kstage = sb("kstage", [128, 1024], F32)
        tsc = sb("tsc", [128, 2 * 2048], F32)
        upstage = sb("upstage", [128, 2048], BF16)
        coefC = sb("coefC", [128, 256], F32)
        coefV = sb("coefV", [128, 256], F32)
        sdg = sb("sdg", [128, 256], F32)
        B_sb = sb("B_sb", [16, 1024], BF16)
        P_sb = sb("P_sb", [16, 1024], BF16)
        T_sb = P_sb[0:1, :]
        vg_sb = sb("vg_sb", [8, 1024], BF16)
        ctriu = sb("ctriu", [128, 128], BF16)
        ctri16 = sb("ctri16", [16, 16], BF16)
        cone16 = sb("cone16", [16, 1], BF16)
        csel = sb("csel", [8, 16], BF16)
        cesel = sb("cesel", [16, 2048], BF16)
        cpos = sb("cpos", [128, 16], F32)
        cbsel = sb("cbsel", [128, 256], BF16)
        runm = sb("runm", [128, 8], F32)
        mrow_sb = sb("mrow_sb", [1, 256], F32)
        msc_sb = sb("msc_sb", [1, 8], F32)
        gbc = sb("gbc", [128, 2], F32)
        sc = sb("sc", [128, 16], F32)
        lnst = sb("lnst", [128, 8], F32)

        psA = pst("psA", [128, 2048], F32)
        psB = pst("psB", [128, 2048], F32)

        sSP = mksem("sSP")
        sCC = mksem("sCC")
        sPE = mksem("sPE")
        sDV = mksem("sDV")
        sAC = mksem("sAC")
        sPC = mksem("sPC")
        sCst = mksem("sCst")
        sW = mksem("sW")
        sWg = mksem("sWg")
        sX = [mksem(f"sX{i}") for i in range(4)]
        sXs = [mksem("sXs0"), mksem("sXs1")]
        sTx = mksem("sTx"); sTu = mksem("sTu"); sTg = mksem("sTg"); sTd = mksem("sTd")
        sQw = mksem("sQw")
        sVs = mksem("sVs")
        sV = [mksem("sV0"), mksem("sV1")]
        sUsp = [mksem("sUsp0"), mksem("sUsp1")]
        sGl = [mksem(f"sGl{i}") for i in range(4)]
        sJl = [mksem(f"sJl{i}") for i in range(4)]
        sGs = [mksem(f"sGs{i}") for i in range(4)]
        sUl = [mksem(f"sUl{i}") for i in range(6)]
        sOb = [mksem("sOb0"), mksem("sOb1")]
        sO = [mksem("sO0"), mksem("sO1")]

        t2f = upstage[:, :].bitcast(F32)

        plan = {k: [] for k in ("sync", "gpsimd", "vector", "scalar", "tensor")}
        _chain = {"vector": sDV, "scalar": sAC}

        def em(eng, fn):
            sem = _chain.get(eng)
            if sem is None:
                plan[eng].append(fn)
                return None
            v0 = sem.n
            if v0 > 0:
                plan[eng].append(lambda e, s=sem.h, vv=v0: e.wait_ge(s, vv))
            v = sem.inc(1)
            plan[eng].append(lambda e, fn=fn, s=sem.h: fn(e).then_inc(s, 1))
            return v

        def empool(fn):
            v = sPC.inc(1)
            plan["gpsimd"].append(lambda e, fn=fn, s=sPC.h: fn(e).then_inc(s, 1))
            return v

        def wait(eng, sem, val):
            if val > 0:
                plan[eng].append(lambda e, s=sem.h, v=val: e.wait_ge(s, v))

        def dma(eng, sem, outap, inap):
            v = sem.inc(16)
            plan[eng].append(lambda e, o=outap, i=inap, s=sem.h: e.dma_start(out=o, in_=i).then_inc(s, 16))
            return v

        def dmaT(eng, sem, outap, inap):
            v = sem.inc(16)
            plan[eng].append(lambda e, o=outap, i=inap, s=sem.h: e.dma_start_transpose(o, i).then_inc(s, 16))
            return v

        def gs(j):
            return gsl[:, (j % 2) * 2048:(j % 2) * 2048 + 2048]

        def ts2(j):
            return tsc[:, (j % 2) * 2048:(j % 2) * 2048 + 2048]

        tscb = tsc[:, :].bitcast(BF16)

        def o_slot_c(j, c0, c1):
            if j < 8:
                return A[:, j * 2048 + c0:j * 2048 + c1]
            if j < 20:
                return B[:, (j - 8) * 2048 + c0:(j - 8) * 2048 + c1]
            if j < 22:
                return gsl[:, (j - 20) * 2048 + c0:(j - 20) * 2048 + c1]
            if j == 22:
                return upsl[:, c0:c1]
            return tscb[:, 4096 + c0:4096 + c1]

        def o_slot(j):
            return o_slot_c(j, 0, 2048)

        def gs3c(j, c0, c1):
            sl = j % 4
            if sl < 2:
                return gsl[:, sl * 2048 + c0:sl * 2048 + c1]
            if sl == 2:
                return upsl[:, c0:c1]
            return ost[:, 4096 + c0:4096 + c1]

        def gs3(j):
            return gs3c(j, 0, 2048)

        def sigb(j, h):
            buf = fstage if j % 2 == 0 else xstage
            return buf[:, h * 1024:(h + 1) * 1024]

        ld_consts = 0
        for o, i in ((ctriu[:, :], c_triuT), (ctri16[:, :], c_tri16T),
                     (cone16[:, :], c_ones16), (csel[:, :], c_sel16),
                     (cesel[:, :], c_esel), (cpos[:, :], c_pos),
                     (cbsel[:, :], c_bsel)):
            ld_consts = dma("sync", sCst, o, i)

        def dequant(src, dst, nblk, scr):
            w = nblk * 32
            src3 = src.rearrange("p (b k) -> p b k", k=32)
            em("vector", lambda v, s3=src3, nb=nblk, scr=scr: v.tensor_reduce(
                scr[:, 0:nb], s3, AX.X, ALU.max, apply_absolute_value=True))
            em("vector", lambda v, nb=nblk, scr=scr: v.tensor_scalar(
                scr[:, 128:128 + nb], scr[:, 0:nb], INV7, 1e-12, ALU.mult, ALU.max))
            em("vector", lambda v, nb=nblk, scr=scr: v.reciprocal(scr[:, 256:256 + nb], scr[:, 128:128 + nb]))
            em("vector", lambda v, s3=src3, nb=nblk, w=w, scr=scr: v.tensor_tensor(
                scr[:, 512:512 + w].rearrange("p (b k) -> p b k", k=32), s3,
                scr[:, 256:256 + nb].unsqueeze(2).broadcast_to([128, nb, 32]), ALU.mult))
            em("vector", lambda v, w=w, scr=scr: v.tensor_scalar(
                scr[:, 512:512 + w], scr[:, 512:512 + w], CM, -CM, ALU.add, ALU.add))
            return em("vector", lambda v, dst=dst, nb=nblk, w=w, scr=scr: v.tensor_tensor(
                dst.rearrange("p (b k) -> p b k", k=32),
                scr[:, 512:512 + w].rearrange("p (b k) -> p b k", k=32),
                scr[:, 128:128 + nb].unsqueeze(2).broadcast_to([128, nb, 32]), ALU.mult))

        ld_ugj = [0] * 3
        for j in range(3):
            for q in range(2):
                ld_ugj[j] = dma("sync", sW,
                                fstage[:, j * 1024 + q * 512:j * 1024 + (q + 1) * 512],
                                wup_sl[j * 128:(j + 1) * 128, q * 512:(q + 1) * 512])

        em("vector", lambda v: v.memset(runm[:, :], 0.0))
        ld_x = [0] * NT
        dv_ln = [0] * NT
        ac_ln = [0] * NT
        sp_xn = [0] * NT
        dv1 = [0] * NT
        dv_up3 = [None] * 3
        dv_g3 = [None] * 3
        ld_g3j = [0] * 3
        sp_ug = 0
        cc_ug = 0
        def lnx(t):
            sl = t % 4
            if sl < 2:
                return xstage[:, sl * 1024:(sl + 1) * 1024]
            return tsc[:, (sl - 2) * 1024 + 2048:(sl - 1) * 1024 + 2048]

        def ln1(t):
            xa = lnx(t)
            a = 4 * (t % 2)
            if t >= 4:
                wait("sync", sDV, dv_ln[t - 4])
                wait("sync", sAC, ac_ln[t - 4])
            for q in range(4):
                ld_x[t] = dma("sync", sX[t % 4], xa[:, q * 256:(q + 1) * 256],
                              x_in[t * 128:(t + 1) * 128, q * 256:(q + 1) * 256])
            wait("vector", sX[t % 4], ld_x[t])
            em("vector", lambda v, xa=xa, a=a: v.reduce_sum(lnst[:, a:a + 1], xa, AX.X))
            dv1[t] = em("vector", lambda v, a=a: v.tensor_scalar(
                lnst[:, a + 1:a + 2], lnst[:, a:a + 1], -1.0 / 1024.0, None, ALU.mult))

        def ln2(t):
            xa = lnx(t)
            xb = xbstage[:, (t % 2) * 1024:(t % 2) * 1024 + 1024]
            a = 4 * (t % 2)
            wait("scalar", sDV, dv1[t])
            ac1 = em("scalar", lambda s, xa=xa, a=a: s.activation(
                kstage[:, 0:1024], xa, AF.Square,
                bias=lnst[:, a + 1:a + 2], scale=1.0, accum_out=lnst[:, a + 3:a + 4]))
            wait("vector", sAC, ac1)
            dv2 = em("vector", lambda v, a=a: v.tensor_scalar(
                lnst[:, a + 3:a + 4], lnst[:, a + 3:a + 4], 1.0 / 1024.0, 1e-5, ALU.mult, ALU.add))
            wait("scalar", sDV, dv2)
            ac2 = em("scalar", lambda s, a=a: s.sqrt(lnst[:, a + 2:a + 3], lnst[:, a + 3:a + 4]))
            wait("vector", sAC, ac2)
            em("vector", lambda v, a=a: v.reciprocal(lnst[:, a + 2:a + 3], lnst[:, a + 2:a + 3]))
            dv3 = em("vector", lambda v, a=a: v.tensor_tensor(
                lnst[:, a + 1:a + 2], lnst[:, a + 1:a + 2], lnst[:, a + 2:a + 3], ALU.mult))
            wait("scalar", sDV, dv3)
            if t >= 2:
                wait("scalar", sXs[t % 2], sp_xn[t - 2])
            ac_ln[t] = em("scalar", lambda s, xa=xa, xb=xb, a=a: s.activation(
                xb, xa, AF.Identity, bias=lnst[:, a + 1:a + 2], scale=lnst[:, a + 2:a + 3]))
            wait("vector", sAC, ac_ln[t])
            em("vector", lambda v, xb=xb, a=a: v.tensor_reduce(
                lnst[:, a + 3:a + 4], xb, AX.X, ALU.max, apply_absolute_value=True))
            dv_ln[t] = em("vector", lambda v, a=a: v.tensor_tensor(
                runm[:, 0:1], runm[:, 0:1], lnst[:, a + 3:a + 4], ALU.max))
            wait("gpsimd", sAC, ac_ln[t])
            for q in range(2):
                sp_xn[t] = dma("gpsimd", sXs[t % 2],
                               xn_d[t * 128:(t + 1) * 128, q * 512:(q + 1) * 512],
                               xb[:, q * 512:(q + 1) * 512])

        for t in range(NT):
            ln1(t)
            if t >= 1:
                ln2(t - 1)
            if t in (1, 2, 3):
                j = t - 1
                wait("vector", sW, ld_ugj[2])
                dv_up3[j] = dequant(fstage[:, j * 1024:(j + 1) * 1024],
                                    wq6(2 * j, 2 * j + 1), 32, tsc[:, 0:2048])
            if t == 3:
                for j in range(3):
                    wait("gpsimd", sDV, dv_up3[j])
                    for q in range(2):
                        ld_g3j[j] = dma("gpsimd", sWg,
                                        fstage[:, j * 1024 + q * 512:j * 1024 + (q + 1) * 512],
                                        wgate_sl[j * 128:(j + 1) * 128, q * 512:(q + 1) * 512])
            if t in (5, 6, 7):
                j = t - 5
                wait("vector", sWg, ld_g3j[2])
                dv_g3[j] = dequant(fstage[:, j * 1024:(j + 1) * 1024],
                                   wq6(2 * j + 1, 2 * j + 2), 32, tsc[:, 0:2048])
            if t == 8:
                wait("gpsimd", sDV, dv_g3[2])
                for j3 in range(3):
                    for q in range(2):
                        sp_ug = dma("gpsimd", sSP,
                                    upgate_in[j3 * 128:(j3 + 1) * 128, q * 1024:(q + 1) * 1024],
                                    wq6(2 * j3 + q, 2 * j3 + q + 1))
                wait("gpsimd", sSP, sp_ug)
                cc_ug = sCC.inc(1)
                em("gpsimd", lambda g: g.collective_compute(
                    "AllGather", ALU.bypass, replica_groups=[list(range(8))],
                    ins=[upgate_in], outs=[upgate_all]).then_inc(sCC.h, 1))
        ln2(NT - 1)

        em("vector", lambda v: v.memset(sc[:, 10:11], CM))

        mr_state = {"msc_free": 0, "gbc_free": 0}

        def max_round(ridx, cols, run_cols, dv_ready):
            n = len(run_cols)
            rc0 = run_cols[0]
            assert run_cols == list(range(rc0, rc0 + n))
            wait("gpsimd", sDV, dv_ready)
            for ci in range(n):
                sp1 = dma("gpsimd", sSP, mrow_d[ridx][ci:ci + 1, :].rearrange("o p -> p o"),
                          runm[:, rc0 + ci:rc0 + ci + 1])
            wait("gpsimd", sSP, sp1)
            sp2 = dma("gpsimd", sSP, mrow_sb[:, 0:n * 128],
                      mrow_d[ridx].rearrange("o p -> (o p)").unsqueeze(0))
            wait("vector", sSP, sp2)
            wait("vector", sSP, mr_state["msc_free"])
            em("vector", lambda v, n=n: v.tensor_reduce(
                msc_sb[:, 0:n], mrow_sb[:, 0:n * 128].rearrange("a (o p) -> a o p", p=128),
                AX.X, ALU.max))
            dvz = None
            for ci in range(len(run_cols), 8):
                dvz = em("vector", lambda v, ci=ci: v.memset(msc_sb[:, ci:ci + 1], 0.0))
            wait("gpsimd", sDV, dvz)
            sp3 = dma("gpsimd", sSP, cc_in[ridx], msc_sb[:, :])
            mr_state["msc_free"] = sp3
            wait("gpsimd", sSP, sp3)
            wait("gpsimd", sCC, sCC.n)
            ccv = sCC.inc(1)
            em("gpsimd", lambda g, ridx=ridx: g.collective_compute(
                "AllReduce", ALU.max, replica_groups=[list(range(8))],
                ins=[cc_in[ridx]], outs=[cc_out[ridx]]).then_inc(sCC.h, 1))
            wait("gpsimd", sCC, ccv)
            if mr_state["gbc_free"]:
                wait("gpsimd", sDV, mr_state["gbc_free"])
            n = len(run_cols)
            sp4 = dma("gpsimd", sSP, gbc[:, 0:n], cc_out[ridx][0:1, 0:n].broadcast_to([128, n]))
            wait("vector", sSP, sp4)
            dvs = None
            for ci, c in enumerate(cols):
                em("vector", lambda v, ci=ci, c=c: v.tensor_scalar(
                    sc[:, 2 * c:2 * c + 1], gbc[:, ci:ci + 1], INVQ, 1e-12, ALU.mult, ALU.max))
                dvs = em("vector", lambda v, c=c: v.reciprocal(sc[:, 2 * c + 1:2 * c + 2], sc[:, 2 * c:2 * c + 1]))
            mr_state["gbc_free"] = dvs
            return ccv, dvs

        cc_m0, dv_s0 = max_round(0, [0], [0], dv_ln[NT - 1])

        wait("sync", sXs[0], sp_xn[NT - 2])
        wait("sync", sXs[1], sp_xn[NT - 1])
        tr_xn1 = [0] * NH
        for k in range(NH):
            tr_xn1[k] = dmaT("sync", sTx, A[:, k * 2048:(k + 1) * 2048], xn_d[:, k * 128:(k + 1) * 128])
        wait("sync", sCC, cc_ug)
        tr_up = [0] * NH
        for k in range(NH):
            tr_up[k] = dmaT("sync", sTu, D[:, k * 3072:(k + 1) * 3072], upgate_all[:, k * 128:(k + 1) * 128])
        tr_g = [0] * NH
        for k in range(NH):
            tr_g[k] = dmaT("sync", sTg, B[:, k * 3072:(k + 1) * 3072],
                           upgate_all[:, 1024 + k * 128: 1024 + (k + 1) * 128])
        dv_xq = [0] * NH
        for k in range(NH):
            if k == 0:
                wait("vector", sTx, tr_xn1[NH - 1])
                wait("vector", sDV, dv_s0)
            em("vector", lambda v, k=k: v.tensor_scalar(
                tsc[:, 0:2048], A[:, k * 2048:(k + 1) * 2048], sc[:, 1:2], CM, ALU.mult, ALU.add))
            dv_xq[k] = em("vector", lambda v, k=k: v.tensor_scalar(
                A[:, k * 2048:(k + 1) * 2048], tsc[:, 0:2048], -CM, sc[:, 0:1], ALU.add, ALU.mult))

        wait("sync", sDV, dv_g3[2])
        ld_wqkv = 0
        for q in range(4):
            ld_wqkv = dma("sync", sW, fstage[:, q * 768:(q + 1) * 768],
                          wqkv_sl[:, q * 768:(q + 1) * 768])
        wait("vector", sW, ld_wqkv)
        wait("vector", sSP, sp_ug)
        dv_wqkv = em("vector", lambda v: v.tensor_copy(wq6(0, 3), fstage[:, 0:3072]))
        wait("gpsimd", sDV, dv_wqkv)
        sp_wqkv = 0
        for q in range(3):
            sp_wqkv = dma("gpsimd", sSP, wqkvq_in[:, q * 1024:(q + 1) * 1024], wq6(q, q + 1))
        wait("gpsimd", sSP, sp_wqkv)
        wait("gpsimd", sCC, sCC.n)
        cc_wqkv = sCC.inc(1)
        em("gpsimd", lambda g: g.collective_compute(
            "AllGather", ALU.bypass, replica_groups=[list(range(8))],
            ins=[wqkvq_in], outs=[wqkvq_all]).then_inc(sCC.h, 1))

        wait("sync", sDV, dv_wqkv)
        ld_wd = 0
        for q in range(4):
            ld_wd = dma("sync", sW, fstage[:, q * 768:(q + 1) * 768],
                        wdown_sl[:, q * 768:(q + 1) * 768])
        wait("vector", sW, ld_wd)
        wait("vector", sSP, sp_wqkv)
        dequant(fstage[:, 0:1536], wq6(0, 1.5), 48, tsc[:, 0:2048])
        dv_wd = dequant(fstage[:, 1536:3072], wq6(1.5, 3), 48, tsc[:, 0:2048])
        wait("gpsimd", sDV, dv_wd)
        sp_wd = 0
        for q in range(3):
            sp_wd = dma("gpsimd", sSP, wdownq_in[:, q * 1024:(q + 1) * 1024], wq6(q, q + 1))
        wait("gpsimd", sSP, sp_wd)
        wait("gpsimd", sCC, sCC.n)
        cc_wd = sCC.inc(1)
        em("gpsimd", lambda g: g.collective_compute(
            "AllGather", ALU.bypass, replica_groups=[list(range(8))],
            ins=[wdownq_in], outs=[wdownq_all]).then_inc(sCC.h, 1))

        upev_ac = {}
        upev_dv = {}
        sp_out_w = {}
        mm_pe = {}
        regions = [psA[:, 0:1024], psA[:, 1024:2048], psB[:, 0:1024], psB[:, 1024:2048]]
        halfseq = []

        def emit_mm_phase(which):
            W = D if which == "up" else B
            dst = up_d if which == "up" else gate_d
            runcol = 1 if which == "up" else 2
            for i in range(NI):
                for half in range(2):
                    key = (which, i, half)
                    halfseq.append(key)
                    g = len(halfseq) - 1
                    slot = regions[g % 4]
                    if g >= 4:
                        prev = halfseq[g - 4]
                        wait("tensor", sAC, upev_ac[prev])
                        wait("tensor", sDV, upev_dv[prev])
                    for k in range(NH):
                        for n2 in range(2):
                            fn = lambda te, W=W, i=i, k=k, half=half, n2=n2, slot=slot: te.matmul(
                                slot[:, n2 * 512:(n2 + 1) * 512],
                                W[:, k * 3072 + i * 128: k * 3072 + (i + 1) * 128],
                                A[:, k * 2048 + half * 1024 + n2 * 512: k * 2048 + half * 1024 + (n2 + 1) * 512],
                                start=(k == 0), stop=(k == NH - 1))
                            if k == NH - 1 and n2 == 1:
                                mm_pe[key] = sPE.inc(1)
                                plan["tensor"].append(lambda te, fn=fn: fn(te).then_inc(sPE.h, 1))
                            else:
                                plan["tensor"].append(fn)
                    wait("scalar", sPE, mm_pe[key])
                    wkey = (which, i - 1)
                    if which == "gate" and i < 1:
                        wkey = ("up", NI - 1)
                    if wkey in sp_out_w:
                        wait("scalar", sUsp[wkey[1] % 2], sp_out_w[wkey])
                    us = upstage[:, half * 1024:half * 1024 + 1024]
                    upev_ac[key] = em("scalar", lambda s, us=us, slot=slot: s.copy(us, slot))
                    wait("vector", sAC, upev_ac[key])
                    em("vector", lambda v, slot=slot: v.tensor_reduce(
                        lnst[:, 3:4], slot, AX.X, ALU.max, apply_absolute_value=True))
                    upev_dv[key] = em("vector", lambda v, rc=runcol: v.tensor_tensor(
                        runm[:, rc:rc + 1], runm[:, rc:rc + 1], lnst[:, 3:4], ALU.max))
                wait("sync", sAC, upev_ac[(which, i, 1)])
                for q in range(2):
                    sp_out_w[(which, i)] = dma(
                        "sync", sUsp[i % 2],
                        dst[i * 128:(i + 1) * 128, q * 1024:(q + 1) * 1024],
                        upstage[:, q * 1024:(q + 1) * 1024])

        wait("tensor", sDV, dv_xq[NH - 1])
        wait("tensor", sTu, tr_up[NH - 1])
        emit_mm_phase("up")
        wait("tensor", sTg, tr_g[NH - 1])
        emit_mm_phase("gate")

        cc_m12, dv_s12a = max_round(1, [1, 2], [1, 2], upev_dv[("gate", NI - 1, 1)])
        dv_s12 = em("vector", lambda v: v.tensor_scalar(
            sc[:, 11:12], sc[:, 4:5], -CM, None, ALU.mult))

        wait("sync", sPE, mm_pe[("gate", NI - 1, 1)])
        wait("sync", sCC, cc_wqkv)
        ld_wq = [0] * NH
        for k in range(NH):
            ld_wq[k] = dma("sync", sQw, B[:, k * 3072:(k + 1) * 3072], wqkvq_all[k * 128:(k + 1) * 128, :])
        tr_xn2 = [0] * NH
        for k in range(NH):
            tr_xn2[k] = dmaT("sync", sTx, A[:, k * 2048:(k + 1) * 2048], xn_d[:, k * 128:(k + 1) * 128])

        g1_dv = [0] * NI
        ld_gt = [0] * NI
        sp_g1 = [0] * NI
        g1_pc = [0] * NI
        dvmv = [0] * NI

        def redG(j):
            wait("vector", sPC, g1_pc[j])
            em("vector", lambda v, j=j: v.tensor_reduce(
                lnst[:, 3:4], gs3(j), AX.X, ALU.max, apply_absolute_value=True))
            g1_dv[j] = em("vector", lambda v: v.tensor_tensor(
                runm[:, 3:4], runm[:, 3:4], lnst[:, 3:4], ALU.max))

        def chainG(j):
            if j >= 4:
                wait("sync", sGl[j % 4], ld_gt[j - 4])
                wait("sync", sGs[j % 4], sp_g1[j - 4])
                wait("sync", sDV, g1_dv[j - 4])
            for q in range(2):
                ld_gt[j] = dma("sync", sGl[j % 4], gs3c(j, q * 1024, (q + 1) * 1024),
                               gate_d[j * 128:(j + 1) * 128, q * 1024:(q + 1) * 1024])
            wait("scalar", sGl[j % 4], ld_gt[j])
            if j == 0:
                wait("scalar", sDV, dv_s12)
            if j >= 2:
                wait("scalar", sDV, dvmv[j - 2])
            ac1 = em("scalar", lambda s, j=j: s.activation(ts2(j), gs3(j), AF.Identity, bias=sc[:, 10:11], scale=sc[:, 5:6]))
            wait("vector", sAC, ac1)
            dvv = em("vector", lambda v, j=j: v.tensor_scalar(gs3(j), ts2(j), -CM, sc[:, 4:5], ALU.add, ALU.mult))
            if j >= 2:
                wait("scalar", sPC, g1_pc[j - 2])
            acs = 0
            for h in range(2):
                sb_ = sigb(j, h)
                em("scalar", lambda s, j=j, h=h, sb_=sb_: s.activation(
                    sb_, tsc[:, (j % 2) * 2048 + h * 1024:(j % 2) * 2048 + (h + 1) * 1024],
                    AF.Sigmoid, bias=sc[:, 11:12], scale=sc[:, 4:5]))
                acs = em("scalar", lambda s, sb_=sb_: s.activation(
                    sb_, sb_, AF.Identity, bias=sc[:, 10:11], scale=65536.0))
            wait("vector", sAC, acs)
            for h in range(2):
                sb_ = sigb(j, h)
                dvmv[j] = em("vector", lambda v, sb_=sb_: v.tensor_scalar(
                    sb_, sb_, -CM, 65535.0, ALU.add, ALU.min))
            wait("gpsimd", sDV, dvmv[j])
            for h in range(2):
                gh = gs3c(j, h * 1024, (h + 1) * 1024)
                g1_pc[j] = empool(lambda g, gh=gh, sb_=sigb(j, h): g.tensor_tensor(gh, sb_, gh, ALU.mult))
            wait("gpsimd", sPC, g1_pc[j])
            for q in range(2):
                sp_g1[j] = dma("gpsimd", sGs[j % 4],
                               gate_d[j * 128:(j + 1) * 128, q * 1024:(q + 1) * 1024],
                               gs3c(j, q * 1024, (q + 1) * 1024))
            if j >= 1:
                redG(j - 1)
            if j == NI - 1:
                redG(j)

        g2_dv = [0] * NI
        ld_g1 = [0] * NI
        ld_ur = [0] * NI
        sp_g2 = [0] * NI
        g2_pc = [0] * NI
        actu_ms = [0] * NI
        actg_ms = [0] * NI
        dvvv = [0] * NI

        def redI(j):
            O = o_slot(j) if j < 20 else gsI(j)
            wait("vector", sPC, g2_pc[j])
            em("vector", lambda v, j=j, O=O: v.tensor_reduce(
                lnst[:, 3:4], O, AX.X, ALU.max, apply_absolute_value=True))
            g2_dv[j] = em("vector", lambda v: v.tensor_tensor(
                runm[:, 4:5], runm[:, 4:5], lnst[:, 3:4], ALU.max))

        def gsI(j, c0=0, c1=2048):
            sl = j % 4
            if sl < 2:
                return gsl[:, sl * 2048 + c0:sl * 2048 + c1]
            if sl == 2:
                return upsl[:, c0:c1]
            return vstage[:, c0:c1]

        kb = kstage[:, :].bitcast(BF16)

        def chainI(j):
            if j == 0:
                wait("sync", sDV, g1_dv[NI - 1])
            if j == 3:
                wait("sync", sVs, sp_v[NT - 1])
                wait("sync", sPE, pe_B_last[0])
            if j < 3:
                wait("sync", sGl[j % 4], ld_gt[20 + j])
                wait("sync", sGs[j % 4], sp_g1[20 + j])
            elif j >= 4:
                wait("sync", sGl[j % 4], ld_g1[j - 4])
                wait("sync", sAC, actg_ms[j - 4])
            for q in range(2):
                ld_g1[j] = dma("sync", sGl[j % 4], gsI(j, q * 1024, (q + 1) * 1024),
                               gate_d[j * 128:(j + 1) * 128, q * 1024:(q + 1) * 1024])
            if j == 0:
                wait("sync", sPE, pe_half[NT - 1][1])
            if j == 20:
                wait("sync", sAC, ac_ev[NT - 1][1])
                wait("sync", sDV, dv_ev[NT - 1][1])
            if j > 20:
                wait("sync", sAC, actu_ms[j - 1])
            ub = o_slot(j) if j < 20 else kb
            for q in range(2):
                ld_ur[j] = dma("sync", sUl[j % 6], o_slot_c(j, q * 1024, (q + 1) * 1024)
                               if j < 20 else kb[:, q * 1024:(q + 1) * 1024],
                               up_d[j * 128:(j + 1) * 128, q * 1024:(q + 1) * 1024])
            wait("scalar", sGl[j % 4], ld_g1[j])
            if j == 0:
                wait("scalar", sDV, mr_state["dv_s3"])
                wait("scalar", sDV, dv_coef)
            if j == 1:
                wait("scalar", sGs[3], sp_g1[23])
            if j >= 2:
                wait("scalar", sDV, dvvv[j - 2])
            actg_ms[j] = em("scalar", lambda s, j=j: s.activation(ts2(j), gsI(j), AF.Identity, bias=sc[:, 10:11], scale=sc[:, 7:8]))
            wait("vector", sAC, actg_ms[j])
            g2b = ost[:, 0:2048] if j % 2 == 0 else ost[:, 2048:4096]
            if j >= 2:
                wait("vector", sPC, g2_pc[j - 2])
            dvvv[j] = em("vector", lambda v, j=j, g2b=g2b: v.tensor_scalar(g2b, ts2(j), -CM, sc[:, 6:7], ALU.add, ALU.mult))
            wait("scalar", sUl[j % 6], ld_ur[j])
            if j >= 2:
                wait("scalar", sPC, g2_pc[j - 2])
            gb0 = (j % 2) * 2048
            if j % 2 == 0:
                ub2h = (xstage[:, 0:2048],)
            else:
                ub2h = (fstage[:, 2048:3072], t2f[:, 0:1024])
            for hi, uh in enumerate(ub2h):
                n = len(ub2h)
                src = (ub if n == 1 else
                       (o_slot_c(j, hi * 1024, (hi + 1) * 1024) if j < 20
                        else kb[:, hi * 1024:(hi + 1) * 1024]))
                actu_ms[j] = em("scalar", lambda s, uh=uh, src=src: s.activation(
                    src if False else uh, src, AF.Identity, bias=sc[:, 10:11], scale=sc[:, 3:4]))
            wait("vector", sAC, actu_ms[j])
            dvu = 0
            for uh in ub2h:
                dvu = em("vector", lambda v, uh=uh: v.tensor_scalar(uh, uh, -CM, None, ALU.add))
            wait("gpsimd", sDV, dvu)
            if len(ub2h) == 1:
                O = o_slot(j) if j < 20 else gsI(j)
                g2_pc[j] = empool(lambda g, O=O, g2b=g2b, uh=ub2h[0]: g.tensor_tensor(O, uh, g2b, ALU.mult))
            else:
                for hi, uh in enumerate(ub2h):
                    Oh = o_slot_c(j, hi * 1024, (hi + 1) * 1024) if j < 20 else gsI(j, hi * 1024, (hi + 1) * 1024)
                    gbh = ost[:, gb0 + hi * 1024:gb0 + (hi + 1) * 1024]
                    g2_pc[j] = empool(lambda g, Oh=Oh, gbh=gbh, uh=uh: g.tensor_tensor(Oh, uh, gbh, ALU.mult))
            if j >= 20:
                wait("gpsimd", sPC, g2_pc[j])
                for q in range(2):
                    sp_g2[j] = dma("gpsimd", sGs[j % 4],
                                   up_d[j * 128:(j + 1) * 128, q * 1024:(q + 1) * 1024],
                                   gsI(j, q * 1024, (q + 1) * 1024))
            if j >= 1:
                redI(j - 1)
            if j == NI - 1:
                redI(j)

        o_dv = [0] * NI
        ld_g2 = [0] * NI
        toenc_ms = [0] * NI

        def chainJ(j):
            if j >= 20:
                if j == 20:
                    for s4 in range(4):
                        wait("sync", sGs[s4], sp_g2[20 + s4])
                    wait("sync", sDV, g2_dv[NI - 1])
                    wait("sync", sPC, g2_pc[NI - 1])
                    wait("sync", sAC, actu_ms[NI - 1])
                if j == 23:
                    wait("sync", sDV, o_dv[22])
                for q in range(2):
                    ld_g2[j] = dma("sync", sJl[j - 20], o_slot_c(j, q * 1024, (q + 1) * 1024),
                                   up_d[j * 128:(j + 1) * 128, q * 1024:(q + 1) * 1024])
            if j == 0:
                wait("scalar", sDV, mr_state["dv_s4"])
            if j >= 2:
                wait("scalar", sDV, o_dv[j - 2])
            if j >= 20:
                wait("scalar", sJl[j - 20], ld_g2[j])
            tb = ts2(j + 1)
            toenc_ms[j] = em("scalar", lambda s, j=j, tb=tb: s.activation(
                tb, o_slot(j), AF.Identity, bias=sc[:, 10:11], scale=sc[:, 9:10]))
            wait("vector", sAC, toenc_ms[j])
            o_dv[j] = em("vector", lambda v, j=j, tb=tb: v.tensor_scalar(
                o_slot(j), tb, -CM, sc[:, 8:9], ALU.add, ALU.mult))

        pe_half = [[0, 0] for _ in range(NT)]
        ac_ev = [[0, 0] for _ in range(NT)]
        dv_ev = [[0, 0] for _ in range(NT)]
        sp_v = [0] * NT
        pe_B_last = [0]

        def qkv_pe(t):
            for h in range(2):
                ps = psA if h == 0 else psB
                if t >= 1:
                    wait("tensor", sAC, ac_ev[t - 1][h])
                    wait("tensor", sDV, dv_ev[t - 1][h])
                for k in range(NH):
                    for n3 in range(3):
                        fn = lambda te, ps=ps, k=k, n3=n3, t=t, h=h: te.matmul(
                            ps[:, n3 * 512:(n3 + 1) * 512],
                            A[:, k * 2048 + t * 128: k * 2048 + (t + 1) * 128],
                            B[:, k * 3072 + h * 1536 + n3 * 512: k * 3072 + h * 1536 + (n3 + 1) * 512],
                            start=(k == 0), stop=(k == NH - 1))
                        if k == NH - 1 and n3 == 2:
                            pe_half[t][h] = sPE.inc(1)
                            plan["tensor"].append(lambda te, fn=fn: fn(te).then_inc(sPE.h, 1))
                        else:
                            plan["tensor"].append(fn)

        def qkv_B(t):
            vt = vstage[:, (t % 2) * 1024:(t % 2) * 1024 + 1024]
            wait("tensor", sAC, ac_ev[t][1])
            plan["tensor"].append(lambda te, vt=vt, t=t: te.matmul(
                psA[0:16, 1536:2048], cbsel[:, t * 16:(t + 1) * 16], vt[:, 0:512],
                start=(t == 0), stop=(t == NT - 1)))
            fnB = lambda te, vt=vt, t=t: te.matmul(
                psB[0:16, 1536:2048], cbsel[:, t * 16:(t + 1) * 16], vt[:, 512:1024],
                start=(t == 0), stop=(t == NT - 1))
            if t == NT - 1:
                pe_B_last[0] = sPE.inc(1)
                plan["tensor"].append(lambda te, fnB=fnB: fnB(te).then_inc(sPE.h, 1))
            else:
                plan["tensor"].append(fnB)

        def qkv_evac(t):
            for h in range(2):
                ps = psA if h == 0 else psB
                wait("scalar", sPE, pe_half[t][h])
                if h == 1:
                    wait("scalar", sDV, dv_ev[t][0])
                elif t >= 1:
                    wait("scalar", sDV, dv_ev[t - 1][1])
                    wait("scalar", sVs, sp_v[t - 1])
                vsl = vstage[:, (t % 2) * 1024 + h * 512:(t % 2) * 1024 + h * 512 + 512]
                em("scalar", lambda s, ps=ps, vsl=vsl: s.copy(
                    vsl.rearrange("p (hh dd) -> p hh dd", dd=64),
                    ps[:, 0:1536].rearrange("p (hh c) -> p hh c", c=192)[:, :, 128:192]))
                ac_ev[t][h] = em("scalar", lambda s, ps=ps: s.copy(
                    kstage[:, 0:512].rearrange("p (hh dd) -> p hh dd", dd=64),
                    ps[:, 0:1536].rearrange("p (hh c) -> p hh c", c=192)[:, :, 64:128]))
                wait("vector", sAC, ac_ev[t][h])
                em("vector", lambda v, ps=ps: v.tensor_tensor(
                    kstage[:, 0:512].rearrange("p (hh dd) -> p hh dd", dd=64),
                    ps[:, 0:1536].rearrange("p (hh c) -> p hh c", c=192)[:, :, 0:64],
                    kstage[:, 0:512].rearrange("p (hh dd) -> p hh dd", dd=64), ALU.mult))
                dv_ev[t][h] = em("vector", lambda v, t=t, h=h: v.tensor_reduce(
                    sdg[:, t * 16 + h * 8: t * 16 + h * 8 + 8],
                    kstage[:, 0:512].rearrange("p (hh dd) -> p hh dd", dd=64),
                    AX.X, ALU.add))
            vt = vstage[:, (t % 2) * 1024:(t % 2) * 1024 + 1024]
            wait("gpsimd", sAC, ac_ev[t][1])
            sp_v[t] = dma("gpsimd", sVs, v_d[t * 128:(t + 1) * 128, :], vt)

        wait("tensor", sCst, ld_consts)
        wait("tensor", sQw, ld_wq[NH - 1])
        wait("tensor", sTx, tr_xn2[NH - 1])
        for i in (NI - 2, NI - 1):
            for h in (0, 1):
                wait("tensor", sAC, upev_ac[("gate", i, h)])
                wait("tensor", sDV, upev_dv[("gate", i, h)])
        wait("sync", sUsp[0], sp_out_w[("gate", NI - 2)])
        wait("sync", sUsp[1], sp_out_w[("gate", NI - 1)])
        wait("sync", sSP, sp_wd)

        chain_left = list(range(NI))
        for t in range(NT):
            qkv_pe(t)
            if t >= 1:
                qkv_B(t - 1)
            for _ in range(2 if t % 2 == 0 else 1):
                if chain_left:
                    chainG(chain_left.pop(0))
            qkv_evac(t)
        qkv_B(NT - 1)

        cc_m3, dv_s3a = max_round(2, [3], [3], g1_dv[NI - 1])
        em("vector", lambda v: v.tensor_scalar(
            sc[:, 6:7], gbc[:, 0:1], float(np.float32(P16) * np.float32(INVQ)), 1e-12, ALU.mult, ALU.max))
        em("vector", lambda v: v.reciprocal(sc[:, 7:8], sc[:, 6:7]))
        dv_s3 = em("vector", lambda v: v.tensor_scalar(sc[:, 7:8], sc[:, 7:8], P16, None, ALU.mult))
        mr_state["gbc_free"] = dv_s3
        mr_state["dv_s3"] = dv_s3

        wait("vector", sPE, pe_B_last[0])
        em("vector", lambda v: v.tensor_copy(B_sb[:, 0:512], psA[0:16, 1536:2048]))
        dv_Bev = em("vector", lambda v: v.tensor_copy(B_sb[:, 512:1024], psB[0:16, 1536:2048]))
        wait("tensor", sDV, dv_Bev)
        plan["tensor"].append(lambda te: te.matmul(psA[0:1, 0:512], cone16[:, :], B_sb[:, 0:512], start=True, stop=True))
        pe_T = sPE.inc(1)
        plan["tensor"].append(lambda te: te.matmul(psA[0:1, 512:1024], cone16[:, :], B_sb[:, 512:1024], start=True, stop=True).then_inc(sPE.h, 1))
        wait("vector", sPE, pe_T)
        dv_T = em("vector", lambda v: v.tensor_copy(T_sb, psA[0:1, 0:1024]))
        wait("gpsimd", sDV, dv_T)
        sp_T = dma("gpsimd", sSP, vsum_in, T_sb)
        wait("gpsimd", sSP, sp_T)
        wait("gpsimd", sCC, sCC.n)
        cc_carry = sCC.inc(1)
        em("gpsimd", lambda g: g.collective_compute(
            "AllGather", ALU.bypass, replica_groups=[list(range(8))],
            ins=[vsum_in], outs=[vsum_all]).then_inc(sCC.h, 1))

        wait("vector", sCst, ld_consts)
        wait("vector", sPC, g1_pc[NI - 1])
        em("vector", lambda v: v.tensor_scalar(xstage[:, 0:256], sdg[:, :], 0.125, 0.0, ALU.mult, ALU.max))
        dv_m = em("vector", lambda v: v.scalar_tensor_tensor(
            xstage[:, 256:512], sdg[:, :], 0.125, xstage[:, 0:256], ALU.mult, ALU.subtract))
        wait("scalar", sDV, dv_m)
        em("scalar", lambda s: s.activation(xstage[:, 512:768], xstage[:, 0:256], AF.Exp, scale=-1.0))
        ac_ab = em("scalar", lambda s: s.activation(xstage[:, 768:1024], xstage[:, 256:512], AF.Exp))
        wait("vector", sAC, ac_ab)
        em("vector", lambda v: v.tensor_tensor(
            xstage[:, 1024:1280].rearrange("p (tt hh) -> p tt hh", hh=16),
            xstage[:, 512:768].rearrange("p (tt hh) -> p tt hh", hh=16),
            cpos[:, :].unsqueeze(2).broadcast_to([128, 16, 16]), ALU.mult))
        em("vector", lambda v: v.tensor_tensor(xstage[:, 1024:1280], xstage[:, 1024:1280], xstage[:, 768:1024], ALU.add))
        em("vector", lambda v: v.reciprocal(xstage[:, 1280:1536], xstage[:, 1024:1280]))
        em("vector", lambda v: v.tensor_tensor(coefC[:, :], xstage[:, 512:768], xstage[:, 1280:1536], ALU.mult))
        em("vector", lambda v: v.tensor_tensor(xstage[:, 1536:1792], xstage[:, 768:1024], xstage[:, 512:768], ALU.subtract))
        dv_coef = em("vector", lambda v: v.tensor_tensor(coefV[:, :], xstage[:, 1536:1792], xstage[:, 1280:1536], ALU.mult))

        for j in range(NI):
            chainI(j)

        wait("gpsimd", sCC, cc_carry)
        sp_vg = dma("gpsimd", sOb[0], vg_sb[:, :], vsum_all)
        wait("tensor", sCst, ld_consts)
        wait("tensor", sOb[0], sp_vg)
        pe_P = 0
        for n2 in range(2):
            plan["tensor"].append(lambda te, n2=n2: te.matmul(
                psB[0:16, n2 * 512:(n2 + 1) * 512], ctri16[:, :],
                B_sb[:, n2 * 512:(n2 + 1) * 512], start=True, stop=False))
            fn = lambda te, n2=n2: te.matmul(
                psB[0:16, n2 * 512:(n2 + 1) * 512], csel[:, :],
                vg_sb[:, n2 * 512:(n2 + 1) * 512], start=False, stop=True)
            if n2 == 1:
                pe_P = sPE.inc(1)
                plan["tensor"].append(lambda te, fn=fn: fn(te).then_inc(sPE.h, 1))
            else:
                plan["tensor"].append(fn)
        wait("vector", sPE, pe_P)
        dv_P = em("vector", lambda v: v.tensor_copy(P_sb[:, :], psB[0:16, 0:1024]))

        cc_m4, dv_s4a = max_round(3, [4], [4], g2_dv[NI - 1])
        em("vector", lambda v: v.tensor_tensor(sc[:, 8:9], gbc[:, 0:1], sc[:, 2:3], ALU.mult))
        em("vector", lambda v: v.tensor_scalar(sc[:, 8:9], sc[:, 8:9], INVQ, 1e-12, ALU.mult, ALU.max))
        em("vector", lambda v: v.reciprocal(sc[:, 9:10], sc[:, 8:9]))
        dv_s4 = em("vector", lambda v: v.tensor_tensor(sc[:, 9:10], sc[:, 9:10], sc[:, 2:3], ALU.mult))
        mr_state["gbc_free"] = dv_s4
        mr_state["dv_s4"] = dv_s4

        for j in range(NI):
            chainJ(j)

        wait("sync", sCC, cc_wd)
        wait("sync", sPE, mm_pe[("up", NI - 1, 1)])
        tr_d = [0] * NI
        for k in range(NI):
            tr_d[k] = dmaT("sync", sTd, D[:, k * 1024:(k + 1) * 1024], wdownq_all[:, k * 128:(k + 1) * 128])

        dn_pe = [0] * NT
        dn_dv = [0] * NT
        attn_pe = [0] * NT
        attn_dv = [0] * NT
        ld_x2 = [0] * NT
        ld_at_v = [0] * NT
        st_out = [0] * NT
        wait("sync", sVs, sp_v[NT - 1])
        wait("sync", sPE, pe_B_last[0])
        wait("sync", sDV, dv_coef)
        wait("tensor", sTd, tr_d[NI - 1])
        wait("tensor", sDV, o_dv[NI - 1])

        for t in range(NT):
            if t >= 2:
                wait("sync", sDV, attn_dv[t - 2])
            for q in range(2):
                ld_at_v[t] = dma("sync", sV[t % 2],
                                 vstage[:, (t % 2) * 1024 + q * 512:(t % 2) * 1024 + (q + 1) * 512],
                                 v_d[t * 128:(t + 1) * 128, q * 512:(q + 1) * 512])
            if t >= 2:
                wait("sync", sDV, dn_dv[t - 2])
            for q in range(4):
                ld_x2[t] = dma("sync", sX[t % 2],
                               xstage[:, (t % 2) * 1024 + q * 256:(t % 2) * 1024 + (q + 1) * 256],
                               x_in[t * 128:(t + 1) * 128, q * 256:(q + 1) * 256])

            pc = psB[:, (t % 2) * 1024:(t % 2) * 1024 + 1024]
            if t >= 2:
                wait("tensor", sDV, attn_dv[t - 2])
            if t == 0:
                wait("tensor", sDV, dv_P)
            wait("tensor", sV[t % 2], ld_at_v[t])
            for n2 in range(2):
                plan["tensor"].append(lambda te, t=t, n2=n2, pc=pc: te.matmul(
                    pc[:, n2 * 512:(n2 + 1) * 512], ctriu[:, :],
                    vstage[:, (t % 2) * 1024 + n2 * 512:(t % 2) * 1024 + (n2 + 1) * 512],
                    start=True, stop=False))
                fn = lambda te, t=t, n2=n2, pc=pc: te.matmul(
                    pc[:, n2 * 512:(n2 + 1) * 512], cesel[:, t * 128:(t + 1) * 128],
                    P_sb[:, n2 * 512:(n2 + 1) * 512], start=False, stop=True)
                if n2 == 1:
                    attn_pe[t] = sPE.inc(1)
                    plan["tensor"].append(lambda te, fn=fn: fn(te).then_inc(sPE.h, 1))
                else:
                    plan["tensor"].append(fn)

            slot = psA[:, (t % 2) * 1024:(t % 2) * 1024 + 1024]
            if t >= 2:
                wait("tensor", sDV, dn_dv[t - 2])
            elif t == 0:
                wait("tensor", sDV, dv_T)
            for k in range(NI):
                for n2 in range(2):
                    fn = lambda te, t=t, k=k, n2=n2, slot=slot: te.matmul(
                        slot[:, n2 * 512:(n2 + 1) * 512],
                        o_slot_c(k, t * 128, (t + 1) * 128),
                        D[:, k * 1024 + n2 * 512: k * 1024 + (n2 + 1) * 512],
                        start=(k == 0), stop=(k == NI - 1))
                    if k == NI - 1 and n2 == 1:
                        dn_pe[t] = sPE.inc(1)
                        plan["tensor"].append(lambda te, fn=fn: fn(te).then_inc(sPE.h, 1))
                    else:
                        plan["tensor"].append(fn)

            wait("vector", sPE, attn_pe[t])
            if t == 0:
                wait("vector", sDV, dv_coef)
            ab = xbstage[:, (t % 2) * 1024:(t % 2) * 1024 + 1024]
            em("vector", lambda v, t=t, pc=pc: v.tensor_tensor(
                kstage[:, 0:1024].rearrange("p (hh dd) -> p hh dd", dd=64),
                pc[:, :].rearrange("p (hh dd) -> p hh dd", dd=64),
                coefC[:, t * 16:(t + 1) * 16].unsqueeze(2).broadcast_to([128, 16, 64]), ALU.mult))
            em("vector", lambda v, t=t: v.tensor_tensor(
                t2f[:, 0:1024].rearrange("p (hh dd) -> p hh dd", dd=64),
                vstage[:, (t % 2) * 1024:(t % 2) * 1024 + 1024].rearrange("p (hh dd) -> p hh dd", dd=64),
                coefV[:, t * 16:(t + 1) * 16].unsqueeze(2).broadcast_to([128, 16, 64]), ALU.mult))
            attn_dv[t] = em("vector", lambda v, ab=ab: v.tensor_tensor(
                ab, kstage[:, 0:1024], t2f[:, 0:1024], ALU.add))

            wait("vector", sPE, dn_pe[t])
            wait("vector", sX[t % 2], ld_x2[t])
            if t >= 2:
                wait("vector", sO[t % 2], st_out[t - 2])
            osl = tsc[:, (t % 2) * 1024:(t % 2) * 1024 + 1024]
            em("vector", lambda v, t=t, slot=slot, osl=osl: v.tensor_tensor(
                osl, slot, xstage[:, (t % 2) * 1024:(t % 2) * 1024 + 1024], ALU.add))
            dn_dv[t] = em("vector", lambda v, t=t, osl=osl, ab=ab: v.tensor_tensor(
                osl, osl, ab, ALU.add))
            wait("gpsimd", sDV, dn_dv[t])
            for q in range(2):
                st_out[t] = dma("gpsimd", sO[t % 2],
                                out_d[t * 128:(t + 1) * 128, q * 512:(q + 1) * 512],
                                osl[:, q * 512:(q + 1) * 512])

        wait("gpsimd", sO[0], st_out[NT - 2])
        wait("gpsimd", sO[1], st_out[NT - 1])

        block = es.enter_context(nc.Block())

        @block.sync
        def _(sync):
            for fn in plan["sync"]:
                fn(sync)

        @block.gpsimd
        def _(gpsimd):
            for fn in plan["gpsimd"]:
                fn(gpsimd)

        @block.vector
        def _(vector):
            for fn in plan["vector"]:
                fn(vector)

        @block.scalar
        def _(scalar):
            for fn in plan["scalar"]:
                fn(scalar)

        @block.tensor
        def _(tensor):
            for fn in plan["tensor"]:
                fn(tensor)

    return nc


def _consts(core_id):
    triuT = np.triu(np.ones((128, 128), np.float32)).astype(bfloat16)
    tri16T = np.triu(np.ones((16, 16), np.float32), 1).astype(bfloat16)
    ones16 = np.ones((16, 1), np.float32).astype(bfloat16)
    sel16 = np.zeros((8, 16), np.float32)
    if core_id % 2 == 1:
        sel16[core_id - 1, :] = 1.0
    sel16 = sel16.astype(bfloat16)
    esel = np.zeros((16, 2048), np.float32)
    for t in range(16):
        esel[t, t * 128:(t + 1) * 128] = 1.0
    esel = esel.astype(bfloat16)
    bsel = np.zeros((128, 256), np.float32)
    for t in range(16):
        bsel[:, t * 16 + t] = 1.0
    bsel = bsel.astype(bfloat16)
    off = (core_id % 2) * T
    pos = (np.arange(T, dtype=np.float32).reshape(16, 128).T + np.float32(off)).astype(np.float32)
    return dict(c_triuT=triuT, c_tri16T=tri16T, c_ones16=ones16,
                c_sel16=sel16, c_esel=esel, c_pos=pos, c_bsel=bsel)


def kernel(L0_attn_x, L0_attn_wqkv, L0_ffn_w_gate, L0_ffn_w_up, L0_ffn_w_down):
    x = np.ascontiguousarray(np.asarray(L0_attn_x, np.float32).reshape(16384, 1024))
    wqkv = np.ascontiguousarray(np.asarray(L0_attn_wqkv, np.float32))
    wg = np.ascontiguousarray(np.asarray(L0_ffn_w_gate, np.float32))
    wu = np.ascontiguousarray(np.asarray(L0_ffn_w_up, np.float32))
    wd = np.ascontiguousarray(np.asarray(L0_ffn_w_down, np.float32))

    nc = build()
    in_maps = []
    for c in range(8):
        m = dict(
            x=x[c * T:(c + 1) * T],
            wqkv_sl=wqkv[c * 128:(c + 1) * 128],
            wup_sl=wu[c * 384:(c + 1) * 384],
            wgate_sl=wg[c * 384:(c + 1) * 384],
            wdown_sl=wd[c * 128:(c + 1) * 128],
        )
        m.update(_consts(c))
        in_maps.append(m)
    res = run_bass_kernel_spmd(nc, in_maps, core_ids=list(range(8)))
    out = np.concatenate([np.asarray(r["out"]) for r in res.results], axis=0)
    return out.reshape(4, 4096, 1024)


# version 27
# speedup vs baseline: 1.0558x; 1.0124x over previous
import numpy as np
import ml_dtypes
from contextlib import ExitStack

import concourse.bass as bass
import concourse.mybir as mybir
from concourse.bass_utils import run_bass_kernel_spmd

F32 = mybir.dt.float32
BF16 = mybir.dt.bfloat16
AF = mybir.ActivationFunctionType
ALU = mybir.AluOpType
AX = mybir.AxisListType
bfloat16 = ml_dtypes.bfloat16

CM = 12582912.0
INV7 = float(np.float32(1.0 / 7.0))
INVQ = float(np.float32(1.0 / 32767.0))
P16 = float(np.float32(2.0 ** -16))
T = 2048
NT = 16
NH = 8
NI = 24


class SemC:
    def __init__(self, h):
        self.h = h
        self.n = 0

    def inc(self, k):
        self.n += k
        return self.n


def build():
    nc = bass.Bass("TRN2", target_bir_lowering=False)

    x_in = nc.dram_tensor("x", [T, 1024], F32, kind="ExternalInput").ap()
    wqkv_sl = nc.dram_tensor("wqkv_sl", [128, 3072], F32, kind="ExternalInput").ap()
    wup_sl = nc.dram_tensor("wup_sl", [384, 1024], F32, kind="ExternalInput").ap()
    wgate_sl = nc.dram_tensor("wgate_sl", [384, 1024], F32, kind="ExternalInput").ap()
    wdown_sl = nc.dram_tensor("wdown_sl", [128, 3072], F32, kind="ExternalInput").ap()
    c_triuT = nc.dram_tensor("c_triuT", [128, 128], BF16, kind="ExternalInput").ap()
    c_tri16T = nc.dram_tensor("c_tri16T", [16, 16], BF16, kind="ExternalInput").ap()
    c_ones16 = nc.dram_tensor("c_ones16", [16, 1], BF16, kind="ExternalInput").ap()
    c_sel16 = nc.dram_tensor("c_sel16", [8, 16], BF16, kind="ExternalInput").ap()
    c_esel = nc.dram_tensor("c_esel", [16, 2048], BF16, kind="ExternalInput").ap()
    c_pos = nc.dram_tensor("c_pos", [128, 16], F32, kind="ExternalInput").ap()
    c_bsel = nc.dram_tensor("c_bsel", [128, 256], BF16, kind="ExternalInput").ap()
    out_d = nc.dram_tensor("out", [T, 1024], F32, kind="ExternalOutput").ap()

    dram = lambda n, s, d: nc.dram_tensor(n, s, d).ap()
    xn_d = dram("xn_d", [T, 1024], BF16)
    wqkvq_in = dram("wqkvq_in", [128, 3072], BF16)
    wqkvq_all = dram("wqkvq_all", [1024, 3072], BF16)
    upgate_in = dram("upgate_in", [384, 2048], BF16)
    upgate_all = dram("upgate_all", [3072, 2048], BF16)
    wdownq_in = dram("wdownq_in", [128, 3072], BF16)
    wdownq_all = dram("wdownq_all", [1024, 3072], BF16)
    v_d = dram("v_d", [T, 1024], BF16)
    up_d = dram("up_d", [3072, T], BF16)
    gate_d = dram("gate_d", [3072, T], BF16)
    vsum_in = dram("vsum_in", [1, 1024], BF16)
    vsum_all = dram("vsum_all", [8, 1024], BF16)
    mrow_d = [dram(f"mrow{r}_d", [2 if r == 1 else 1, 128], F32) for r in range(4)]
    cc_in = [dram(f"cc{r}_in", [1, 8], F32) for r in range(4)]
    cc_out = [dram(f"cc{r}_out", [1, 8], F32) for r in range(4)]

    es = ExitStack()
    with es:
        sb = lambda n, s, d: es.enter_context(nc.sbuf_tensor(n, s, d))
        pst = lambda n, s, d: es.enter_context(nc.psum_tensor(n, s, d))
        mksem = lambda n: SemC(es.enter_context(nc.semaphore(n)))

        A = sb("A", [128, NH * T], BF16)
        B = sb("B", [128, NH * 3072], BF16)
        D = sb("Dw", [128, NH * 3072], BF16)
        gsl = sb("gsl", [128, 2 * 2048], BF16)
        upsl = sb("upsl", [128, 2048], BF16)
        ost = sb("ost", [128, 2 * 3072], BF16)
        fstage = ost[:, :].bitcast(F32)

        def wq6(a, b):
            a = int(a * 1024); b = int(b * 1024)
            if b <= 4096:
                assert a < b <= 4096
                return gsl[:, a:b]
            assert a >= 4096
            return upsl[:, a - 4096:b - 4096]
        xstage = sb("xstage", [128, 2048], F32)
        xbstage = sb("xbstage", [128, 2048], BF16)
        vstage = sb("vstage", [128, 2048], BF16)
        kstage = sb("kstage", [128, 1024], F32)
        tsc = sb("tsc", [128, 2 * 2048], F32)
        upstage = sb("upstage", [128, 2048], BF16)
        coefC = sb("coefC", [128, 256], F32)
        coefV = sb("coefV", [128, 256], F32)
        sdg = sb("sdg", [128, 256], F32)
        B_sb = sb("B_sb", [16, 1024], BF16)
        P_sb = sb("P_sb", [16, 1024], BF16)
        T_sb = P_sb[0:1, :]
        vg_sb = sb("vg_sb", [8, 1024], BF16)
        ctriu = sb("ctriu", [128, 128], BF16)
        ctri16 = sb("ctri16", [16, 16], BF16)
        cone16 = sb("cone16", [16, 1], BF16)
        csel = sb("csel", [8, 16], BF16)
        cesel = sb("cesel", [16, 2048], BF16)
        cpos = sb("cpos", [128, 16], F32)
        cbsel = sb("cbsel", [128, 256], BF16)
        runm = sb("runm", [128, 8], F32)
        mrow_sb = sb("mrow_sb", [1, 256], F32)
        msc_sb = sb("msc_sb", [1, 8], F32)
        gbc = sb("gbc", [128, 2], F32)
        sc = sb("sc", [128, 16], F32)
        lnst = sb("lnst", [128, 8], F32)

        psA = pst("psA", [128, 2048], F32)
        psB = pst("psB", [128, 2048], F32)

        sSP = mksem("sSP")
        sCC = mksem("sCC")
        sPE = mksem("sPE")
        sDV = mksem("sDV")
        sAC = mksem("sAC")
        sPC = mksem("sPC")
        sCst = mksem("sCst")
        sW = mksem("sW")
        sWg = mksem("sWg")
        sX = [mksem(f"sX{i}") for i in range(4)]
        sXs = [mksem("sXs0"), mksem("sXs1")]
        sTx = mksem("sTx"); sTu = mksem("sTu"); sTg = mksem("sTg"); sTd = mksem("sTd")
        sQw = mksem("sQw")
        sVs = mksem("sVs")
        sV = [mksem("sV0"), mksem("sV1")]
        sUsp = [mksem("sUsp0"), mksem("sUsp1")]
        sGl = [mksem(f"sGl{i}") for i in range(4)]
        sJl = [mksem(f"sJl{i}") for i in range(4)]
        sGs = [mksem(f"sGs{i}") for i in range(4)]
        sUl = [mksem(f"sUl{i}") for i in range(6)]
        sOb = [mksem("sOb0"), mksem("sOb1")]
        sO = [mksem("sO0"), mksem("sO1")]

        t2f = upstage[:, :].bitcast(F32)

        plan = {k: [] for k in ("sync", "gpsimd", "vector", "scalar", "tensor")}
        _chain = {"vector": sDV, "scalar": sAC}

        def em(eng, fn):
            sem = _chain.get(eng)
            if sem is None:
                plan[eng].append(fn)
                return None
            v0 = sem.n
            if v0 > 0:
                plan[eng].append(lambda e, s=sem.h, vv=v0: e.wait_ge(s, vv))
            v = sem.inc(1)
            plan[eng].append(lambda e, fn=fn, s=sem.h: fn(e).then_inc(s, 1))
            return v

        def empool(fn):
            v = sPC.inc(1)
            plan["gpsimd"].append(lambda e, fn=fn, s=sPC.h: fn(e).then_inc(s, 1))
            return v

        def wait(eng, sem, val):
            if val > 0:
                plan[eng].append(lambda e, s=sem.h, v=val: e.wait_ge(s, v))

        def dma(eng, sem, outap, inap):
            v = sem.inc(16)
            plan[eng].append(lambda e, o=outap, i=inap, s=sem.h: e.dma_start(out=o, in_=i).then_inc(s, 16))
            return v

        def dmaT(eng, sem, outap, inap):
            v = sem.inc(16)
            plan[eng].append(lambda e, o=outap, i=inap, s=sem.h: e.dma_start_transpose(o, i).then_inc(s, 16))
            return v

        def gs(j):
            return gsl[:, (j % 2) * 2048:(j % 2) * 2048 + 2048]

        def ts2(j):
            return tsc[:, (j % 2) * 2048:(j % 2) * 2048 + 2048]

        tscb = tsc[:, :].bitcast(BF16)

        def o_slot_c(j, c0, c1):
            if j < 8:
                return A[:, j * 2048 + c0:j * 2048 + c1]
            if j < 20:
                return B[:, (j - 8) * 2048 + c0:(j - 8) * 2048 + c1]
            if j < 22:
                return gsl[:, (j - 20) * 2048 + c0:(j - 20) * 2048 + c1]
            if j == 22:
                return upsl[:, c0:c1]
            return tscb[:, 4096 + c0:4096 + c1]

        def o_slot(j):
            return o_slot_c(j, 0, 2048)

        def gs3c(j, c0, c1):
            sl = j % 4
            if sl < 2:
                return gsl[:, sl * 2048 + c0:sl * 2048 + c1]
            if sl == 2:
                return upsl[:, c0:c1]
            return ost[:, 4096 + c0:4096 + c1]

        def gs3(j):
            return gs3c(j, 0, 2048)

        def sigb(j, h):
            buf = fstage if j % 2 == 0 else xstage
            return buf[:, h * 1024:(h + 1) * 1024]

        ld_consts = 0
        for o, i in ((ctriu[:, :], c_triuT), (ctri16[:, :], c_tri16T),
                     (cone16[:, :], c_ones16), (csel[:, :], c_sel16),
                     (cesel[:, :], c_esel), (cpos[:, :], c_pos),
                     (cbsel[:, :], c_bsel)):
            ld_consts = dma("sync", sCst, o, i)

        def dequant(src, dst, nblk, scr):
            w = nblk * 32
            src3 = src.rearrange("p (b k) -> p b k", k=32)
            em("vector", lambda v, s3=src3, nb=nblk, scr=scr: v.tensor_reduce(
                scr[:, 0:nb], s3, AX.X, ALU.max, apply_absolute_value=True))
            em("vector", lambda v, nb=nblk, scr=scr: v.tensor_scalar(
                scr[:, 128:128 + nb], scr[:, 0:nb], INV7, 1e-12, ALU.mult, ALU.max))
            em("vector", lambda v, nb=nblk, scr=scr: v.reciprocal(scr[:, 256:256 + nb], scr[:, 128:128 + nb]))
            em("vector", lambda v, s3=src3, nb=nblk, w=w, scr=scr: v.tensor_tensor(
                scr[:, 512:512 + w].rearrange("p (b k) -> p b k", k=32), s3,
                scr[:, 256:256 + nb].unsqueeze(2).broadcast_to([128, nb, 32]), ALU.mult))
            em("vector", lambda v, w=w, scr=scr: v.tensor_scalar(
                scr[:, 512:512 + w], scr[:, 512:512 + w], CM, -CM, ALU.add, ALU.add))
            return em("vector", lambda v, dst=dst, nb=nblk, w=w, scr=scr: v.tensor_tensor(
                dst.rearrange("p (b k) -> p b k", k=32),
                scr[:, 512:512 + w].rearrange("p (b k) -> p b k", k=32),
                scr[:, 128:128 + nb].unsqueeze(2).broadcast_to([128, nb, 32]), ALU.mult))

        ld_ugj = [0] * 3
        for j in range(3):
            for q in range(2):
                ld_ugj[j] = dma("sync", sW,
                                fstage[:, j * 1024 + q * 512:j * 1024 + (q + 1) * 512],
                                wup_sl[j * 128:(j + 1) * 128, q * 512:(q + 1) * 512])

        em("vector", lambda v: v.memset(runm[:, :], 0.0))
        ld_x = [0] * NT
        dv_ln = [0] * NT
        ac_ln = [0] * NT
        sp_xn = [0] * NT
        dv1 = [0] * NT
        dv_up3 = [None] * 3
        dv_g3 = [None] * 3
        ld_g3j = [0] * 3
        sp_ug = 0
        cc_ug = 0
        def lnx(t):
            sl = t % 4
            if sl < 2:
                return xstage[:, sl * 1024:(sl + 1) * 1024]
            return tsc[:, (sl - 2) * 1024 + 2048:(sl - 1) * 1024 + 2048]

        def ln1(t):
            xa = lnx(t)
            a = 4 * (t % 2)
            if t >= 4:
                wait("sync", sDV, dv_ln[t - 4])
                wait("sync", sAC, ac_ln[t - 4])
            for q in range(4):
                ld_x[t] = dma("sync", sX[t % 4], xa[:, q * 256:(q + 1) * 256],
                              x_in[t * 128:(t + 1) * 128, q * 256:(q + 1) * 256])
            wait("vector", sX[t % 4], ld_x[t])
            em("vector", lambda v, xa=xa, a=a: v.reduce_sum(lnst[:, a:a + 1], xa, AX.X))
            dv1[t] = em("vector", lambda v, a=a: v.tensor_scalar(
                lnst[:, a + 1:a + 2], lnst[:, a:a + 1], -1.0 / 1024.0, None, ALU.mult))

        def ln2(t):
            xa = lnx(t)
            xb = xbstage[:, (t % 2) * 1024:(t % 2) * 1024 + 1024]
            a = 4 * (t % 2)
            wait("scalar", sDV, dv1[t])
            ac1 = em("scalar", lambda s, xa=xa, a=a: s.activation(
                kstage[:, 0:1024], xa, AF.Square,
                bias=lnst[:, a + 1:a + 2], scale=1.0, accum_out=lnst[:, a + 3:a + 4]))
            wait("vector", sAC, ac1)
            dv2 = em("vector", lambda v, a=a: v.tensor_scalar(
                lnst[:, a + 3:a + 4], lnst[:, a + 3:a + 4], 1.0 / 1024.0, 1e-5, ALU.mult, ALU.add))
            wait("scalar", sDV, dv2)
            ac2 = em("scalar", lambda s, a=a: s.sqrt(lnst[:, a + 2:a + 3], lnst[:, a + 3:a + 4]))
            wait("vector", sAC, ac2)
            em("vector", lambda v, a=a: v.reciprocal(lnst[:, a + 2:a + 3], lnst[:, a + 2:a + 3]))
            dv3 = em("vector", lambda v, a=a: v.tensor_tensor(
                lnst[:, a + 1:a + 2], lnst[:, a + 1:a + 2], lnst[:, a + 2:a + 3], ALU.mult))
            wait("scalar", sDV, dv3)
            if t >= 2:
                wait("scalar", sXs[t % 2], sp_xn[t - 2])
            ac_ln[t] = em("scalar", lambda s, xa=xa, xb=xb, a=a: s.activation(
                xb, xa, AF.Identity, bias=lnst[:, a + 1:a + 2], scale=lnst[:, a + 2:a + 3]))
            wait("vector", sAC, ac_ln[t])
            em("vector", lambda v, xb=xb, a=a: v.tensor_reduce(
                lnst[:, a + 3:a + 4], xb, AX.X, ALU.max, apply_absolute_value=True))
            dv_ln[t] = em("vector", lambda v, a=a: v.tensor_tensor(
                runm[:, 0:1], runm[:, 0:1], lnst[:, a + 3:a + 4], ALU.max))
            wait("gpsimd", sAC, ac_ln[t])
            for q in range(2):
                sp_xn[t] = dma("gpsimd", sXs[t % 2],
                               xn_d[t * 128:(t + 1) * 128, q * 512:(q + 1) * 512],
                               xb[:, q * 512:(q + 1) * 512])

        for t in range(NT):
            ln1(t)
            if t >= 1:
                ln2(t - 1)
            if t in (1, 2, 3):
                j = t - 1
                wait("vector", sW, ld_ugj[2])
                dv_up3[j] = dequant(fstage[:, j * 1024:(j + 1) * 1024],
                                    wq6(2 * j, 2 * j + 1), 32, tsc[:, 0:2048])
            if t == 3:
                for j in range(3):
                    wait("gpsimd", sDV, dv_up3[j])
                    for q in range(2):
                        ld_g3j[j] = dma("gpsimd", sWg,
                                        fstage[:, j * 1024 + q * 512:j * 1024 + (q + 1) * 512],
                                        wgate_sl[j * 128:(j + 1) * 128, q * 512:(q + 1) * 512])
            if t in (5, 6, 7):
                j = t - 5
                wait("vector", sWg, ld_g3j[2])
                dv_g3[j] = dequant(fstage[:, j * 1024:(j + 1) * 1024],
                                   wq6(2 * j + 1, 2 * j + 2), 32, tsc[:, 0:2048])
            if t == 8:
                wait("gpsimd", sDV, dv_g3[2])
                for j3 in range(3):
                    for q in range(2):
                        sp_ug = dma("gpsimd", sSP,
                                    upgate_in[j3 * 128:(j3 + 1) * 128, q * 1024:(q + 1) * 1024],
                                    wq6(2 * j3 + q, 2 * j3 + q + 1))
                wait("gpsimd", sSP, sp_ug)
                cc_ug = sCC.inc(1)
                em("gpsimd", lambda g: g.collective_compute(
                    "AllGather", ALU.bypass, replica_groups=[list(range(8))],
                    ins=[upgate_in], outs=[upgate_all]).then_inc(sCC.h, 1))
        ln2(NT - 1)

        em("vector", lambda v: v.memset(sc[:, 10:11], CM))

        mr_state = {"msc_free": 0, "gbc_free": 0}

        def max_round(ridx, cols, run_cols, dv_ready):
            n = len(run_cols)
            rc0 = run_cols[0]
            assert run_cols == list(range(rc0, rc0 + n))
            wait("gpsimd", sDV, dv_ready)
            for ci in range(n):
                sp1 = dma("gpsimd", sSP, mrow_d[ridx][ci:ci + 1, :].rearrange("o p -> p o"),
                          runm[:, rc0 + ci:rc0 + ci + 1])
            wait("gpsimd", sSP, sp1)
            sp2 = dma("gpsimd", sSP, mrow_sb[:, 0:n * 128],
                      mrow_d[ridx].rearrange("o p -> (o p)").unsqueeze(0))
            wait("vector", sSP, sp2)
            wait("vector", sSP, mr_state["msc_free"])
            em("vector", lambda v, n=n: v.tensor_reduce(
                msc_sb[:, 0:n], mrow_sb[:, 0:n * 128].rearrange("a (o p) -> a o p", p=128),
                AX.X, ALU.max))
            dvz = None
            for ci in range(len(run_cols), 8):
                dvz = em("vector", lambda v, ci=ci: v.memset(msc_sb[:, ci:ci + 1], 0.0))
            wait("gpsimd", sDV, dvz)
            sp3 = dma("gpsimd", sSP, cc_in[ridx], msc_sb[:, :])
            mr_state["msc_free"] = sp3
            wait("gpsimd", sSP, sp3)
            wait("gpsimd", sCC, sCC.n)
            ccv = sCC.inc(1)
            em("gpsimd", lambda g, ridx=ridx: g.collective_compute(
                "AllReduce", ALU.max, replica_groups=[list(range(8))],
                ins=[cc_in[ridx]], outs=[cc_out[ridx]]).then_inc(sCC.h, 1))
            wait("gpsimd", sCC, ccv)
            if mr_state["gbc_free"]:
                wait("gpsimd", sDV, mr_state["gbc_free"])
            n = len(run_cols)
            sp4 = dma("gpsimd", sSP, gbc[:, 0:n], cc_out[ridx][0:1, 0:n].broadcast_to([128, n]))
            wait("vector", sSP, sp4)
            dvs = None
            for ci, c in enumerate(cols):
                em("vector", lambda v, ci=ci, c=c: v.tensor_scalar(
                    sc[:, 2 * c:2 * c + 1], gbc[:, ci:ci + 1], INVQ, 1e-12, ALU.mult, ALU.max))
                dvs = em("vector", lambda v, c=c: v.reciprocal(sc[:, 2 * c + 1:2 * c + 2], sc[:, 2 * c:2 * c + 1]))
            mr_state["gbc_free"] = dvs
            return ccv, dvs

        cc_m0, dv_s0 = max_round(0, [0], [0], dv_ln[NT - 1])

        wait("sync", sXs[0], sp_xn[NT - 2])
        wait("sync", sXs[1], sp_xn[NT - 1])
        tr_xn1 = [0] * NH
        for k in range(NH):
            tr_xn1[k] = dmaT("sync", sTx, A[:, k * 2048:(k + 1) * 2048], xn_d[:, k * 128:(k + 1) * 128])
        wait("sync", sCC, cc_ug)
        tr_up = [0] * NH
        for k in range(NH):
            tr_up[k] = dmaT("sync", sTu, D[:, k * 3072:(k + 1) * 3072], upgate_all[:, k * 128:(k + 1) * 128])
        tr_g = [0] * NH
        for k in range(NH):
            tr_g[k] = dmaT("sync", sTg, B[:, k * 3072:(k + 1) * 3072],
                           upgate_all[:, 1024 + k * 128: 1024 + (k + 1) * 128])
        dv_xq = [0] * NH
        for k in range(NH):
            if k == 0:
                wait("vector", sTx, tr_xn1[NH - 1])
                wait("vector", sDV, dv_s0)
            em("vector", lambda v, k=k: v.tensor_scalar(
                tsc[:, 0:2048], A[:, k * 2048:(k + 1) * 2048], sc[:, 1:2], CM, ALU.mult, ALU.add))
            dv_xq[k] = em("vector", lambda v, k=k: v.tensor_scalar(
                A[:, k * 2048:(k + 1) * 2048], tsc[:, 0:2048], -CM, sc[:, 0:1], ALU.add, ALU.mult))

        wait("sync", sDV, dv_g3[2])
        ld_wqkv = 0
        for q in range(4):
            ld_wqkv = dma("sync", sW, fstage[:, q * 768:(q + 1) * 768],
                          wqkv_sl[:, q * 768:(q + 1) * 768])
        wait("vector", sW, ld_wqkv)
        wait("vector", sSP, sp_ug)
        dv_wqkv = em("vector", lambda v: v.tensor_copy(wq6(0, 3), fstage[:, 0:3072]))
        wait("gpsimd", sDV, dv_wqkv)
        sp_wqkv = 0
        for q in range(3):
            sp_wqkv = dma("gpsimd", sSP, wqkvq_in[:, q * 1024:(q + 1) * 1024], wq6(q, q + 1))
        wait("gpsimd", sSP, sp_wqkv)
        wait("gpsimd", sCC, sCC.n)
        cc_wqkv = sCC.inc(1)
        em("gpsimd", lambda g: g.collective_compute(
            "AllGather", ALU.bypass, replica_groups=[list(range(8))],
            ins=[wqkvq_in], outs=[wqkvq_all]).then_inc(sCC.h, 1))

        wait("sync", sDV, dv_wqkv)
        ld_wd = 0
        for q in range(4):
            ld_wd = dma("sync", sW, fstage[:, q * 768:(q + 1) * 768],
                        wdown_sl[:, q * 768:(q + 1) * 768])
        wait("vector", sW, ld_wd)
        wait("vector", sSP, sp_wqkv)
        dequant(fstage[:, 0:1536], wq6(0, 1.5), 48, tsc[:, 0:2048])
        dv_wd = dequant(fstage[:, 1536:3072], wq6(1.5, 3), 48, tsc[:, 0:2048])
        wait("gpsimd", sDV, dv_wd)
        sp_wd = 0
        for q in range(3):
            sp_wd = dma("gpsimd", sSP, wdownq_in[:, q * 1024:(q + 1) * 1024], wq6(q, q + 1))
        wait("gpsimd", sSP, sp_wd)
        wait("gpsimd", sCC, sCC.n)
        cc_wd = sCC.inc(1)
        em("gpsimd", lambda g: g.collective_compute(
            "AllGather", ALU.bypass, replica_groups=[list(range(8))],
            ins=[wdownq_in], outs=[wdownq_all]).then_inc(sCC.h, 1))

        upev_ac = {}
        upev_dv = {}
        sp_out_w = {}
        mm_pe = {}
        regions = [psA[:, 0:1024], psA[:, 1024:2048], psB[:, 0:1024], psB[:, 1024:2048]]
        halfseq = []

        def emit_mm_phase(which):
            W = D if which == "up" else B
            dst = up_d if which == "up" else gate_d
            runcol = 1 if which == "up" else 2
            for i in range(NI):
                for half in range(2):
                    key = (which, i, half)
                    halfseq.append(key)
                    g = len(halfseq) - 1
                    slot = regions[g % 4]
                    if g >= 4:
                        prev = halfseq[g - 4]
                        wait("tensor", sAC, upev_ac[prev])
                        wait("tensor", sDV, upev_dv[prev])
                    for k in range(NH):
                        for n2 in range(2):
                            fn = lambda te, W=W, i=i, k=k, half=half, n2=n2, slot=slot: te.matmul(
                                slot[:, n2 * 512:(n2 + 1) * 512],
                                W[:, k * 3072 + i * 128: k * 3072 + (i + 1) * 128],
                                A[:, k * 2048 + half * 1024 + n2 * 512: k * 2048 + half * 1024 + (n2 + 1) * 512],
                                start=(k == 0), stop=(k == NH - 1))
                            if k == NH - 1 and n2 == 1:
                                mm_pe[key] = sPE.inc(1)
                                plan["tensor"].append(lambda te, fn=fn: fn(te).then_inc(sPE.h, 1))
                            else:
                                plan["tensor"].append(fn)
                    wait("scalar", sPE, mm_pe[key])
                    wkey = (which, i - 1)
                    if which == "gate" and i < 1:
                        wkey = ("up", NI - 1)
                    if wkey in sp_out_w:
                        wait("scalar", sUsp[wkey[1] % 2], sp_out_w[wkey])
                    us = upstage[:, half * 1024:half * 1024 + 1024]
                    upev_ac[key] = em("scalar", lambda s, us=us, slot=slot: s.copy(us, slot))
                    wait("vector", sAC, upev_ac[key])
                    em("vector", lambda v, slot=slot: v.tensor_reduce(
                        lnst[:, 3:4], slot, AX.X, ALU.max, apply_absolute_value=True))
                    upev_dv[key] = em("vector", lambda v, rc=runcol: v.tensor_tensor(
                        runm[:, rc:rc + 1], runm[:, rc:rc + 1], lnst[:, 3:4], ALU.max))
                wait("sync", sAC, upev_ac[(which, i, 1)])
                for q in range(4):
                    sp_out_w[(which, i)] = dma(
                        "sync", sUsp[i % 2],
                        dst[i * 128:(i + 1) * 128, q * 512:(q + 1) * 512],
                        upstage[:, q * 512:(q + 1) * 512])

        wait("tensor", sDV, dv_xq[NH - 1])
        wait("tensor", sTu, tr_up[NH - 1])
        emit_mm_phase("up")
        wait("tensor", sTg, tr_g[NH - 1])
        emit_mm_phase("gate")

        cc_m12, dv_s12a = max_round(1, [1, 2], [1, 2], upev_dv[("gate", NI - 1, 1)])
        dv_s12 = em("vector", lambda v: v.tensor_scalar(
            sc[:, 11:12], sc[:, 4:5], -CM, None, ALU.mult))

        wait("sync", sPE, mm_pe[("gate", NI - 1, 1)])
        wait("sync", sCC, cc_wqkv)
        ld_wq = [0] * NH
        for k in range(NH):
            ld_wq[k] = dma("sync", sQw, B[:, k * 3072:(k + 1) * 3072], wqkvq_all[k * 128:(k + 1) * 128, :])
        tr_xn2 = [0] * NH
        for k in range(NH):
            tr_xn2[k] = dmaT("sync", sTx, A[:, k * 2048:(k + 1) * 2048], xn_d[:, k * 128:(k + 1) * 128])

        g1_dv = [0] * NI
        ld_gt = [0] * NI
        sp_g1 = [0] * NI
        g1_pc = [0] * NI
        dvmv = [0] * NI

        def redG(j):
            wait("vector", sPC, g1_pc[j])
            em("vector", lambda v, j=j: v.tensor_reduce(
                lnst[:, 3:4], gs3(j), AX.X, ALU.max, apply_absolute_value=True))
            g1_dv[j] = em("vector", lambda v: v.tensor_tensor(
                runm[:, 3:4], runm[:, 3:4], lnst[:, 3:4], ALU.max))

        def chainG(j):
            if j >= 4:
                wait("sync", sGl[j % 4], ld_gt[j - 4])
                wait("sync", sGs[j % 4], sp_g1[j - 4])
                wait("sync", sDV, g1_dv[j - 4])
            for q in range(4):
                ld_gt[j] = dma("sync", sGl[j % 4], gs3c(j, q * 512, (q + 1) * 512),
                               gate_d[j * 128:(j + 1) * 128, q * 512:(q + 1) * 512])
            wait("scalar", sGl[j % 4], ld_gt[j])
            if j == 0:
                wait("scalar", sDV, dv_s12)
            if j >= 2:
                wait("scalar", sDV, dvmv[j - 2])
            ac1 = em("scalar", lambda s, j=j: s.activation(ts2(j), gs3(j), AF.Identity, bias=sc[:, 10:11], scale=sc[:, 5:6]))
            wait("vector", sAC, ac1)
            dvv = em("vector", lambda v, j=j: v.tensor_scalar(gs3(j), ts2(j), -CM, sc[:, 4:5], ALU.add, ALU.mult))
            if j >= 2:
                wait("scalar", sPC, g1_pc[j - 2])
            acs = 0
            for h in range(2):
                sb_ = sigb(j, h)
                em("scalar", lambda s, j=j, h=h, sb_=sb_: s.activation(
                    sb_, tsc[:, (j % 2) * 2048 + h * 1024:(j % 2) * 2048 + (h + 1) * 1024],
                    AF.Sigmoid, bias=sc[:, 11:12], scale=sc[:, 4:5]))
                acs = em("scalar", lambda s, sb_=sb_: s.activation(
                    sb_, sb_, AF.Identity, bias=sc[:, 10:11], scale=65536.0))
            wait("vector", sAC, acs)
            for h in range(2):
                sb_ = sigb(j, h)
                dvmv[j] = em("vector", lambda v, sb_=sb_: v.tensor_scalar(
                    sb_, sb_, -CM, 65535.0, ALU.add, ALU.min))
            wait("gpsimd", sDV, dvmv[j])
            for h in range(2):
                gh = gs3c(j, h * 1024, (h + 1) * 1024)
                g1_pc[j] = empool(lambda g, gh=gh, sb_=sigb(j, h): g.tensor_tensor(gh, sb_, gh, ALU.mult))
            wait("gpsimd", sPC, g1_pc[j])
            for q in range(4):
                sp_g1[j] = dma("gpsimd", sGs[j % 4],
                               gate_d[j * 128:(j + 1) * 128, q * 512:(q + 1) * 512],
                               gs3c(j, q * 512, (q + 1) * 512))
            if j >= 1:
                redG(j - 1)
            if j == NI - 1:
                redG(j)

        g2_dv = [0] * NI
        ld_g1 = [0] * NI
        ld_ur = [0] * NI
        sp_g2 = [0] * NI
        g2_pc = [0] * NI
        actu_ms = [0] * NI
        actg_ms = [0] * NI
        dvvv = [0] * NI

        def redI(j):
            O = o_slot(j) if j < 20 else gsI(j)
            wait("vector", sPC, g2_pc[j])
            em("vector", lambda v, j=j, O=O: v.tensor_reduce(
                lnst[:, 3:4], O, AX.X, ALU.max, apply_absolute_value=True))
            g2_dv[j] = em("vector", lambda v: v.tensor_tensor(
                runm[:, 4:5], runm[:, 4:5], lnst[:, 3:4], ALU.max))

        def gsI(j, c0=0, c1=2048):
            sl = j % 4
            if sl < 2:
                return gsl[:, sl * 2048 + c0:sl * 2048 + c1]
            if sl == 2:
                return upsl[:, c0:c1]
            return vstage[:, c0:c1]

        kb = kstage[:, :].bitcast(BF16)

        def chainI(j):
            if j == 0:
                wait("sync", sDV, g1_dv[NI - 1])
            if j == 3:
                wait("sync", sVs, sp_v[NT - 1])
                wait("sync", sPE, pe_B_last[0])
            if j < 3:
                wait("sync", sGl[j % 4], ld_gt[20 + j])
                wait("sync", sGs[j % 4], sp_g1[20 + j])
            elif j >= 4:
                wait("sync", sGl[j % 4], ld_g1[j - 4])
                wait("sync", sAC, actg_ms[j - 4])
            for q in range(4):
                ld_g1[j] = dma("sync", sGl[j % 4], gsI(j, q * 512, (q + 1) * 512),
                               gate_d[j * 128:(j + 1) * 128, q * 512:(q + 1) * 512])
            if j == 0:
                wait("sync", sPE, pe_half[NT - 1][1])
            if j == 20:
                wait("sync", sAC, ac_ev[NT - 1][1])
                wait("sync", sDV, dv_ev[NT - 1][1])
            if j > 20:
                wait("sync", sAC, actu_ms[j - 1])
            ub = o_slot(j) if j < 20 else kb
            for q in range(2):
                ld_ur[j] = dma("sync", sUl[j % 6], o_slot_c(j, q * 1024, (q + 1) * 1024)
                               if j < 20 else kb[:, q * 1024:(q + 1) * 1024],
                               up_d[j * 128:(j + 1) * 128, q * 1024:(q + 1) * 1024])
            wait("scalar", sGl[j % 4], ld_g1[j])
            if j == 0:
                wait("scalar", sDV, mr_state["dv_s3"])
                wait("scalar", sDV, dv_coef)
            if j == 1:
                wait("scalar", sGs[3], sp_g1[23])
            if j >= 2:
                wait("scalar", sDV, dvvv[j - 2])
            actg_ms[j] = em("scalar", lambda s, j=j: s.activation(ts2(j), gsI(j), AF.Identity, bias=sc[:, 10:11], scale=sc[:, 7:8]))
            wait("vector", sAC, actg_ms[j])
            g2b = ost[:, 0:2048] if j % 2 == 0 else ost[:, 2048:4096]
            if j >= 2:
                wait("vector", sPC, g2_pc[j - 2])
            dvvv[j] = em("vector", lambda v, j=j, g2b=g2b: v.tensor_scalar(g2b, ts2(j), -CM, sc[:, 6:7], ALU.add, ALU.mult))
            wait("scalar", sUl[j % 6], ld_ur[j])
            if j >= 2:
                wait("scalar", sPC, g2_pc[j - 2])
            gb0 = (j % 2) * 2048
            if j % 2 == 0:
                ub2h = (xstage[:, 0:2048],)
            else:
                ub2h = (fstage[:, 2048:3072], t2f[:, 0:1024])
            for hi, uh in enumerate(ub2h):
                n = len(ub2h)
                src = (ub if n == 1 else
                       (o_slot_c(j, hi * 1024, (hi + 1) * 1024) if j < 20
                        else kb[:, hi * 1024:(hi + 1) * 1024]))
                actu_ms[j] = em("scalar", lambda s, uh=uh, src=src: s.activation(
                    src if False else uh, src, AF.Identity, bias=sc[:, 10:11], scale=sc[:, 3:4]))
            wait("vector", sAC, actu_ms[j])
            dvu = 0
            for uh in ub2h:
                dvu = em("vector", lambda v, uh=uh: v.tensor_scalar(uh, uh, -CM, None, ALU.add))
            wait("gpsimd", sDV, dvu)
            if len(ub2h) == 1:
                O = o_slot(j) if j < 20 else gsI(j)
                g2_pc[j] = empool(lambda g, O=O, g2b=g2b, uh=ub2h[0]: g.tensor_tensor(O, uh, g2b, ALU.mult))
            else:
                for hi, uh in enumerate(ub2h):
                    Oh = o_slot_c(j, hi * 1024, (hi + 1) * 1024) if j < 20 else gsI(j, hi * 1024, (hi + 1) * 1024)
                    gbh = ost[:, gb0 + hi * 1024:gb0 + (hi + 1) * 1024]
                    g2_pc[j] = empool(lambda g, Oh=Oh, gbh=gbh, uh=uh: g.tensor_tensor(Oh, uh, gbh, ALU.mult))
            if j >= 20:
                wait("gpsimd", sPC, g2_pc[j])
                for q in range(4):
                    sp_g2[j] = dma("gpsimd", sGs[j % 4],
                                   up_d[j * 128:(j + 1) * 128, q * 512:(q + 1) * 512],
                                   gsI(j, q * 512, (q + 1) * 512))
            if j >= 1:
                redI(j - 1)
            if j == NI - 1:
                redI(j)

        o_dv = [0] * NI
        ld_g2 = [0] * NI
        toenc_ms = [0] * NI

        def chainJ(j):
            if j >= 20:
                if j == 20:
                    for s4 in range(4):
                        wait("sync", sGs[s4], sp_g2[20 + s4])
                    wait("sync", sDV, g2_dv[NI - 1])
                    wait("sync", sPC, g2_pc[NI - 1])
                    wait("sync", sAC, actu_ms[NI - 1])
                if j == 23:
                    wait("sync", sDV, o_dv[22])
                for q in range(2):
                    ld_g2[j] = dma("sync", sJl[j - 20], o_slot_c(j, q * 1024, (q + 1) * 1024),
                                   up_d[j * 128:(j + 1) * 128, q * 1024:(q + 1) * 1024])
            if j == 0:
                wait("scalar", sDV, mr_state["dv_s4"])
            if j >= 2:
                wait("scalar", sDV, o_dv[j - 2])
            if j >= 20:
                wait("scalar", sJl[j - 20], ld_g2[j])
            tb = ts2(j + 1)
            toenc_ms[j] = em("scalar", lambda s, j=j, tb=tb: s.activation(
                tb, o_slot(j), AF.Identity, bias=sc[:, 10:11], scale=sc[:, 9:10]))
            wait("vector", sAC, toenc_ms[j])
            o_dv[j] = em("vector", lambda v, j=j, tb=tb: v.tensor_scalar(
                o_slot(j), tb, -CM, sc[:, 8:9], ALU.add, ALU.mult))

        pe_half = [[0, 0] for _ in range(NT)]
        ac_ev = [[0, 0] for _ in range(NT)]
        dv_ev = [[0, 0] for _ in range(NT)]
        sp_v = [0] * NT
        pe_B_last = [0]

        def qkv_pe(t):
            for h in range(2):
                ps = psA if h == 0 else psB
                if t >= 1:
                    wait("tensor", sAC, ac_ev[t - 1][h])
                    wait("tensor", sDV, dv_ev[t - 1][h])
                for k in range(NH):
                    for n3 in range(3):
                        fn = lambda te, ps=ps, k=k, n3=n3, t=t, h=h: te.matmul(
                            ps[:, n3 * 512:(n3 + 1) * 512],
                            A[:, k * 2048 + t * 128: k * 2048 + (t + 1) * 128],
                            B[:, k * 3072 + h * 1536 + n3 * 512: k * 3072 + h * 1536 + (n3 + 1) * 512],
                            start=(k == 0), stop=(k == NH - 1))
                        if k == NH - 1 and n3 == 2:
                            pe_half[t][h] = sPE.inc(1)
                            plan["tensor"].append(lambda te, fn=fn: fn(te).then_inc(sPE.h, 1))
                        else:
                            plan["tensor"].append(fn)

        def qkv_B(t):
            vt = vstage[:, (t % 2) * 1024:(t % 2) * 1024 + 1024]
            wait("tensor", sAC, ac_ev[t][1])
            plan["tensor"].append(lambda te, vt=vt, t=t: te.matmul(
                psA[0:16, 1536:2048], cbsel[:, t * 16:(t + 1) * 16], vt[:, 0:512],
                start=(t == 0), stop=(t == NT - 1)))
            fnB = lambda te, vt=vt, t=t: te.matmul(
                psB[0:16, 1536:2048], cbsel[:, t * 16:(t + 1) * 16], vt[:, 512:1024],
                start=(t == 0), stop=(t == NT - 1))
            if t == NT - 1:
                pe_B_last[0] = sPE.inc(1)
                plan["tensor"].append(lambda te, fnB=fnB: fnB(te).then_inc(sPE.h, 1))
            else:
                plan["tensor"].append(fnB)

        def qkv_evac(t):
            for h in range(2):
                ps = psA if h == 0 else psB
                wait("scalar", sPE, pe_half[t][h])
                if h == 1:
                    wait("scalar", sDV, dv_ev[t][0])
                elif t >= 1:
                    wait("scalar", sDV, dv_ev[t - 1][1])
                    wait("scalar", sVs, sp_v[t - 1])
                vsl = vstage[:, (t % 2) * 1024 + h * 512:(t % 2) * 1024 + h * 512 + 512]
                em("scalar", lambda s, ps=ps, vsl=vsl: s.copy(
                    vsl.rearrange("p (hh dd) -> p hh dd", dd=64),
                    ps[:, 0:1536].rearrange("p (hh c) -> p hh c", c=192)[:, :, 128:192]))
                ac_ev[t][h] = em("scalar", lambda s, ps=ps: s.copy(
                    kstage[:, 0:512].rearrange("p (hh dd) -> p hh dd", dd=64),
                    ps[:, 0:1536].rearrange("p (hh c) -> p hh c", c=192)[:, :, 64:128]))
                wait("vector", sAC, ac_ev[t][h])
                em("vector", lambda v, ps=ps: v.tensor_tensor(
                    kstage[:, 0:512].rearrange("p (hh dd) -> p hh dd", dd=64),
                    ps[:, 0:1536].rearrange("p (hh c) -> p hh c", c=192)[:, :, 0:64],
                    kstage[:, 0:512].rearrange("p (hh dd) -> p hh dd", dd=64), ALU.mult))
                dv_ev[t][h] = em("vector", lambda v, t=t, h=h: v.tensor_reduce(
                    sdg[:, t * 16 + h * 8: t * 16 + h * 8 + 8],
                    kstage[:, 0:512].rearrange("p (hh dd) -> p hh dd", dd=64),
                    AX.X, ALU.add))
            vt = vstage[:, (t % 2) * 1024:(t % 2) * 1024 + 1024]
            wait("gpsimd", sAC, ac_ev[t][1])
            sp_v[t] = dma("gpsimd", sVs, v_d[t * 128:(t + 1) * 128, :], vt)

        wait("tensor", sCst, ld_consts)
        wait("tensor", sQw, ld_wq[NH - 1])
        wait("tensor", sTx, tr_xn2[NH - 1])
        for i in (NI - 2, NI - 1):
            for h in (0, 1):
                wait("tensor", sAC, upev_ac[("gate", i, h)])
                wait("tensor", sDV, upev_dv[("gate", i, h)])
        wait("sync", sUsp[0], sp_out_w[("gate", NI - 2)])
        wait("sync", sUsp[1], sp_out_w[("gate", NI - 1)])
        wait("sync", sSP, sp_wd)

        chain_left = list(range(NI))
        for t in range(NT):
            qkv_pe(t)
            if t >= 1:
                qkv_B(t - 1)
            for _ in range(2 if t % 2 == 0 else 1):
                if chain_left:
                    chainG(chain_left.pop(0))
            qkv_evac(t)
        qkv_B(NT - 1)

        cc_m3, dv_s3a = max_round(2, [3], [3], g1_dv[NI - 1])
        em("vector", lambda v: v.tensor_scalar(
            sc[:, 6:7], gbc[:, 0:1], float(np.float32(P16) * np.float32(INVQ)), 1e-12, ALU.mult, ALU.max))
        em("vector", lambda v: v.reciprocal(sc[:, 7:8], sc[:, 6:7]))
        dv_s3 = em("vector", lambda v: v.tensor_scalar(sc[:, 7:8], sc[:, 7:8], P16, None, ALU.mult))
        mr_state["gbc_free"] = dv_s3
        mr_state["dv_s3"] = dv_s3

        wait("vector", sPE, pe_B_last[0])
        em("vector", lambda v: v.tensor_copy(B_sb[:, 0:512], psA[0:16, 1536:2048]))
        dv_Bev = em("vector", lambda v: v.tensor_copy(B_sb[:, 512:1024], psB[0:16, 1536:2048]))
        wait("tensor", sDV, dv_Bev)
        plan["tensor"].append(lambda te: te.matmul(psA[0:1, 0:512], cone16[:, :], B_sb[:, 0:512], start=True, stop=True))
        pe_T = sPE.inc(1)
        plan["tensor"].append(lambda te: te.matmul(psA[0:1, 512:1024], cone16[:, :], B_sb[:, 512:1024], start=True, stop=True).then_inc(sPE.h, 1))
        wait("vector", sPE, pe_T)
        dv_T = em("vector", lambda v: v.tensor_copy(T_sb, psA[0:1, 0:1024]))
        wait("gpsimd", sDV, dv_T)
        sp_T = dma("gpsimd", sSP, vsum_in, T_sb)
        wait("gpsimd", sSP, sp_T)
        wait("gpsimd", sCC, sCC.n)
        cc_carry = sCC.inc(1)
        em("gpsimd", lambda g: g.collective_compute(
            "AllGather", ALU.bypass, replica_groups=[list(range(8))],
            ins=[vsum_in], outs=[vsum_all]).then_inc(sCC.h, 1))

        wait("vector", sCst, ld_consts)
        wait("vector", sPC, g1_pc[NI - 1])
        em("vector", lambda v: v.tensor_scalar(xstage[:, 0:256], sdg[:, :], 0.125, 0.0, ALU.mult, ALU.max))
        dv_m = em("vector", lambda v: v.scalar_tensor_tensor(
            xstage[:, 256:512], sdg[:, :], 0.125, xstage[:, 0:256], ALU.mult, ALU.subtract))
        wait("scalar", sDV, dv_m)
        em("scalar", lambda s: s.activation(xstage[:, 512:768], xstage[:, 0:256], AF.Exp, scale=-1.0))
        ac_ab = em("scalar", lambda s: s.activation(xstage[:, 768:1024], xstage[:, 256:512], AF.Exp))
        wait("vector", sAC, ac_ab)
        em("vector", lambda v: v.tensor_tensor(
            xstage[:, 1024:1280].rearrange("p (tt hh) -> p tt hh", hh=16),
            xstage[:, 512:768].rearrange("p (tt hh) -> p tt hh", hh=16),
            cpos[:, :].unsqueeze(2).broadcast_to([128, 16, 16]), ALU.mult))
        em("vector", lambda v: v.tensor_tensor(xstage[:, 1024:1280], xstage[:, 1024:1280], xstage[:, 768:1024], ALU.add))
        em("vector", lambda v: v.reciprocal(xstage[:, 1280:1536], xstage[:, 1024:1280]))
        em("vector", lambda v: v.tensor_tensor(coefC[:, :], xstage[:, 512:768], xstage[:, 1280:1536], ALU.mult))
        em("vector", lambda v: v.tensor_tensor(xstage[:, 1536:1792], xstage[:, 768:1024], xstage[:, 512:768], ALU.subtract))
        dv_coef = em("vector", lambda v: v.tensor_tensor(coefV[:, :], xstage[:, 1536:1792], xstage[:, 1280:1536], ALU.mult))

        for j in range(NI):
            chainI(j)

        wait("gpsimd", sCC, cc_carry)
        sp_vg = dma("gpsimd", sOb[0], vg_sb[:, :], vsum_all)
        wait("tensor", sCst, ld_consts)
        wait("tensor", sOb[0], sp_vg)
        pe_P = 0
        for n2 in range(2):
            plan["tensor"].append(lambda te, n2=n2: te.matmul(
                psB[0:16, n2 * 512:(n2 + 1) * 512], ctri16[:, :],
                B_sb[:, n2 * 512:(n2 + 1) * 512], start=True, stop=False))
            fn = lambda te, n2=n2: te.matmul(
                psB[0:16, n2 * 512:(n2 + 1) * 512], csel[:, :],
                vg_sb[:, n2 * 512:(n2 + 1) * 512], start=False, stop=True)
            if n2 == 1:
                pe_P = sPE.inc(1)
                plan["tensor"].append(lambda te, fn=fn: fn(te).then_inc(sPE.h, 1))
            else:
                plan["tensor"].append(fn)
        wait("vector", sPE, pe_P)
        dv_P = em("vector", lambda v: v.tensor_copy(P_sb[:, :], psB[0:16, 0:1024]))

        cc_m4, dv_s4a = max_round(3, [4], [4], g2_dv[NI - 1])
        em("vector", lambda v: v.tensor_tensor(sc[:, 8:9], gbc[:, 0:1], sc[:, 2:3], ALU.mult))
        em("vector", lambda v: v.tensor_scalar(sc[:, 8:9], sc[:, 8:9], INVQ, 1e-12, ALU.mult, ALU.max))
        em("vector", lambda v: v.reciprocal(sc[:, 9:10], sc[:, 8:9]))
        dv_s4 = em("vector", lambda v: v.tensor_tensor(sc[:, 9:10], sc[:, 9:10], sc[:, 2:3], ALU.mult))
        mr_state["gbc_free"] = dv_s4
        mr_state["dv_s4"] = dv_s4

        for j in range(NI):
            chainJ(j)

        wait("sync", sCC, cc_wd)
        wait("sync", sPE, mm_pe[("up", NI - 1, 1)])
        tr_d = [0] * NI
        for k in range(NI):
            tr_d[k] = dmaT("sync", sTd, D[:, k * 1024:(k + 1) * 1024], wdownq_all[:, k * 128:(k + 1) * 128])

        dn_pe = [0] * NT
        dn_dv = [0] * NT
        attn_pe = [0] * NT
        attn_dv = [0] * NT
        ld_x2 = [0] * NT
        ld_at_v = [0] * NT
        st_out = [0] * NT
        wait("sync", sVs, sp_v[NT - 1])
        wait("sync", sPE, pe_B_last[0])
        wait("sync", sDV, dv_coef)
        wait("tensor", sTd, tr_d[NI - 1])
        wait("tensor", sDV, o_dv[NI - 1])

        for t in range(NT):
            if t >= 2:
                wait("sync", sDV, attn_dv[t - 2])
            for q in range(2):
                ld_at_v[t] = dma("sync", sV[t % 2],
                                 vstage[:, (t % 2) * 1024 + q * 512:(t % 2) * 1024 + (q + 1) * 512],
                                 v_d[t * 128:(t + 1) * 128, q * 512:(q + 1) * 512])
            if t >= 2:
                wait("sync", sDV, dn_dv[t - 2])
            for q in range(4):
                ld_x2[t] = dma("sync", sX[t % 2],
                               xstage[:, (t % 2) * 1024 + q * 256:(t % 2) * 1024 + (q + 1) * 256],
                               x_in[t * 128:(t + 1) * 128, q * 256:(q + 1) * 256])

            pc = psB[:, (t % 2) * 1024:(t % 2) * 1024 + 1024]
            if t >= 2:
                wait("tensor", sDV, attn_dv[t - 2])
            if t == 0:
                wait("tensor", sDV, dv_P)
            wait("tensor", sV[t % 2], ld_at_v[t])
            for n2 in range(2):
                plan["tensor"].append(lambda te, t=t, n2=n2, pc=pc: te.matmul(
                    pc[:, n2 * 512:(n2 + 1) * 512], ctriu[:, :],
                    vstage[:, (t % 2) * 1024 + n2 * 512:(t % 2) * 1024 + (n2 + 1) * 512],
                    start=True, stop=False))
                fn = lambda te, t=t, n2=n2, pc=pc: te.matmul(
                    pc[:, n2 * 512:(n2 + 1) * 512], cesel[:, t * 128:(t + 1) * 128],
                    P_sb[:, n2 * 512:(n2 + 1) * 512], start=False, stop=True)
                if n2 == 1:
                    attn_pe[t] = sPE.inc(1)
                    plan["tensor"].append(lambda te, fn=fn: fn(te).then_inc(sPE.h, 1))
                else:
                    plan["tensor"].append(fn)

            slot = psA[:, (t % 2) * 1024:(t % 2) * 1024 + 1024]
            if t >= 2:
                wait("tensor", sDV, dn_dv[t - 2])
            elif t == 0:
                wait("tensor", sDV, dv_T)
            for k in range(NI):
                for n2 in range(2):
                    fn = lambda te, t=t, k=k, n2=n2, slot=slot: te.matmul(
                        slot[:, n2 * 512:(n2 + 1) * 512],
                        o_slot_c(k, t * 128, (t + 1) * 128),
                        D[:, k * 1024 + n2 * 512: k * 1024 + (n2 + 1) * 512],
                        start=(k == 0), stop=(k == NI - 1))
                    if k == NI - 1 and n2 == 1:
                        dn_pe[t] = sPE.inc(1)
                        plan["tensor"].append(lambda te, fn=fn: fn(te).then_inc(sPE.h, 1))
                    else:
                        plan["tensor"].append(fn)

            wait("vector", sPE, attn_pe[t])
            if t == 0:
                wait("vector", sDV, dv_coef)
            ab = xbstage[:, (t % 2) * 1024:(t % 2) * 1024 + 1024]
            em("vector", lambda v, t=t, pc=pc: v.tensor_tensor(
                kstage[:, 0:1024].rearrange("p (hh dd) -> p hh dd", dd=64),
                pc[:, :].rearrange("p (hh dd) -> p hh dd", dd=64),
                coefC[:, t * 16:(t + 1) * 16].unsqueeze(2).broadcast_to([128, 16, 64]), ALU.mult))
            em("vector", lambda v, t=t: v.tensor_tensor(
                t2f[:, 0:1024].rearrange("p (hh dd) -> p hh dd", dd=64),
                vstage[:, (t % 2) * 1024:(t % 2) * 1024 + 1024].rearrange("p (hh dd) -> p hh dd", dd=64),
                coefV[:, t * 16:(t + 1) * 16].unsqueeze(2).broadcast_to([128, 16, 64]), ALU.mult))
            attn_dv[t] = em("vector", lambda v, ab=ab: v.tensor_tensor(
                ab, kstage[:, 0:1024], t2f[:, 0:1024], ALU.add))

            wait("vector", sPE, dn_pe[t])
            wait("vector", sX[t % 2], ld_x2[t])
            if t >= 2:
                wait("vector", sO[t % 2], st_out[t - 2])
            osl = tsc[:, (t % 2) * 1024:(t % 2) * 1024 + 1024]
            em("vector", lambda v, t=t, slot=slot, osl=osl: v.tensor_tensor(
                osl, slot, xstage[:, (t % 2) * 1024:(t % 2) * 1024 + 1024], ALU.add))
            dn_dv[t] = em("vector", lambda v, t=t, osl=osl, ab=ab: v.tensor_tensor(
                osl, osl, ab, ALU.add))
            wait("gpsimd", sDV, dn_dv[t])
            for q in range(2):
                st_out[t] = dma("gpsimd", sO[t % 2],
                                out_d[t * 128:(t + 1) * 128, q * 512:(q + 1) * 512],
                                osl[:, q * 512:(q + 1) * 512])

        wait("gpsimd", sO[0], st_out[NT - 2])
        wait("gpsimd", sO[1], st_out[NT - 1])

        block = es.enter_context(nc.Block())

        @block.sync
        def _(sync):
            for fn in plan["sync"]:
                fn(sync)

        @block.gpsimd
        def _(gpsimd):
            for fn in plan["gpsimd"]:
                fn(gpsimd)

        @block.vector
        def _(vector):
            for fn in plan["vector"]:
                fn(vector)

        @block.scalar
        def _(scalar):
            for fn in plan["scalar"]:
                fn(scalar)

        @block.tensor
        def _(tensor):
            for fn in plan["tensor"]:
                fn(tensor)

    return nc


def _consts(core_id):
    triuT = np.triu(np.ones((128, 128), np.float32)).astype(bfloat16)
    tri16T = np.triu(np.ones((16, 16), np.float32), 1).astype(bfloat16)
    ones16 = np.ones((16, 1), np.float32).astype(bfloat16)
    sel16 = np.zeros((8, 16), np.float32)
    if core_id % 2 == 1:
        sel16[core_id - 1, :] = 1.0
    sel16 = sel16.astype(bfloat16)
    esel = np.zeros((16, 2048), np.float32)
    for t in range(16):
        esel[t, t * 128:(t + 1) * 128] = 1.0
    esel = esel.astype(bfloat16)
    bsel = np.zeros((128, 256), np.float32)
    for t in range(16):
        bsel[:, t * 16 + t] = 1.0
    bsel = bsel.astype(bfloat16)
    off = (core_id % 2) * T
    pos = (np.arange(T, dtype=np.float32).reshape(16, 128).T + np.float32(off)).astype(np.float32)
    return dict(c_triuT=triuT, c_tri16T=tri16T, c_ones16=ones16,
                c_sel16=sel16, c_esel=esel, c_pos=pos, c_bsel=bsel)


def kernel(L0_attn_x, L0_attn_wqkv, L0_ffn_w_gate, L0_ffn_w_up, L0_ffn_w_down):
    x = np.ascontiguousarray(np.asarray(L0_attn_x, np.float32).reshape(16384, 1024))
    wqkv = np.ascontiguousarray(np.asarray(L0_attn_wqkv, np.float32))
    wg = np.ascontiguousarray(np.asarray(L0_ffn_w_gate, np.float32))
    wu = np.ascontiguousarray(np.asarray(L0_ffn_w_up, np.float32))
    wd = np.ascontiguousarray(np.asarray(L0_ffn_w_down, np.float32))

    nc = build()
    in_maps = []
    for c in range(8):
        m = dict(
            x=x[c * T:(c + 1) * T],
            wqkv_sl=wqkv[c * 128:(c + 1) * 128],
            wup_sl=wu[c * 384:(c + 1) * 384],
            wgate_sl=wg[c * 384:(c + 1) * 384],
            wdown_sl=wd[c * 128:(c + 1) * 128],
        )
        m.update(_consts(c))
        in_maps.append(m)
    res = run_bass_kernel_spmd(nc, in_maps, core_ids=list(range(8)))
    out = np.concatenate([np.asarray(r["out"]) for r in res.results], axis=0)
    return out.reshape(4, 4096, 1024)
